# version 12
# baseline (speedup 1.0000x reference)
import numpy as np
import ml_dtypes
from contextlib import ExitStack

import concourse.bass as bass
import concourse.mybir as mybir
from concourse.bass_core_rust import IndirectOffsetOnAxis as IOA
from concourse.bass_utils import run_bass_kernel_spmd

F32 = mybir.dt.float32
BF16 = mybir.dt.bfloat16
I32 = mybir.dt.int32
AF = mybir.ActivationFunctionType
ALU = mybir.AluOpType
AX = mybir.AxisListType.X

D = 1024
KD = 8
SEQ = 4096
OWN = 2048
NT1 = SEQ // 128
NT3 = OWN // 128
NE = 7
H = 4096
MH = 32
CAP = 512
NT4 = CAP // 128
SENT = 1.0e7
BF = ml_dtypes.bfloat16

ENGS = ("tensor", "vector", "scalar", "sync", "gpsimd")


class Prog:
    def __init__(self):
        self.q = {e: [] for e in ENGS}
        self.sems = {}

    def add_sem(self, name, handle):
        self.sems[name] = [handle, 0]

    def cur(self, name):
        return self.sems[name][1]

    def w(self, eng, name, val):
        if val > self.sems[name][1]:
            raise RuntimeError(f"wait on {name} for {val} > emitted {self.sems[name][1]}")
        if val > 0:
            self.q[eng].append(("w", name, val))

    def wt(self, eng, st_):
        if st_ is not None:
            self.w(eng, st_[0], st_[1])

    def o(self, eng, fn, incs=()):
        tick = None
        rec = []
        for nm, n in incs:
            self.sems[nm][1] += n
            rec.append((nm, n))
            if tick is None:
                tick = self.sems[nm][1]
        self.q[eng].append(("o", fn, rec))
        return tick

    def emit(self, blk):
        def runner(eng_name):
            def body(eng):
                for item in self.q[eng_name]:
                    if item[0] == "w":
                        eng.wait_ge(self.sems[item[1]][0], item[2])
                    else:
                        ins = item[1](eng)
                        for nm, n in item[2]:
                            ins = ins.then_inc(self.sems[nm][0], n)
            return body
        blk.tensor(runner("tensor"))
        blk.vector(runner("vector"))
        blk.scalar(runner("scalar"))
        blk.sync(runner("sync"))
        blk.gpsimd(runner("gpsimd"))


def build_nc(stage=8, dbg=False):
    nc = bass.Bass("TRN2", target_bir_lowering=False, debug=False)

    xs = nc.dram_tensor("xs", [SEQ, D], F32, kind="ExternalInput").ap()
    wq = nc.dram_tensor("wq", [24, 128, 1024], F32, kind="ExternalInput").ap()
    gam = nc.dram_tensor("gam", [128, 8], F32, kind="ExternalInput").ap()
    rw = nc.dram_tensor("rw", [128, 56], F32, kind="ExternalInput").ap()
    identf = nc.dram_tensor("identf", [128, 128], F32, kind="ExternalInput").ap()
    identb = nc.dram_tensor("identb", [128, 128], BF16, kind="ExternalInput").ap()
    ecolf = nc.dram_tensor("ecolf", [7, 2], F32, kind="ExternalInput").ap()
    tokidf = nc.dram_tensor("tokidf", [128, 32], F32, kind="ExternalInput").ap()
    wg = nc.dram_tensor("wg", [NE, 16, 128, 2048], F32, kind="ExternalInput").ap()
    wu = nc.dram_tensor("wu", [NE, 16, 128, 2048], F32, kind="ExternalInput").ap()
    wd = nc.dram_tensor("wd", [NE, 8, 128, 4096], F32, kind="ExternalInput").ap()
    swg = nc.dram_tensor("swg", [16, 128, 2048], F32, kind="ExternalInput").ap()
    swu = nc.dram_tensor("swu", [16, 128, 2048], F32, kind="ExternalInput").ap()
    swd = nc.dram_tensor("swd", [8, 128, 4096], F32, kind="ExternalInput").ap()
    yo = nc.dram_tensor("yo", [OWN, D], F32, kind="ExternalOutput").ap()
    dbgo = nc.dram_tensor("dbgo", [128, 96], F32, kind="ExternalOutput").ap() if dbg else None
    dbg2o = nc.dram_tensor("dbg2o", [256, D], BF16, kind="ExternalOutput").ap() if dbg else None
    dbg3o = nc.dram_tensor("dbg3o", [512, D], F32, kind="ExternalOutput").ap() if dbg else None
    dbg4o = nc.dram_tensor("dbg4o", [128, KD * OWN], F32, kind="ExternalOutput").ap() if dbg else None
    ln2tok = nc.dram_tensor("ln2tok", [OWN + 1, D], BF16, kind="Internal").ap()
    listf = nc.dram_tensor("listf", [128 * 56 + 1, 1], F32, kind="Internal").ap()
    accC = nc.dram_tensor("accC", [2 * OWN + 1, D], BF16, kind="Internal").ap()

    p = Prog()
    es = ExitStack()
    with es:
        sb = lambda n, s, d: es.enter_context(nc.sbuf_tensor(n, s, d))
        psu = lambda n, s, d: es.enter_context(nc.psum_tensor(n, s, d))

        gam_s = sb("gam_s", [128, 8], F32)
        rw_s = sb("rw_s", [128, 56], BF16)
        identf_s = sb("identf_s", [128, 128], F32)
        identb_s = sb("identb_s", [128, 128], BF16)
        epscol = sb("epscol", [128, 1], F32)
        ones1 = sb("ones1", [1, 128], F32)
        ecol_s = sb("ecol_s", [7, 2], F32)
        tokid_s = sb("tokid_s", [128, 32], F32)
        sent_s = sb("sent_s", [128, 56], F32)
        st = sb("st", [128, 64], F32)
        p4b = sb("p4b", [128, 64], F32)
        sgall = sb("sgall", [128, 16], F32)
        tauall = sb("tauall", [128, 16], F32)
        gidx_f = sb("gidx_f", [128, 56], F32)
        gidx_i = sb("gidx_i", [128, 56], I32)
        gsc_f = sb("gsc_f", [128, 56], F32)
        gsc_i = sb("gsc_i", [128, 56], I32)
        fdt_f = sb("fdt_f", [128, 32], F32)
        fdt_i = sb("fdt_i", [128, 32], I32)
        x_dT = sb("x_dT", [128, KD * OWN], F32)
        AR = 34816
        arena = sb("arena", [128, AR], F32)

        def acar(off, cols):
            assert off + cols <= AR
            return arena[:, off:off + cols]

        def acarb(off, cols_bf16):
            assert off * 2 + cols_bf16 <= 2 * AR
            return arena[:, off:off + (cols_bf16 + 1) // 2].bitcast(BF16)

        def acari(off, cols):
            assert off + cols <= AR
            return arena[:, off:off + cols].bitcast(I32)

        h_off = 0

        def h_dT(kd, c0, c1):
            v = arena[:, h_off + kd * 2048:h_off + (kd + 1) * 2048].bitcast(BF16)
            return v[:, c0:c1]
        SLOT = [16384 + k * 3584 for k in range(4)]
        y_off = 16384
        kv_off = 20480
        a_off = 24576
        t1_off = 28672
        wqr_off = [31744, 32256]
        wqg_off = [32768, 33280]
        zz_off = 33792


        ln2_off = 0

        def ln2_dT(kd, c0, c1):
            v = arena[:, ln2_off + kd * 1024:ln2_off + (kd + 1) * 1024].bitcast(BF16)
            return v[:, c0:c1]
        fo = 8192
        wg_off = [fo, fo + 1024]; fo += 2048
        wu_off = [fo, fo + 1024]; fo += 2048
        wd_off = [fo, fo + 512, fo + 1024, fo + 1536]; fo += 2048
        h1hi_off = fo; fo += 8192

        def h1(m, c0, c1, kind):
            if kind == "rt":
                off = (m * 512) if m < 16 else (h1hi_off + (m - 16) * 512)
            else:
                off = h1hi_off + m * 512
            v = arena[:, off:off + 512].bitcast(BF16)
            return v[:, c0:c1]
        gln_off = fo; fo += 4096

        def gln_dT(b, kd, c0, c1):
            v = arena[:, gln_off + kd * 512:gln_off + (kd + 1) * 512].bitcast(BF16)
            return v[:, c0:c1]
        gtok_off = [31744, 32256]
        fo += 1024
        sil_off = fo; fo += 1024
        dout_off = fo; fo += 4096
        ttok_off = [fo, fo + 512]; fo += 1024
        assert fo <= AR
        fa1_off = [22528, 24576, 14336, 16384]
        fa2_off = [23040, 25088, 14848, 16896]
        fxt_off = [23552, 25600, 15360, 17408]

        mskT_off = [16384, 18432]
        PB = [20480 + k * 2048 for k in range(7)]

        pA = psu("pA", [128, 1024], F32)
        pB = psu("pB", [128, 1024], F32)
        pC = psu("pC", [128, 1024], F32)
        pD = psu("pD", [128, 1024], F32)
        PS = [pA, pB, pC, pD]

        def bank(i):
            return PS[i // 2][:, (i % 2) * 512:(i % 2) * 512 + 512]

        SEMS = ("spe", "sve", "sac", "sou",
                "sdc", "srw", "spf", "szr", "sx0", "sx1", "sq0", "sq1", "sl2",
                "swg0", "swg1", "swu0", "swu1", "swd0", "swd1",
                "sg0", "sg1", "ssc", "srb", "so0", "so1", "sal", "sov")
        for nm in SEMS:
            p.add_sem(nm, es.enter_context(nc.semaphore(nm)))

        blk = es.enter_context(nc.Block())

        p.o("sync", lambda e: e.dma_start(out=gam_s[:, :], in_=gam), (("sdc", 16),))
        p.o("sync", lambda e: e.dma_start(out=identf_s[:, :], in_=identf), (("sdc", 16),))
        p.o("sync", lambda e: e.dma_start(out=identb_s[:, :], in_=identb), (("sdc", 16),))
        p.o("sync", lambda e: e.dma_start(out=ecol_s[:, :], in_=ecolf), (("sdc", 16),))
        p.o("sync", lambda e: e.dma_start(out=tokid_s[:, :], in_=tokidf), (("sdc", 16),))
        p.o("gpsimd", lambda e: e.dma_start(out=rw_s[:, :], in_=rw), (("srw", 16),))
        t_consts = p.cur("sdc")
        t_rw = p.cur("srw")
        t_eps = p.o("vector", lambda e: e.memset(epscol[:, :], 1e-5), (("sve", 1),))
        p.o("vector", lambda e: e.memset(ones1[:, :], 1.0), (("sve", 1),))
        p.o("vector", lambda e: e.memset(sent_s[:, :], SENT), (("sve", 1),))
        zz = acar(zz_off, 1024)
        t_zz = p.o("vector", lambda e: e.memset(zz, 0.0), (("sve", 1),))
        t_setup_v = p.cur("sve")
        t_gt0 = p.o("vector", lambda e: e.memset(acarb(gtok_off[0], 1024), 0.0), (("sve", 1),))
        t_gt1 = p.o("vector", lambda e: e.memset(acarb(gtok_off[1], 1024), 0.0), (("sve", 1),))
        p.w("gpsimd", "sve", t_setup_v)
        t_pref = p.o("gpsimd", lambda e: e.dma_start(
            out=listf[0:128 * 56].rearrange("(q x) one -> q (x one)", q=128), in_=sent_s[:, :]), (("spf", 16),))
        for t in range(32):
            p.o("gpsimd", lambda e, t=t: e.dma_start(
                out=accC[t * 128:(t + 1) * 128, :], in_=zz), (("szr", 16),))
        p.o("gpsimd", lambda e: e.dma_start(
            out=ln2tok[OWN:OWN + 1, :], in_=zz[0:1, :]), (("szr", 16),))
        t_prezero = p.cur("szr")

        tslot_free = [None] * 8
        tslot_i = [0]

        def tslot(n_avail=8, base=0):
            i = base + (tslot_i[0] % n_avail)
            tslot_i[0] += 1
            return i

        def pe_transpose(src_ap, dst_ap, bf16, extra_waits=(), n_avail=8, base=0,
                         copy_eng="vector", rows=128, cols=128):
            i = tslot(n_avail, base)
            if bf16:
                pview = bank(i)[:, 0:64].bitcast(BF16)
            else:
                pview = bank(i)[:, 0:128]
            pout = pview[0:rows, :]
            pcp = pview[0:rows, 0:cols]
            p.w("tensor", "sdc", t_consts)
            p.wt("tensor", tslot_free[i])
            for w_ in extra_waits:
                p.wt("tensor", w_)
            tpe = p.o("tensor", lambda e, pout=pout, src_ap=src_ap: e.transpose(
                pout, src_ap, identb_s[:, :] if bf16 else identf_s[:, :]), (("spe", 1),))
            sem = "sve" if copy_eng == "vector" else "sac"
            p.w(copy_eng, "spe", tpe)
            if copy_eng == "vector":
                tcp = p.o(copy_eng, lambda e, dst_ap=dst_ap, pcp=pcp: e.tensor_copy(
                    dst_ap, pcp), ((sem, 1),))
            else:
                tcp = p.o(copy_eng, lambda e, dst_ap=dst_ap, pcp=pcp: e.copy(
                    dst_ap, pcp), ((sem, 1),))
            tslot_free[i] = (sem, tcp)
            return tpe, (sem, tcp)

        xt_free = [None] * 4
        xt_load = [0] * 4
        last_htr = [None]
        G = 4

        def slotap(k):
            xt = acar(SLOT[k], 1024)
            xsq = acar(SLOT[k] + 1024, 1024)
            z = acar(SLOT[k] + 2048, 1024)
            htok = acarb(SLOT[k] + 3072, 1024)
            return xt, xsq, z, htok

        def norm_steps(t, k, rms, src_dma):
            xt, xsq, z, htok = slotap(k)
            c = k * 16
            s1 = st[:, c + 0:c + 1]; mcol = st[:, c + 1:c + 2]; ssc = st[:, c + 2:c + 3]
            msq = st[:, c + 3:c + 4]; var = st[:, c + 4:c + 5]; svc = st[:, c + 5:c + 6]
            rstd = st[:, c + 6:c + 7]; zss = st[:, c + 7:c + 8]; nrm = st[:, c + 8:c + 9]
            nmx = st[:, c + 9:c + 10]; rec = st[:, c + 10:c + 11]
            tk = {}

            def wt_ld(eng):
                for x_ in tk["ld"]:
                    p.wt(eng, x_)

            def st0():
                tk["ld"] = src_dma()

            def st1():
                wt_ld("scalar")
                tk["s1"] = p.o("scalar", lambda e: e.activation(
                    xsq, xt, AF.Identity, accum_out=s1), (("sac", 1),))
                tk["ss"] = p.o("scalar", lambda e: e.activation(
                    xsq, xt, AF.Square, accum_out=ssc), (("sac", 1),))

            def st2():
                p.w("vector", "sac", tk["s1"])
                tk["m"] = p.o("vector", lambda e: e.tensor_scalar(
                    mcol, s1, 1.0 / D, None, ALU.mult), (("sve", 1),))

            def st3():
                p.w("vector", "sve", tk["m"])
                tk["msq"] = p.o("vector", lambda e: e.tensor_mul(msq, mcol, mcol), (("sve", 1),))

            def st4():
                p.w("vector", "sve", tk["msq"])
                p.w("vector", "sac", tk["ss"])
                tk["var"] = p.o("vector", lambda e: e.scalar_tensor_tensor(
                    var, ssc, 1.0 / D, msq, ALU.mult, ALU.subtract), (("sve", 1),))

            def st5():
                p.w("scalar", "sve", tk["var"])
                p.w("scalar", "sve", t_eps)
                tk["sv"] = p.o("scalar", lambda e: e.activation(
                    svc, var, AF.Sqrt, bias=epscol[:, 0:1]), (("sac", 1),))

            def st6():
                p.w("vector", "sac", tk["sv"])
                tk["rstd"] = p.o("vector", lambda e: e.reciprocal(rstd, svc), (("sve", 1),))

            def st7():
                p.w("vector", "sve", tk["rstd"])
                if rms:
                    tk["z"] = p.o("vector", lambda e: e.tensor_scalar(
                        z, xt, mcol, rstd, ALU.subtract, ALU.mult), (("sve", 1),))
                else:
                    if last_htr[0] is not None:
                        p.w("vector", "spe", last_htr[0])
                    if tk.get("l2gate") is not None:
                        p.wt("vector", tk["l2gate"])
                    tk["z"] = p.o("vector", lambda e: e.tensor_scalar(
                        htok, xt, mcol, rstd, ALU.subtract, ALU.mult), (("sve", 1),))

            def st8():
                p.w("scalar", "sve", tk["z"])
                tk["zss"] = p.o("scalar", lambda e: e.activation(
                    xsq, z, AF.Square, accum_out=zss), (("sac", 1),))

            def st9():
                p.w("scalar", "sac", tk["zss"])
                tk["nrm"] = p.o("scalar", lambda e: e.activation(nrm, zss, AF.Sqrt), (("sac", 1),))

            def st10():
                p.w("vector", "sac", tk["nrm"])
                tv = p.o("vector", lambda e: e.tensor_scalar_max(nmx, nrm, 1e-12), (("sve", 1),))
                p.w("vector", "sve", tv)
                tk["rec"] = p.o("vector", lambda e: e.reciprocal(rec, nmx), (("sve", 1),))

            def st11():
                p.w("vector", "sve", tk["rec"])
                if last_htr[0] is not None:
                    p.w("vector", "spe", last_htr[0])
                tk["h"] = p.o("vector", lambda e: e.tensor_scalar(
                    htok, z, rec, float(np.sqrt(D)), ALU.mult, ALU.mult), (("sve", 1),))

            if rms:
                return [st0, st1, st2, st3, st4, st5, st6, st7, st8, st9, st10, st11], tk
            return [st0, st1, st2, st3, st4, st5, st6, st7], tk

        for g in range(NT1 // G):
            steps = []
            tks = []
            for k in range(G):
                t = g * G + k
                xt, xsq, z, htok = slotap(k)

                def mk_dma(t=t, k=k, xt=xt):
                    def src_dma():
                        xsem = "sx%d" % (k % 2)
                        p.wt("sync", xt_free[k])
                        tick = p.o("sync", lambda e: e.dma_start(
                            out=xt, in_=xs[t * 128:(t + 1) * 128, :]), ((xsem, 16),))
                        return [(xsem, tick)]
                    return src_dma
                stp, tk = norm_steps(t, k, True, mk_dma())
                steps.append(stp)
                tks.append(tk)
            nst = max(len(x) for x in steps)
            for si in range(nst):
                for k in range(G):
                    if si < len(steps[k]):
                        steps[k][si]()
            for k in range(G):
                t = g * G + k
                xt, xsq, z, htok = slotap(k)
                for kd in range(KD):
                    tpe, tcp = pe_transpose(htok[:, kd * 128:(kd + 1) * 128],
                                            h_dT(kd, t * 128, (t + 1) * 128), True,
                                            extra_waits=(("sve", tks[k]["h"]),),
                                            copy_eng="scalar" if kd % 2 else "vector")
                last_htr[0] = tpe
                if t >= 16:
                    for kd in range(KD):
                        ew = list(tks[k]["ld"]) if kd == 0 else []
                        tpe, tcp = pe_transpose(
                            xt[:, kd * 128:(kd + 1) * 128],
                            x_dT[:, kd * OWN + (t - 16) * 128:kd * OWN + (t - 16) * 128 + 128],
                            False, extra_waits=ew,
                            copy_eng="scalar" if kd % 2 else "vector")
                    xt_free[k] = ("spe", tpe)
                else:
                    xt_free[k] = ("sve", tks[k]["z"])

        t_h_all = p.cur("sve")
        t_h_sac = p.cur("sac")

        if stage >= 2:
            kv_s = acar(kv_off, 4096); a_s = acar(a_off, 4096); y_s = acar(y_off, 4096)
            wqr_free = [None, None]
            wqg_rdy = [0, 0]
            wqg_pe_done = [None, None]
            ci_load = [0, 0]

            def wq_chunk_sched(ci):
                bi = ci % 2
                wqr = acarb(wqr_off[bi], 1024)
                wqgb = acarb(wqg_off[bi], 1024)
                p.wt("gpsimd", wqr_free[bi])
                qsem = "sq%d" % bi
                ci_load[bi] = p.o("gpsimd", lambda e, wqr=wqr, ci=ci: e.dma_start(
                    out=wqr, in_=wq[ci]), ((qsem, 16),))
                p.w("vector", qsem, ci_load[bi])
                p.w("vector", "sdc", t_consts)
                p.wt("vector", wqg_pe_done[bi])
                tl = None
                for kd in range(KD):
                    tl = p.o("vector", lambda e, wqgb=wqgb, wqr=wqr, kd=kd: e.tensor_scalar(
                        wqgb[:, kd * 128:(kd + 1) * 128], wqr[:, kd * 128:(kd + 1) * 128],
                        gam_s[:, kd:kd + 1], None, ALU.mult), (("sve", 1),))
                wqr_free[bi] = ("sve", tl)
                wqg_rdy[bi] = tl

            t_p1pe = p.cur("spe")
            wq_chunk_sched(0)
            wq_chunk_sched(1)

            mm_slot_free = [None] * 8
            mm_slot_i = [0]

            def mm_slot():
                i = mm_slot_i[0] % 8
                mm_slot_i[0] += 1
                return i

            first_kv = [True]
            for j in range(KD):
                for part, dst in ((0, kv_s), (1, a_s)):
                    ci = 3 * j + part
                    bi = ci % 2
                    wqgb = acarb(wqg_off[bi], 1024)
                    p.w("tensor", "sve", wqg_rdy[bi])
                    if first_kv[0]:
                        p.w("tensor", "sve", t_h_all)
                        p.w("tensor", "sac", t_h_sac)
                        first_kv[0] = False
                    tpe = None
                    for n in range(8):
                        i = mm_slot()
                        psl = bank(i)
                        p.wt("tensor", mm_slot_free[i])
                        for kd in range(KD):
                            tpe = p.o("tensor", lambda e, psl=psl, wqgb=wqgb, kd=kd, n=n: e.matmul(
                                psl, wqgb[:, kd * 128:(kd + 1) * 128],
                                h_dT(kd, n * 512, (n + 1) * 512),
                                start=(kd == 0), stop=(kd == KD - 1)),
                                (("spe", 1),) if kd == KD - 1 else ())
                        if part == 0:
                            p.w("vector", "spe", tpe)
                            tev = p.o("vector", lambda e, dst=dst, psl=psl, n=n: e.tensor_copy(
                                dst[:, n * 512:(n + 1) * 512], psl), (("sve", 1),))
                            mm_slot_free[i] = ("sve", tev)
                        else:
                            p.w("scalar", "spe", tpe)
                            tev = p.o("scalar", lambda e, dst=dst, psl=psl, n=n: e.activation(
                                dst[:, n * 512:(n + 1) * 512], psl, AF.Sigmoid), (("sac", 1),))
                            mm_slot_free[i] = ("sac", tev)
                    wqg_pe_done[bi] = ("spe", tpe)
                    if ci + 2 < 24:
                        wq_chunk_sched(ci + 2)
                p.w("vector", "sac", p.cur("sac"))
                if j == 0:
                    p.w("vector", "spe", t_p1pe)
                p.o("vector", lambda e: e.tensor_tensor_scan(
                    y_s[:, :], a_s[:, :], kv_s[:, :], 0.0, ALU.mult, ALU.add), (("sve", 1),))
                ci = 3 * j + 2
                bi = ci % 2
                wqgb = acarb(wqg_off[bi], 1024)
                p.w("tensor", "sve", wqg_rdy[bi])
                tpe = None
                for n in range(4):
                    i = mm_slot()
                    psl = bank(i)
                    p.wt("tensor", mm_slot_free[i])
                    for kd in range(KD):
                        tpe = p.o("tensor", lambda e, psl=psl, wqgb=wqgb, kd=kd, n=n: e.matmul(
                            psl, wqgb[:, kd * 128:(kd + 1) * 128],
                            h_dT(kd, 2048 + n * 512, 2048 + (n + 1) * 512),
                            start=(kd == 0), stop=(kd == KD - 1)),
                            (("spe", 1),) if kd == KD - 1 else ())
                    p.w("vector", "spe", tpe)
                    t1 = acar(t1_off, 512)
                    p.o("vector", lambda e, t1=t1, psl=psl, n=n: e.scalar_tensor_tensor(
                        t1, psl, 1.0, y_s[:, 2048 + n * 512:2048 + (n + 1) * 512],
                        ALU.mult, ALU.mult), (("sve", 1),))
                    xsl = x_dT[:, j * OWN + n * 512:j * OWN + (n + 1) * 512]
                    tv = p.o("vector", lambda e, xsl=xsl, t1=t1: e.tensor_add(xsl, xsl, t1), (("sve", 1),))
                    mm_slot_free[i] = ("sve", tv)
                wqg_pe_done[bi] = ("spe", tpe)
                if ci + 2 < 24:
                    wq_chunk_sched(ci + 2)

        t_p2pe = p.cur("spe")

        t_ln2dma = 0
        if stage >= 3:
            ln2dma_slot = [None] * 4
            for g in range(NT3 // G):
                steps = []
                tks = []
                for k in range(G):
                    t = g * G + k
                    xt, xsq, z, htok = slotap(k)

                    def mk_src(t=t, k=k, xt=xt, g=g):
                        def src_dma():
                            out = []
                            for kd in range(KD):
                                ew = [xt_free[k]] if kd == 0 else []
                                if g == 0 and k == 0 and kd == 0:
                                    ew.append(("sve", p.cur("sve")))
                                tpe, tcp = pe_transpose(
                                    x_dT[:, kd * OWN + t * 128:kd * OWN + t * 128 + 128],
                                    xt[:, kd * 128:(kd + 1) * 128], False,
                                    extra_waits=ew,
                                    copy_eng="scalar" if kd % 2 else "vector")
                                if kd >= KD - 2:
                                    out.append(tcp)
                            return out
                        return src_dma

                    stp, tk = norm_steps(t, k, False, mk_src())
                    tk["l2gate"] = ln2dma_slot[k]
                    steps.append(stp)
                    tks.append(tk)
                nst = max(len(x) for x in steps)
                for si in range(nst):
                    for k in range(G):
                        if si < len(steps[k]):
                            steps[k][si]()
                for k in range(G):
                    t = g * G + k
                    xt, xsq, z, htok = slotap(k)
                    p.w("sync", "sve", tks[k]["z"])
                    t_ln2dma = p.o("sync", lambda e, htok=htok, t=t: e.dma_start(
                        out=ln2tok[t * 128:(t + 1) * 128, :], in_=htok), (("sl2", 16),))
                    ln2dma_slot[k] = ("sl2", t_ln2dma)
                    for kd in range(KD):
                        tpe, _ = pe_transpose(htok[:, kd * 128:(kd + 1) * 128],
                                              ln2_dT(kd, t * 128, (t + 1) * 128), True,
                                              extra_waits=(("sve", tks[k]["z"]),), copy_eng="scalar")
                    last_htr[0] = tpe
                    xt_free[k] = ("sve", tks[k]["z"])

        t_ln2 = p.cur("sve")
        t_ln2sac = p.cur("sac")
        t_ln2tok_dma = p.cur("sl2")

        if stage >= 4:
            mskT = [acar(mskT_off[0], 2048), acar(mskT_off[1], 2048)]
            lg_ev = None
            mt_ev = None
            for t in range(NT3):
                bi = t % 2
                c = bi * 16
                ltile = p4b[:, 0:7]
                msk1 = p4b[:, 8:15]
                l2 = p4b[:, 16:23]
                msk2 = p4b[:, 24:31]
                m1 = st[:, c + 0:c + 1]; m2 = st[:, c + 1:c + 2]; d12 = st[:, c + 2:c + 3]
                d21 = st[:, c + 5:c + 6]
                sg = sgall[:, t:t + 1]; tau = tauall[:, t:t + 1]
                pl = pD[:, 0:7]
                p.w("tensor", "srw", t_rw)
                if t == 0:
                    p.w("tensor", "sve", t_ln2)
                if lg_ev is not None:
                    p.w("tensor", "sve", lg_ev)
                tpe = None
                for kd in range(KD):
                    tpe = p.o("tensor", lambda e, pl=pl, kd=kd, t=t: e.matmul(
                        pl, ln2_dT(kd, t * 128, (t + 1) * 128), rw_s[:, kd * 7:(kd + 1) * 7],
                        start=(kd == 0), stop=(kd == KD - 1)),
                        (("spe", 1),) if kd == KD - 1 else ())
                p.w("vector", "spe", tpe)
                if mt_ev is not None:
                    p.w("vector", "spe", mt_ev)
                lg_ev = p.o("vector", lambda e, ltile=ltile, pl=pl: e.tensor_copy(ltile, pl), (("sve", 1),))
                p.w("vector", "sve", lg_ev)
                tv = p.o("vector", lambda e, m1=m1, ltile=ltile: e.reduce_max(m1, ltile, axis=AX), (("sve", 1),))
                p.w("vector", "sve", tv)
                tv = p.o("vector", lambda e, msk1=msk1, ltile=ltile, m1=m1: e.tensor_scalar(
                    msk1, ltile, m1, None, ALU.is_ge), (("sve", 1),))
                p.w("vector", "sve", tv)
                tv = p.o("vector", lambda e, l2=l2, msk1=msk1, ltile=ltile: e.scalar_tensor_tensor(
                    l2, msk1, -1e30, ltile, ALU.mult, ALU.add), (("sve", 1),))
                p.w("vector", "sve", tv)
                tv = p.o("vector", lambda e, m2=m2, l2=l2: e.reduce_max(m2, l2, axis=AX), (("sve", 1),))
                p.w("vector", "sve", tv)
                tv = p.o("vector", lambda e, d12=d12, m1=m1, m2=m2: e.tensor_scalar(
                    d12, m1, m2, None, ALU.subtract), (("sve", 1),))
                p.w("vector", "sve", tv)
                tvb = p.o("vector", lambda e, d21=d21, m2=m2, m1=m1: e.tensor_scalar(
                    d21, m2, m1, None, ALU.subtract), (("sve", 1),))
                tv2 = p.o("vector", lambda e, msk2=msk2, l2=l2, m2=m2: e.tensor_scalar(
                    msk2, l2, m2, None, ALU.is_ge), (("sve", 1),))
                p.w("scalar", "sve", tvb)
                p.o("scalar", lambda e, sg=sg, d12=d12: e.activation(sg, d12, AF.Sigmoid), (("sac", 1),))
                p.o("scalar", lambda e, tau=tau, d21=d21: e.activation(tau, d21, AF.Sigmoid), (("sac", 1),))
                for r in range(2):
                    src = msk1 if r == 0 else msk2
                    mt_pe, _ = pe_transpose(
                        src, mskT[r][0:7, t * 128:(t + 1) * 128], False,
                        extra_waits=(("sve", tv2),), n_avail=6, base=0, rows=7)
                mt_ev = mt_pe
        t_p4 = p.cur("sve")

        if stage >= 5:
            ones7 = acar(PB[0], 2048)
            csB = acar(PB[1], 2048)
            tB2 = acar(PB[2], 2048)
            tB3 = acar(PB[3], 2048)
            selB = acar(PB[4], 2048)
            fdB = acar(PB[5], 2048)
            iB = acari(PB[6], 2048)
            iN = acari(PB[2], 2048)
            p.w("vector", "szr", t_prezero)
            t_o7 = p.o("vector", lambda e: e.memset(ones7[0:7, :], 1.0), (("sve", 1),))
            t_scat_ids = []
            msum = tB2
            p.w("vector", "sve", t_p4)
            tv = p.o("vector", lambda e: e.tensor_add(msum[0:7, :], mskT[0][0:7, :], mskT[1][0:7, :]), (("sve", 1),))
            p.w("vector", "sve", tv)
            tv = p.o("vector", lambda e: e.tensor_tensor_scan(
                csB[0:7, :], ones7[0:7, :], msum[0:7, :], 0.0, ALU.mult, ALU.add), (("sve", 1),))
            p.w("vector", "sve", tv)
            tv = p.o("vector", lambda e: e.tensor_scalar(
                tB3[0:7, :], csB[0:7, :], 1.0, None, ALU.subtract), (("sve", 1),))
            p.w("vector", "sve", tv)
            tv = p.o("vector", lambda e: e.tensor_copy(iB[0:7, :], tB3[0:7, :]), (("sve", 1),))
            p.w("vector", "sve", tv)
            tv = p.o("vector", lambda e: e.tensor_single_scalar(
                iN[0:7, :], iB[0:7, :], 127, ALU.bitwise_and), (("sve", 1),))
            p.w("vector", "sve", tv)
            tv = p.o("vector", lambda e: e.tensor_single_scalar(
                iB[0:7, :], iB[0:7, :], 7, ALU.arith_shift_right), (("sve", 1),))
            p.w("vector", "sve", tv)
            tv = p.o("vector", lambda e: e.tensor_copy(tB3[0:7, :], iN[0:7, :]), (("sve", 1),))
            p.w("vector", "sve", tv)
            tv = p.o("vector", lambda e: e.tensor_copy(tB2[0:7, :], iB[0:7, :]), (("sve", 1),))
            p.w("vector", "sve", tv)
            tv = p.o("vector", lambda e: e.scalar_tensor_tensor(
                tB3[0:7, :], tB3[0:7, :], 56.0, tB2[0:7, :], ALU.mult, ALU.add), (("sve", 1),))
            p.w("vector", "sve", tv)
            tv = p.o("vector", lambda e: e.tensor_scalar(
                tB3[0:7, :], tB3[0:7, :], ecol_s[0:7, 0:1], None, ALU.add), (("sve", 1),))
            p.w("vector", "sve", tv)
            tv = p.o("vector", lambda e: e.tensor_single_scalar(
                tB2[0:7, :], csB[0:7, :], 2 * CAP + 0.5, ALU.is_ge), (("sve", 1),))
            p.w("vector", "sve", tv)
            tv = p.o("vector", lambda e: e.scalar_tensor_tensor(
                tB3[0:7, :], tB2[0:7, :], SENT, tB3[0:7, :], ALU.mult, ALU.add), (("sve", 1),))
            p.w("vector", "sve", tv)
            for r in range(2):
                mt = mskT[r]
                t_sel = p.o("vector", lambda e, mt=mt: e.tensor_tensor(
                    selB[0:7, :], tB3[0:7, :], mt[0:7, :], ALU.mult), (("sve", 1),))
                t_fd = None
                for nch in range(4):
                    bk = bank(6 + nch % 2)
                    p.w("tensor", "sve", t_sel)
                    p.wt("tensor", tslot_free[6 + nch % 2])
                    tmm = p.o("tensor", lambda e, bk=bk, nch=nch: e.matmul(
                        bk[0:1, :], ones7[0:7, 0:1], selB[0:7, nch * 512:(nch + 1) * 512],
                        start=True, stop=True), (("spe", 1),))
                    p.w("vector", "spe", tmm)
                    t_fd = p.o("vector", lambda e, bk=bk, nch=nch: e.tensor_copy(
                        csB[0:1, nch * 512:(nch + 1) * 512], bk[0:1, :]), (("sve", 1),))
                    tslot_free[6 + nch % 2] = ("sve", t_fd)
                t_fb = None
                for nch in range(4):
                    bk = bank(6 + nch % 2)
                    p.w("tensor", "sve", t_fd)
                    p.wt("tensor", tslot_free[6 + nch % 2])
                    tmm = p.o("tensor", lambda e, bk=bk, nch=nch: e.matmul(
                        bk[:, :], ones1[0:1, :], csB[0:1, nch * 512:(nch + 1) * 512],
                        start=True, stop=True), (("spe", 1),))
                    p.w("vector", "spe", tmm)
                    t_fb = p.o("vector", lambda e, bk=bk, nch=nch: e.tensor_copy(
                        fdB[:, nch * 512:(nch + 1) * 512], bk[:, :]), (("sve", 1),))
                    tslot_free[6 + nch % 2] = ("sve", t_fb)
                for a in range(16):
                    tpe, tcp = pe_transpose(fdB[:, a * 128:(a + 1) * 128],
                                            fdt_f[:, r * 16 + a:r * 16 + a + 1],
                                            False, extra_waits=(("sve", t_fb),),
                                            n_avail=2, base=6, cols=1)
            p.w("vector", "sve", p.cur("sve"))
            tv = p.o("vector", lambda e: e.tensor_scalar_min(
                fdt_f[:, :], fdt_f[:, :], float(128 * 56)), (("sve", 1),))
            p.w("vector", "sve", tv)
            t_ci = p.o("vector", lambda e: e.tensor_copy(fdt_i[:, :], fdt_f[:, :]), (("sve", 1),))
        else:
            t_ci = p.cur("sve")
        t_gidx = None
        wgu_free = {("g", 0): None, ("g", 1): None, ("u", 0): None, ("u", 1): None}
        wd_free = [None, None, None, None]
        wgu_load = {}
        pgA_free = ("sve", t_p4)
        puB_free = ("sve", t_p4)
        pdC_free = ("sve", t_p4)
        sil_free = [None]
        first_wdma = [True]
        t_down_prev = [None]

        def wdma(src_ap, off, cols_bf, free, semname):
            if first_wdma[0]:
                p.w("gpsimd", "spe", t_p2pe)
                first_wdma[0] = False
            p.wt("gpsimd", free)
            return (semname, p.o("gpsimd", lambda e, src_ap=src_ap, off=off, cols_bf=cols_bf: e.dma_start(
                out=acarb(off, cols_bf), in_=src_ap), ((semname, 16),)))

        def wd_slice(src_e, sl):
            dj, mq = sl // 4, sl % 4
            return src_e[dj][:, mq * 1024:(mq + 1) * 1024]

        prep_done = [False]

        def emit_prep_dmas():
            prep_done[0] = True
            p.w("gpsimd", "spf", t_pref)
            p.w("gpsimd", "sve", t_ci)
            t_sc = None
            for c in range(32):
                t_sc = p.o("gpsimd", lambda e, c=c: e.indirect_dma_start(
                    out=listf, out_offset=IOA(fdt_i[:, c:c + 1], 0),
                    in_=tokid_s[:, c:c + 1], in_offset=None), (("ssc", 16),))
            p.w("gpsimd", "ssc", t_sc)
            t_rb = p.o("gpsimd", lambda e: e.dma_start(
                out=gidx_f[:, :], in_=listf[0:128 * 56].rearrange("(q x) one -> q (x one)", q=128)), (("srb", 16),))
            p.w("vector", "srb", t_rb)
            tv = p.o("vector", lambda e: e.tensor_scalar_min(
                gsc_f[:, :], gidx_f[:, :], float(2 * OWN)), (("sve", 1),))
            p.w("vector", "sve", tv)
            tv = p.o("vector", lambda e: e.tensor_copy(gsc_i[:, :], gsc_f[:, :]), (("sve", 1),))
            p.w("vector", "sve", tv)
            tv = p.o("vector", lambda e: e.tensor_single_scalar(
                gidx_i[:, :], gidx_f[:, :], OWN - 0.5, ALU.is_ge), (("sve", 1),))
            p.w("vector", "sve", tv)
            tv = p.o("vector", lambda e: e.tensor_copy(gsc_f[:, :], gidx_i[:, :]), (("sve", 1),))
            p.w("vector", "sve", tv)
            tv = p.o("vector", lambda e: e.scalar_tensor_tensor(
                gidx_f[:, :], gsc_f[:, :], -float(OWN), gidx_f[:, :], ALU.mult, ALU.add), (("sve", 1),))
            p.w("vector", "sve", tv)
            tv = p.o("vector", lambda e: e.tensor_scalar_min(
                gidx_f[:, :], gidx_f[:, :], float(OWN)), (("sve", 1),))
            p.w("vector", "sve", tv)
            tk_ = p.o("vector", lambda e: e.tensor_copy(gidx_i[:, :], gidx_f[:, :]), (("sve", 1),))
            return tk_


        if stage >= 6:
            for sp in range(2):
                tok0 = sp * 1024
                for g_i in (0, 1):
                    wgu_load[("g", g_i)] = wdma(swg[g_i], wg_off[g_i % 2], 2048, wgu_free[("g", g_i % 2)], "swg%d" % (g_i % 2))
                    wgu_load[("u", g_i)] = wdma(swu[g_i], wu_off[g_i % 2], 2048, wgu_free[("u", g_i % 2)], "swu%d" % (g_i % 2))
                for sl in (0, 1, 2, 3):
                    wgu_load[("d", sl)] = wdma(wd_slice(swd, sl), wd_off[sl % 4], 1024, wd_free[sl % 4], "swd%d" % (sl % 2))
                silb = acar(ttok_off[1], 1024)
                for m in range(MH):
                    g_i = m // 2
                    mm = m % 2
                    wgb = acarb(wg_off[g_i % 2], 2048)
                    wub = acarb(wu_off[g_i % 2], 2048)
                    p.wt("tensor", wgu_load[("g", g_i)])
                    p.wt("tensor", wgu_load[("u", g_i)])
                    if m == 0:
                        p.w("tensor", "sac", t_ln2sac)
                    p.wt("tensor", pgA_free)
                    t_g = None
                    for kd in range(KD):
                        for tt in range(2):
                            t_g = p.o("tensor", lambda e, wgb=wgb, mm=mm, kd=kd, tt=tt, tok0=tok0: e.matmul(
                                pA[:, tt * 512:(tt + 1) * 512],
                                wgb[:, (mm * 8 + kd) * 128:(mm * 8 + kd) * 128 + 128],
                                ln2_dT(kd, tok0 + tt * 512, tok0 + (tt + 1) * 512),
                                start=(kd == 0), stop=(kd == KD - 1)),
                                (("spe", 1),) if (kd == KD - 1 and tt == 1) else ())
                    p.wt("tensor", puB_free)
                    t_u = None
                    for kd in range(KD):
                        for tt in range(2):
                            t_u = p.o("tensor", lambda e, wub=wub, mm=mm, kd=kd, tt=tt, tok0=tok0: e.matmul(
                                pB[:, tt * 512:(tt + 1) * 512],
                                wub[:, (mm * 8 + kd) * 128:(mm * 8 + kd) * 128 + 128],
                                ln2_dT(kd, tok0 + tt * 512, tok0 + (tt + 1) * 512),
                                start=(kd == 0), stop=(kd == KD - 1)),
                                (("spe", 1),) if (kd == KD - 1 and tt == 1) else ())
                    p.w("scalar", "spe", t_g)
                    p.wt("scalar", sil_free[0])
                    if sp == 0 and m == 0:
                        p.w("scalar", "sve", t_ci)
                        p.w("scalar", "szr", t_prezero)
                    t_sil = p.o("scalar", lambda e, silb=silb: e.activation(
                        silb, pA[:, :], AF.Silu), (("sac", 1),))
                    pgA_free = ("sac", t_sil)
                    p.w("vector", "spe", t_u)
                    p.w("vector", "sac", t_sil)
                    if m == 0 and t_down_prev[0] is not None:
                        p.w("vector", "spe", t_down_prev[0])
                    t_mul = p.o("vector", lambda e, m=m, silb=silb: e.tensor_tensor(
                        h1(m, 0, 1024, "sh"), pB[:, :], silb, ALU.mult), (("sve", 1),))
                    puB_free = ("sve", t_mul)
                    sil_free[0] = ("sve", t_mul)
                    if mm == 1:
                        wgu_free[("g", g_i % 2)] = ("sac", t_sil)
                        wgu_free[("u", g_i % 2)] = ("sve", t_mul)
                        if g_i + 2 < 16:
                            wgu_load[("g", g_i + 2)] = wdma(swg[g_i + 2], wg_off[(g_i + 2) % 2],
                                                            2048, wgu_free[("g", (g_i + 2) % 2)], "swg%d" % ((g_i + 2) % 2))
                            wgu_load[("u", g_i + 2)] = wdma(swu[g_i + 2], wu_off[(g_i + 2) % 2],
                                                            2048, wgu_free[("u", (g_i + 2) % 2)], "swu%d" % ((g_i + 2) % 2))
                t_h1_all = p.cur("sve")
                t_d = None
                for dj in range(8):
                    p.wt("tensor", pdC_free)
                    t_d = None
                    for mq in range(4):
                        sl = dj * 4 + mq
                        p.wt("tensor", wgu_load[("d", sl)])
                        if sl == 0:
                            p.w("tensor", "sve", t_h1_all)
                        wdb = acarb(wd_off[sl % 4], 1024)
                        for mloc in range(8):
                            m = mq * 8 + mloc
                            for tt in range(2):
                                t_d = p.o("tensor", lambda e, wdb=wdb, m=m, mloc=mloc, tt=tt: e.matmul(
                                    pC[:, tt * 512:(tt + 1) * 512],
                                    wdb[:, mloc * 128:(mloc + 1) * 128],
                                    h1(m, tt * 512, (tt + 1) * 512, "sh"),
                                    start=(m == 0), stop=(m == MH - 1)),
                                    (("spe", 1),) if (mloc == 7 and tt == 1) else ())
                        wd_free[sl % 4] = ("spe", t_d if t_d is not None else 0)
                        if sl + 4 < 32:
                            wgu_load[("d", sl + 4)] = wdma(wd_slice(swd, sl + 4), wd_off[sl % 4],
                                                           1024, wd_free[sl % 4], "swd%d" % (sl % 2))
                    xsl = x_dT[:, dj * OWN + tok0:dj * OWN + tok0 + 1024]
                    p.w("vector", "spe", t_d)
                    t_acc = p.o("vector", lambda e, xsl=xsl: e.tensor_add(
                        xsl, xsl, pC[:, :]), (("sve", 1),))
                    pdC_free = ("sve", t_acc)
                t_down_prev[0] = t_d
                if sp == 0 and stage >= 5:
                    t_gidx_mid = emit_prep_dmas()

        if stage >= 5 and not prep_done[0]:
            t_gidx = emit_prep_dmas()
        elif prep_done[0]:
            t_gidx = t_gidx_mid
        else:
            t_gidx = p.cur("sve")

        gt_free = [("sve", t_gt0), ("sve", t_gt1)]
        gt_load = [0, 0]
        t_scat_out = []
        scat_last = [None, None]
        if stage >= 7:
            t_shared_done = p.cur("sve")
            t_shared_pe = p.cur("spe")
            tslot_free[6] = ("sve", t_shared_done)
            tslot_free[7] = ("sve", t_shared_done)
            gt_free = [("spe", t_shared_pe), ("spe", t_shared_pe)]
            pgA_free = ("sve", t_shared_done)
            puB_free = ("sve", t_shared_done)
            pdC_free = ("sve", t_shared_done)
            t_gu_done = ("spe", t_shared_pe)
            t_gln_buf = [None]

            def emit_gather(li):
                for t8 in range(8):
                    gb = t8 % 2
                    col = li * 8 + t8
                    p.wt("gpsimd", gt_free[gb])
                    p.w("gpsimd", "sve", t_gidx)
                    p.w("gpsimd", "sl2", t_ln2tok_dma)
                    p.w("gpsimd", "szr", t_prezero)
                    gsem = "sg%d" % gb
                    gt_load[gb] = p.o("gpsimd", lambda e, gb=gb, col=col: e.indirect_dma_start(
                        out=acarb(gtok_off[gb], 1024), out_offset=None,
                        in_=ln2tok, in_offset=IOA(gidx_i[:, col:col + 1], 0)), ((gsem, 16),))
                    for kd in range(KD):
                        tpe, tcp = pe_transpose(
                            acarb(gtok_off[gb], 1024)[:, kd * 128:(kd + 1) * 128],
                            gln_dT(0, kd, t8 * 128, (t8 + 1) * 128), True,
                            extra_waits=((gsem, gt_load[gb]), t_gu_done),
                            n_avail=2, base=6)
                    gt_free[gb] = ("spe", tpe)
                t_gln_buf[0] = p.cur("sve")

            emit_gather(0)
            nx_load = {}
            for li in range(7):
                e_i = li
                if ("g", 0) in nx_load:
                    for g_i in (0, 1):
                        wgu_load[("g", g_i)] = nx_load[("g", g_i)]
                        wgu_load[("u", g_i)] = nx_load[("u", g_i)]
                else:
                    for g_i in (0, 1):
                        wgu_load[("g", g_i)] = wdma(wg[e_i, g_i], wg_off[g_i % 2], 2048, wgu_free[("g", g_i % 2)], "swg%d" % (g_i % 2))
                        wgu_load[("u", g_i)] = wdma(wu[e_i, g_i], wu_off[g_i % 2], 2048, wgu_free[("u", g_i % 2)], "swu%d" % (g_i % 2))
                for sl in (0, 1, 2, 3):
                    wgu_load[("d", sl)] = wdma(wd_slice(wd[e_i], sl), wd_off[sl % 4], 1024, wd_free[sl % 4], "swd%d" % (sl % 2))
                silb = acar(sil_off, 1024)
                t_gln_cur = t_gln_buf[0]
                t_u = None
                for m in range(MH):
                    g_i = m // 2
                    mm = m % 2
                    wgb = acarb(wg_off[g_i % 2], 2048)
                    wub = acarb(wu_off[g_i % 2], 2048)
                    p.wt("tensor", wgu_load[("g", g_i)])
                    p.wt("tensor", wgu_load[("u", g_i)])
                    if m == 0:
                        p.w("tensor", "sve", t_gln_cur)
                    p.wt("tensor", pgA_free)
                    t_g = None
                    for kd in range(KD):
                        for tt in range(2):
                            t_g = p.o("tensor", lambda e, wgb=wgb, mm=mm, kd=kd, tt=tt: e.matmul(
                                pA[:, tt * 512:(tt + 1) * 512],
                                wgb[:, (mm * 8 + kd) * 128:(mm * 8 + kd) * 128 + 128],
                                gln_dT(0, kd, tt * 512, (tt + 1) * 512),
                                start=(kd == 0), stop=(kd == KD - 1)),
                                (("spe", 1),) if (kd == KD - 1 and tt == 1) else ())
                    p.wt("tensor", puB_free)
                    t_u = None
                    for kd in range(KD):
                        for tt in range(2):
                            t_u = p.o("tensor", lambda e, wub=wub, mm=mm, kd=kd, tt=tt: e.matmul(
                                pB[:, tt * 512:(tt + 1) * 512],
                                wub[:, (mm * 8 + kd) * 128:(mm * 8 + kd) * 128 + 128],
                                gln_dT(0, kd, tt * 512, (tt + 1) * 512),
                                start=(kd == 0), stop=(kd == KD - 1)),
                                (("spe", 1),) if (kd == KD - 1 and tt == 1) else ())
                    p.w("scalar", "spe", t_g)
                    p.wt("scalar", sil_free[0])
                    t_sil = p.o("scalar", lambda e, silb=silb: e.activation(
                        silb, pA[:, :], AF.Silu), (("sac", 1),))
                    pgA_free = ("sac", t_sil)
                    p.w("vector", "spe", t_u)
                    p.w("vector", "sac", t_sil)
                    if m == 0 and t_down_prev[0] is not None:
                        p.w("vector", "spe", t_down_prev[0])
                    t_mul = p.o("vector", lambda e, m=m, silb=silb: e.tensor_tensor(
                        h1(m, 0, 1024, "rt"), pB[:, :], silb, ALU.mult), (("sve", 1),))
                    puB_free = ("sve", t_mul)
                    sil_free[0] = ("sve", t_mul)
                    if mm == 1:
                        wgu_free[("g", g_i % 2)] = ("sac", t_sil)
                        wgu_free[("u", g_i % 2)] = ("sve", t_mul)
                        if g_i + 2 < 16:
                            wgu_load[("g", g_i + 2)] = wdma(wg[e_i, g_i + 2], wg_off[(g_i + 2) % 2],
                                                            2048, wgu_free[("g", (g_i + 2) % 2)], "swg%d" % ((g_i + 2) % 2))
                            wgu_load[("u", g_i + 2)] = wdma(wu[e_i, g_i + 2], wu_off[(g_i + 2) % 2],
                                                            2048, wgu_free[("u", (g_i + 2) % 2)], "swu%d" % ((g_i + 2) % 2))
                t_gu_done = ("spe", t_u)
                t_h1_all = p.cur("sve")
                if li + 1 < 7:
                    e_n = li + 1
                    for g_i in (0, 1):
                        nx_load[("g", g_i)] = wdma(wg[e_n, g_i], wg_off[g_i % 2], 2048, wgu_free[("g", g_i % 2)], "swg%d" % (g_i % 2))
                        nx_load[("u", g_i)] = wdma(wu[e_n, g_i], wu_off[g_i % 2], 2048, wgu_free[("u", g_i % 2)], "swu%d" % (g_i % 2))
                    emit_gather(li + 1)
                dout = acarb(dout_off, 8192)
                t_d = None
                for dj in range(8):
                    p.wt("tensor", pdC_free)
                    t_d = None
                    for mq in range(4):
                        sl = dj * 4 + mq
                        p.wt("tensor", wgu_load[("d", sl)])
                        if sl == 0:
                            p.w("tensor", "sve", t_h1_all)
                        wdb = acarb(wd_off[sl % 4], 1024)
                        for mloc in range(8):
                            m = mq * 8 + mloc
                            for tt in range(2):
                                t_d = p.o("tensor", lambda e, wdb=wdb, m=m, mloc=mloc, tt=tt: e.matmul(
                                    pC[:, tt * 512:(tt + 1) * 512],
                                    wdb[:, mloc * 128:(mloc + 1) * 128],
                                    h1(m, tt * 512, (tt + 1) * 512, "rt"),
                                    start=(m == 0), stop=(m == MH - 1)),
                                    (("spe", 1),) if (mloc == 7 and tt == 1) else ())
                        wd_free[sl % 4] = ("spe", t_d if t_d is not None else 0)
                        if sl + 4 < 32:
                            wgu_load[("d", sl + 4)] = wdma(wd_slice(wd[e_i], sl + 4), wd_off[sl % 4],
                                                           1024, wd_free[sl % 4], "swd%d" % (sl % 2))
                    p.w("vector", "spe", t_d)
                    t_dc = p.o("vector", lambda e, dout=dout, dj=dj: e.tensor_copy(
                        dout[:, dj * 1024:(dj + 1) * 1024], pC[:, :]), (("sve", 1),))
                    pdC_free = ("sve", t_dc)
                t_down_prev[0] = t_d
                t_dout = p.cur("sve")
                for t8 in range(8):
                    tb = t8 % 2
                    ttok = acarb(ttok_off[tb], 1024)
                    col = li * 8 + t8
                    ssem = "so%d" % tb
                    tcp = None
                    for dj in range(KD):
                        ew = [("sve", t_dout)]
                        if dj == 0 and scat_last[tb] is not None:
                            ew.append(scat_last[tb])
                        tpe, tcp = pe_transpose(
                            dout[:, dj * 1024 + t8 * 128:dj * 1024 + t8 * 128 + 128],
                            ttok[:, dj * 128:(dj + 1) * 128], True,
                            extra_waits=ew, n_avail=2, base=6)
                    if li == 0 and t8 == 0:
                        p.w("gpsimd", "szr", t_prezero)
                    p.wt("gpsimd", tcp)
                    ts_ = p.o("gpsimd", lambda e, ttok=ttok, col=col: e.indirect_dma_start(
                        out=accC, out_offset=IOA(gsc_i[:, col:col + 1], 0),
                        in_=ttok, in_offset=None), ((ssem, 16),))
                    scat_last[tb] = (ssem, ts_)
                    t_scat_out.append(ts_)

        t_scat0 = p.cur("so0")
        t_scat1 = p.cur("so1")

        if stage >= 5:
            out_tick = [None, None, None, None]
            a_load = [None, None, None, None]
            p.w("vector", "sac", p.cur("sac"))
            p.w("sync", "spe", p.cur("spe"))
            p.w("sync", "so0", t_scat0)
            p.w("sync", "so1", t_scat1)
            p.w("sync", "szr", t_prezero)
            for t in range(NT3):
                b = t % 4
                xtk = acar(fxt_off[b], 1024)
                a1t = acarb(fa1_off[b], 1024)
                a2t = acarb(fa2_off[b], 1024)
                p.wt("sync", out_tick[b])
                a_load[b] = p.o("sync", lambda e, a1t=a1t, t=t: e.dma_start(
                    out=a1t, in_=accC[t * 128:(t + 1) * 128, :]), (("sal", 16),))
                a_load[b] = p.o("sync", lambda e, a2t=a2t, t=t: e.dma_start(
                    out=a2t, in_=accC[OWN + t * 128:OWN + (t + 1) * 128, :]), (("sal", 16),))
                tcp = None
                for kd in range(KD):
                    ew = []
                    if t == 0 and kd == 0:
                        ew.append(("sve", p.cur("sve")))
                    tpe, tcp = pe_transpose(
                        x_dT[:, kd * OWN + t * 128:kd * OWN + t * 128 + 128],
                        xtk[:, kd * 128:(kd + 1) * 128], False,
                        extra_waits=ew, n_avail=2, base=6)
                p.wt("vector", tcp)
                p.w("vector", "sal", a_load[b])
                tv = p.o("vector", lambda e, a1t=a1t, xtk=xtk, t=t: e.scalar_tensor_tensor(
                    xtk, a1t, sgall[:, t:t + 1], xtk, ALU.mult, ALU.add), (("sve", 1),))
                p.w("vector", "sve", tv)
                tv = p.o("vector", lambda e, a2t=a2t, xtk=xtk, t=t: e.scalar_tensor_tensor(
                    xtk, a2t, tauall[:, t:t + 1], xtk, ALU.mult, ALU.add), (("sve", 1),))
                osem = "sou" if b % 2 == 0 else "sov"
                p.w("sync", "sve", tv)
                t_out = p.o("sync", lambda e, xtk=xtk, t=t: e.dma_start(
                    out=yo[t * 128:(t + 1) * 128, :], in_=xtk), ((osem, 16),))
                out_tick[b] = (osem, t_out)
            p.w("sync", "sou", p.cur("sou"))
            p.w("sync", "sov", p.cur("sov"))
        else:
            p.w("sync", "sve", p.cur("sve"))
            p.o("sync", lambda e: e.dma_start(
                out=yo.rearrange("(q a) d -> q (a d)", q=128), in_=acar(0, 16384)), (("sou", 16),))
            p.w("sync", "sou", 16)

        p.emit(blk)
    return nc


def _host_inputs(L0_attn_x, L0_attn_w_qkva, L0_attn_gamma, L0_ffn_router_w,
                 L0_ffn_w_gate, L0_ffn_w_up, L0_ffn_w_down,
                 L0_ffn_sw_gate, L0_ffn_sw_up, L0_ffn_sw_down):
    f = np.float32
    x = np.ascontiguousarray(np.asarray(L0_attn_x, dtype=f))
    wqkva = np.asarray(L0_attn_w_qkva, dtype=f)
    gamma = np.asarray(L0_attn_gamma, dtype=f)
    router = np.asarray(L0_ffn_router_w, dtype=f)

    W = wqkva.reshape(3, 8, 128, 8, 128)
    arr = W.transpose(1, 0, 4, 3, 2)
    wq_t = np.ascontiguousarray(arr[:, [1, 2, 0]].reshape(24, 128, 1024))

    gam_t = np.ascontiguousarray(gamma.reshape(8, 128).T)
    rw_t = np.ascontiguousarray(router.reshape(7, 8, 128).transpose(2, 1, 0).reshape(128, 56))
    identf = np.eye(128, dtype=f)
    identb = np.eye(128, dtype=BF)
    ecol = np.zeros((7, 2), f)
    for e in range(7):
        ecol[e, 0] = e * 8
    tokid = np.zeros((128, 32), f)
    for a_ in range(16):
        tokid[:, a_] = a_ * 128 + np.arange(128)
        tokid[:, 16 + a_] = 2048 + a_ * 128 + np.arange(128)

    def gu(w):
        E = w.shape[0]
        return np.ascontiguousarray(
            np.asarray(w, f).reshape(E, 8, 128, 16, 2, 128).transpose(0, 3, 2, 4, 1, 5)
            .reshape(E, 16, 128, 2048))

    def dn(w):
        E = w.shape[0]
        return np.ascontiguousarray(
            np.asarray(w, f).reshape(E, 32, 128, 8, 128).transpose(0, 3, 2, 1, 4)
            .reshape(E, 8, 128, 4096))

    wg_t = gu(L0_ffn_w_gate); wu_t = gu(L0_ffn_w_up); wd_t = dn(L0_ffn_w_down)
    swg_t = gu(L0_ffn_sw_gate)[0]; swu_t = gu(L0_ffn_sw_up)[0]; swd_t = dn(L0_ffn_sw_down)[0]

    common = dict(wq=wq_t, gam=gam_t, rw=rw_t, identf=identf, identb=identb,
                  ecolf=ecol, tokidf=tokid,
                  wg=wg_t, wu=wu_t, wd=wd_t, swg=swg_t, swu=swu_t, swd=swd_t)
    in_maps = []
    for c in range(8):
        b, h = c // 2, c % 2
        own = x[b, h * OWN:(h + 1) * OWN]
        prefix = x[b, 0:OWN] if h == 1 else np.zeros((OWN, D), f)
        xs_ = np.ascontiguousarray(np.concatenate([prefix, own], axis=0))
        in_maps.append(dict(common, xs=xs_))
    return in_maps


def kernel(L0_attn_x, L0_attn_w_qkva, L0_attn_gamma, L0_ffn_router_w,
           L0_ffn_w_gate, L0_ffn_w_up, L0_ffn_w_down,
           L0_ffn_sw_gate, L0_ffn_sw_up, L0_ffn_sw_down):
    in_maps = _host_inputs(L0_attn_x, L0_attn_w_qkva, L0_attn_gamma, L0_ffn_router_w,
                           L0_ffn_w_gate, L0_ffn_w_up, L0_ffn_w_down,
                           L0_ffn_sw_gate, L0_ffn_sw_up, L0_ffn_sw_down)
    nc = build_nc(stage=8)
    res = run_bass_kernel_spmd(nc, in_maps, core_ids=list(range(8)))
    out = np.empty((4, SEQ, D), np.float32)
    for c in range(8):
        b, h = c // 2, c % 2
        out[b, h * OWN:(h + 1) * OWN] = np.asarray(res.results[c]["yo"])
    return out


# version 13
# speedup vs baseline: 1.1206x; 1.1206x over previous
import numpy as np
import ml_dtypes
from contextlib import ExitStack

import concourse.bass as bass
import concourse.mybir as mybir
from concourse.bass_core_rust import IndirectOffsetOnAxis as IOA
from concourse.bass_utils import run_bass_kernel_spmd

F32 = mybir.dt.float32
BF16 = mybir.dt.bfloat16
I32 = mybir.dt.int32
AF = mybir.ActivationFunctionType
ALU = mybir.AluOpType
AX = mybir.AxisListType.X

D = 1024
KD = 8
SEQ = 4096
OWN = 2048
NT1 = SEQ // 128
NT3 = OWN // 128
NE = 7
H = 4096
MH = 32
CAP = 512
NT4 = CAP // 128
SENT = 1.0e7
BF = ml_dtypes.bfloat16

ENGS = ("tensor", "vector", "scalar", "sync", "gpsimd")


class Prog:
    def __init__(self):
        self.q = {e: [] for e in ENGS}
        self.sems = {}

    def add_sem(self, name, handle):
        self.sems[name] = [handle, 0]

    def cur(self, name):
        return self.sems[name][1]

    def w(self, eng, name, val):
        if val > self.sems[name][1]:
            raise RuntimeError(f"wait on {name} for {val} > emitted {self.sems[name][1]}")
        if val > 0:
            self.q[eng].append(("w", name, val))

    def wt(self, eng, st_):
        if st_ is not None:
            self.w(eng, st_[0], st_[1])

    def o(self, eng, fn, incs=()):
        tick = None
        rec = []
        for nm, n in incs:
            self.sems[nm][1] += n
            rec.append((nm, n))
            if tick is None:
                tick = self.sems[nm][1]
        self.q[eng].append(("o", fn, rec))
        return tick

    def emit(self, blk):
        def runner(eng_name):
            def body(eng):
                for item in self.q[eng_name]:
                    if item[0] == "w":
                        eng.wait_ge(self.sems[item[1]][0], item[2])
                    else:
                        ins = item[1](eng)
                        for nm, n in item[2]:
                            ins = ins.then_inc(self.sems[nm][0], n)
            return body
        blk.tensor(runner("tensor"))
        blk.vector(runner("vector"))
        blk.scalar(runner("scalar"))
        blk.sync(runner("sync"))
        blk.gpsimd(runner("gpsimd"))


def build_nc(stage=8, dbg=False):
    nc = bass.Bass("TRN2", target_bir_lowering=False, debug=False)

    xs = nc.dram_tensor("xs", [SEQ, D], F32, kind="ExternalInput").ap()
    wq = nc.dram_tensor("wq", [24, 128, 1024], F32, kind="ExternalInput").ap()
    gam = nc.dram_tensor("gam", [128, 8], F32, kind="ExternalInput").ap()
    rw = nc.dram_tensor("rw", [128, 56], F32, kind="ExternalInput").ap()
    identf = nc.dram_tensor("identf", [128, 128], F32, kind="ExternalInput").ap()
    identb = nc.dram_tensor("identb", [128, 128], BF16, kind="ExternalInput").ap()
    ecolf = nc.dram_tensor("ecolf", [7, 2], F32, kind="ExternalInput").ap()
    tokidf = nc.dram_tensor("tokidf", [128, 32], F32, kind="ExternalInput").ap()
    wg = nc.dram_tensor("wg", [NE, 16, 128, 2048], F32, kind="ExternalInput").ap()
    wu = nc.dram_tensor("wu", [NE, 16, 128, 2048], F32, kind="ExternalInput").ap()
    wd = nc.dram_tensor("wd", [NE, 8, 128, 4096], F32, kind="ExternalInput").ap()
    swg = nc.dram_tensor("swg", [16, 128, 2048], F32, kind="ExternalInput").ap()
    swu = nc.dram_tensor("swu", [16, 128, 2048], F32, kind="ExternalInput").ap()
    swd = nc.dram_tensor("swd", [8, 128, 4096], F32, kind="ExternalInput").ap()
    yo = nc.dram_tensor("yo", [OWN, D], F32, kind="ExternalOutput").ap()
    dbgo = nc.dram_tensor("dbgo", [128, 96], F32, kind="ExternalOutput").ap() if dbg else None
    dbg2o = nc.dram_tensor("dbg2o", [256, D], BF16, kind="ExternalOutput").ap() if dbg else None
    dbg3o = nc.dram_tensor("dbg3o", [512, D], F32, kind="ExternalOutput").ap() if dbg else None
    dbg4o = nc.dram_tensor("dbg4o", [128, KD * OWN], F32, kind="ExternalOutput").ap() if dbg else None
    ln2tok = nc.dram_tensor("ln2tok", [OWN + 1, D], BF16, kind="Internal").ap()
    listf = nc.dram_tensor("listf", [128 * 56 + 1, 1], F32, kind="Internal").ap()
    accC = nc.dram_tensor("accC", [2 * OWN + 1, D], BF16, kind="Internal").ap()

    p = Prog()
    es = ExitStack()
    with es:
        sb = lambda n, s, d: es.enter_context(nc.sbuf_tensor(n, s, d))
        psu = lambda n, s, d: es.enter_context(nc.psum_tensor(n, s, d))

        gam_s = sb("gam_s", [128, 8], F32)
        rw_s = sb("rw_s", [128, 56], BF16)
        identf_s = sb("identf_s", [128, 128], F32)
        identb_s = sb("identb_s", [128, 128], BF16)
        epscol = sb("epscol", [128, 1], F32)
        ones1 = sb("ones1", [1, 128], F32)
        ecol_s = sb("ecol_s", [7, 2], F32)
        tokid_s = sb("tokid_s", [128, 32], F32)
        sent_s = sb("sent_s", [128, 56], F32)
        st = sb("st", [128, 64], F32)
        p4b = sb("p4b", [128, 64], F32)
        sgall = sb("sgall", [128, 16], F32)
        tauall = sb("tauall", [128, 16], F32)
        gidx_f = sb("gidx_f", [128, 56], F32)
        gidx_i = sb("gidx_i", [128, 56], I32)
        gsc_f = sb("gsc_f", [128, 56], F32)
        gsc_i = sb("gsc_i", [128, 56], I32)
        fdt_f = sb("fdt_f", [128, 32], F32)
        fdt_i = sb("fdt_i", [128, 32], I32)
        x_dT = sb("x_dT", [128, KD * OWN], F32)
        AR = 34816
        arena = sb("arena", [128, AR], F32)

        def acar(off, cols):
            assert off + cols <= AR
            return arena[:, off:off + cols]

        def acarb(off, cols_bf16):
            assert off * 2 + cols_bf16 <= 2 * AR
            return arena[:, off:off + (cols_bf16 + 1) // 2].bitcast(BF16)

        def acari(off, cols):
            assert off + cols <= AR
            return arena[:, off:off + cols].bitcast(I32)

        h_off = 0

        def h_dT(kd, c0, c1):
            v = arena[:, h_off + kd * 2048:h_off + (kd + 1) * 2048].bitcast(BF16)
            return v[:, c0:c1]

        def h_dT3(t):
            v = arena[:, 0:16384].bitcast(BF16).rearrange("p (kd c) -> p kd c", kd=8)
            return v[:, :, t * 128:(t + 1) * 128]
        SLOT = [16384 + k * 3584 for k in range(4)]
        y_off = 16384
        kv_off = 20480
        a_off = 24576
        t1_off = 28672
        wqr_off = [31744, 32256]
        wqg_off = [32768, 33280]
        zz_off = 33792


        ln2_off = 0

        def ln2_dT(kd, c0, c1):
            v = arena[:, ln2_off + kd * 1024:ln2_off + (kd + 1) * 1024].bitcast(BF16)
            return v[:, c0:c1]
        fo = 8192
        wg_off = [fo, fo + 1024]; fo += 2048
        wu_off = [fo, fo + 1024]; fo += 2048
        wd_off = [fo, fo + 512, fo + 1024, fo + 1536]; fo += 2048
        h1hi_off = fo; fo += 8192

        def h1(m, c0, c1, kind):
            if kind == "rt":
                off = (m * 512) if m < 16 else (h1hi_off + (m - 16) * 512)
            else:
                off = h1hi_off + m * 512
            v = arena[:, off:off + 512].bitcast(BF16)
            return v[:, c0:c1]
        gln_off = fo; fo += 4096

        def gln_dT(b, kd, c0, c1):
            v = arena[:, gln_off + kd * 512:gln_off + (kd + 1) * 512].bitcast(BF16)
            return v[:, c0:c1]
        gtok_off = [31744, 32256]
        fo += 1024
        sil_off = fo; fo += 1024
        dout_off = fo; fo += 4096
        ttok_off = [fo, fo + 512]; fo += 1024
        assert fo <= AR
        fa1_off = [22528, 24576, 14336, 16384]
        fa2_off = [23040, 25088, 14848, 16896]
        fxt_off = [23552, 25600, 15360, 17408]

        mskT_off = [16384, 18432]
        PB = [20480 + k * 2048 for k in range(7)]

        pA = psu("pA", [128, 1024], F32)
        pB = psu("pB", [128, 1024], F32)
        pC = psu("pC", [128, 1024], F32)
        pD = psu("pD", [128, 1024], F32)
        PS = [pA, pB, pC, pD]

        def bank(i):
            return PS[i // 2][:, (i % 2) * 512:(i % 2) * 512 + 512]

        SEMS = ("spe", "sve", "sac", "sou",
                "sdc", "srw", "spf", "szr", "sx0", "sx1", "sq0", "sq1", "sl2",
                "swg0", "swg1", "swu0", "swu1", "swd0", "swd1",
                "sg0", "sg1", "ssc", "srb", "so0", "so1", "sal", "sov")
        for nm in SEMS:
            p.add_sem(nm, es.enter_context(nc.semaphore(nm)))

        blk = es.enter_context(nc.Block())

        p.o("sync", lambda e: e.dma_start(out=gam_s[:, :], in_=gam), (("sdc", 16),))
        p.o("sync", lambda e: e.dma_start(out=identf_s[:, :], in_=identf), (("sdc", 16),))
        p.o("sync", lambda e: e.dma_start(out=identb_s[:, :], in_=identb), (("sdc", 16),))
        p.o("sync", lambda e: e.dma_start(out=ecol_s[:, :], in_=ecolf), (("sdc", 16),))
        p.o("sync", lambda e: e.dma_start(out=tokid_s[:, :], in_=tokidf), (("sdc", 16),))
        p.o("gpsimd", lambda e: e.dma_start(out=rw_s[:, :], in_=rw), (("srw", 16),))
        t_consts = p.cur("sdc")
        t_rw = p.cur("srw")
        t_eps = p.o("vector", lambda e: e.memset(epscol[:, :], 1e-5), (("sve", 1),))
        p.o("vector", lambda e: e.memset(ones1[:, :], 1.0), (("sve", 1),))
        p.o("vector", lambda e: e.memset(sent_s[:, :], SENT), (("sve", 1),))
        zz = acar(zz_off, 1024)
        t_zz = p.o("vector", lambda e: e.memset(zz, 0.0), (("sve", 1),))
        t_setup_v = p.cur("sve")
        t_gt0 = p.o("vector", lambda e: e.memset(acarb(gtok_off[0], 1024), 0.0), (("sve", 1),))
        t_gt1 = p.o("vector", lambda e: e.memset(acarb(gtok_off[1], 1024), 0.0), (("sve", 1),))
        p.w("gpsimd", "sve", t_setup_v)
        t_pref = p.o("gpsimd", lambda e: e.dma_start(
            out=listf[0:128 * 56].rearrange("(q x) one -> q (x one)", q=128), in_=sent_s[:, :]), (("spf", 16),))
        for t in range(32):
            p.o("gpsimd", lambda e, t=t: e.dma_start(
                out=accC[t * 128:(t + 1) * 128, :], in_=zz), (("szr", 16),))
        p.o("gpsimd", lambda e: e.dma_start(
            out=ln2tok[OWN:OWN + 1, :], in_=zz[0:1, :]), (("szr", 16),))
        t_prezero = p.cur("szr")

        tslot_free = [None] * 8
        tslot_i = [0]

        def tslot(n_avail=8, base=0):
            i = base + (tslot_i[0] % n_avail)
            tslot_i[0] += 1
            return i

        def pe_tr_group(srcs, dst_ap, bf16, extra_waits=(), n_avail=8, base=0,
                        copy_eng="vector"):
            assert len(srcs) <= (8 if bf16 else 4)
            i = tslot(n_avail, base)
            bk = bank(i)
            bview = bk.bitcast(BF16) if bf16 else bk
            p.w("tensor", "sdc", t_consts)
            p.wt("tensor", tslot_free[i])
            for w_ in extra_waits:
                p.wt("tensor", w_)
            tpe = None
            for j, src_ap in enumerate(srcs):
                pv = bview[:, j * 128:(j + 1) * 128]
                tpe = p.o("tensor", lambda e, pv=pv, src_ap=src_ap: e.transpose(
                    pv, src_ap, identb_s[:, :] if bf16 else identf_s[:, :]), (("spe", 1),))
            sem = "sve" if copy_eng == "vector" else "sac"
            p.w(copy_eng, "spe", tpe)
            pall = bview[:, 0:len(srcs) * 128]
            if copy_eng == "vector":
                tcp = p.o(copy_eng, lambda e, dst_ap=dst_ap, pall=pall: e.tensor_copy(
                    dst_ap, pall), ((sem, 1),))
            else:
                tcp = p.o(copy_eng, lambda e, dst_ap=dst_ap, pall=pall: e.copy(
                    dst_ap, pall), ((sem, 1),))
            tslot_free[i] = (sem, tcp)
            return tpe, (sem, tcp)

        def pe_transpose(src_ap, dst_ap, bf16, extra_waits=(), n_avail=8, base=0,
                         copy_eng="vector", rows=128, cols=128):
            i = tslot(n_avail, base)
            if bf16:
                pview = bank(i)[:, 0:64].bitcast(BF16)
            else:
                pview = bank(i)[:, 0:128]
            pout = pview[0:rows, :]
            pcp = pview[0:rows, 0:cols]
            p.w("tensor", "sdc", t_consts)
            p.wt("tensor", tslot_free[i])
            for w_ in extra_waits:
                p.wt("tensor", w_)
            tpe = p.o("tensor", lambda e, pout=pout, src_ap=src_ap: e.transpose(
                pout, src_ap, identb_s[:, :] if bf16 else identf_s[:, :]), (("spe", 1),))
            sem = "sve" if copy_eng == "vector" else "sac"
            p.w(copy_eng, "spe", tpe)
            if copy_eng == "vector":
                tcp = p.o(copy_eng, lambda e, dst_ap=dst_ap, pcp=pcp: e.tensor_copy(
                    dst_ap, pcp), ((sem, 1),))
            else:
                tcp = p.o(copy_eng, lambda e, dst_ap=dst_ap, pcp=pcp: e.copy(
                    dst_ap, pcp), ((sem, 1),))
            tslot_free[i] = (sem, tcp)
            return tpe, (sem, tcp)

        xt_free = [None] * 4
        xt_load = [0] * 4
        last_htr = [None]
        G = 4

        def slotap(k):
            xt = acar(SLOT[k], 1024)
            xsq = acar(SLOT[k] + 1024, 1024)
            z = acar(SLOT[k] + 2048, 1024)
            htok = acarb(SLOT[k] + 3072, 1024)
            return xt, xsq, z, htok

        def norm_steps(t, k, rms, src_dma):
            xt, xsq, z, htok = slotap(k)
            c = k * 16
            s1 = st[:, c + 0:c + 1]; mcol = st[:, c + 1:c + 2]; ssc = st[:, c + 2:c + 3]
            msq = st[:, c + 3:c + 4]; var = st[:, c + 4:c + 5]; svc = st[:, c + 5:c + 6]
            rstd = st[:, c + 6:c + 7]; zss = st[:, c + 7:c + 8]; nrm = st[:, c + 8:c + 9]
            nmx = st[:, c + 9:c + 10]; rec = st[:, c + 10:c + 11]
            tk = {}

            def wt_ld(eng):
                for x_ in tk["ld"]:
                    p.wt(eng, x_)

            def st0():
                tk["ld"] = src_dma()

            def st1():
                wt_ld("scalar")
                tk["s1"] = p.o("scalar", lambda e: e.activation(
                    xsq, xt, AF.Identity, accum_out=s1), (("sac", 1),))
                tk["ss"] = p.o("scalar", lambda e: e.activation(
                    xsq, xt, AF.Square, accum_out=ssc), (("sac", 1),))

            def st2():
                p.w("vector", "sac", tk["s1"])
                tk["m"] = p.o("vector", lambda e: e.tensor_scalar(
                    mcol, s1, 1.0 / D, None, ALU.mult), (("sve", 1),))

            def st3():
                p.w("vector", "sve", tk["m"])
                tk["msq"] = p.o("vector", lambda e: e.tensor_mul(msq, mcol, mcol), (("sve", 1),))

            def st4():
                p.w("vector", "sve", tk["msq"])
                p.w("vector", "sac", tk["ss"])
                tk["var"] = p.o("vector", lambda e: e.scalar_tensor_tensor(
                    var, ssc, 1.0 / D, msq, ALU.mult, ALU.subtract), (("sve", 1),))

            def st5():
                p.w("scalar", "sve", tk["var"])
                p.w("scalar", "sve", t_eps)
                tk["sv"] = p.o("scalar", lambda e: e.activation(
                    svc, var, AF.Sqrt, bias=epscol[:, 0:1]), (("sac", 1),))

            def st6():
                p.w("vector", "sac", tk["sv"])
                tk["rstd"] = p.o("vector", lambda e: e.reciprocal(rstd, svc), (("sve", 1),))

            def st7():
                p.w("vector", "sve", tk["rstd"])
                if rms:
                    tk["z"] = p.o("vector", lambda e: e.tensor_scalar(
                        z, xt, mcol, rstd, ALU.subtract, ALU.mult), (("sve", 1),))
                else:
                    if last_htr[0] is not None:
                        p.w("vector", "spe", last_htr[0])
                    if tk.get("l2gate") is not None:
                        p.wt("vector", tk["l2gate"])
                    tk["z"] = p.o("vector", lambda e: e.tensor_scalar(
                        htok, xt, mcol, rstd, ALU.subtract, ALU.mult), (("sve", 1),))

            def st8():
                p.w("scalar", "sve", tk["z"])
                tk["zss"] = p.o("scalar", lambda e: e.activation(
                    xsq, z, AF.Square, accum_out=zss), (("sac", 1),))

            def st9():
                p.w("scalar", "sac", tk["zss"])
                tk["nrm"] = p.o("scalar", lambda e: e.activation(nrm, zss, AF.Sqrt), (("sac", 1),))

            def st10():
                p.w("vector", "sac", tk["nrm"])
                tv = p.o("vector", lambda e: e.tensor_scalar_max(nmx, nrm, 1e-12), (("sve", 1),))
                p.w("vector", "sve", tv)
                tk["rec"] = p.o("vector", lambda e: e.reciprocal(rec, nmx), (("sve", 1),))

            def st11():
                p.w("vector", "sve", tk["rec"])
                if last_htr[0] is not None:
                    p.w("vector", "spe", last_htr[0])
                tk["h"] = p.o("vector", lambda e: e.tensor_scalar(
                    htok, z, rec, float(np.sqrt(D)), ALU.mult, ALU.mult), (("sve", 1),))

            if rms:
                return [st0, st1, st2, st3, st4, st5, st6, st7, st8, st9, st10, st11], tk
            return [st0, st1, st2, st3, st4, st5, st6, st7], tk

        for g in range(NT1 // G):
            steps = []
            tks = []
            for k in range(G):
                t = g * G + k
                xt, xsq, z, htok = slotap(k)

                def mk_dma(t=t, k=k, xt=xt):
                    def src_dma():
                        xsem = "sx%d" % (k % 2)
                        p.wt("sync", xt_free[k])
                        tick = p.o("sync", lambda e: e.dma_start(
                            out=xt, in_=xs[t * 128:(t + 1) * 128, :]), ((xsem, 16),))
                        return [(xsem, tick)]
                    return src_dma
                stp, tk = norm_steps(t, k, True, mk_dma())
                steps.append(stp)
                tks.append(tk)
            nst = max(len(x) for x in steps)
            for si in range(nst):
                for k in range(G):
                    if si < len(steps[k]):
                        steps[k][si]()
            xv3 = x_dT.ap().rearrange("p (kd c) -> p kd c", kd=8)
            for k in range(G):
                t = g * G + k
                xt, xsq, z, htok = slotap(k)
                tpe, tcp = pe_tr_group(
                    [htok[:, kd * 128:(kd + 1) * 128] for kd in range(KD)],
                    h_dT3(t), True, extra_waits=(("sve", tks[k]["h"]),),
                    copy_eng="scalar" if k % 2 else "vector")
                last_htr[0] = tpe
                if t >= 16:
                    for j0 in range(2):
                        ew = list(tks[k]["ld"]) if j0 == 0 else []
                        tpe, tcp = pe_tr_group(
                            [xt[:, (j0 * 4 + j) * 128:(j0 * 4 + j + 1) * 128] for j in range(4)],
                            xv3[:, j0 * 4:(j0 + 1) * 4, (t - 16) * 128:(t - 16) * 128 + 128],
                            False, extra_waits=ew,
                            copy_eng="scalar" if j0 else "vector")
                    xt_free[k] = ("spe", tpe)
                else:
                    xt_free[k] = ("sve", tks[k]["z"])

        t_h_all = p.cur("sve")
        t_h_sac = p.cur("sac")

        if stage >= 2:
            kv_s = acar(kv_off, 4096); a_s = acar(a_off, 4096); y_s = acar(y_off, 4096)
            wqr_free = [None, None]
            wqg_rdy = [0, 0]
            wqg_pe_done = [None, None]
            ci_load = [0, 0]

            def wq_chunk_sched(ci):
                bi = ci % 2
                wqr = acarb(wqr_off[bi], 1024)
                wqgb = acarb(wqg_off[bi], 1024)
                p.wt("gpsimd", wqr_free[bi])
                qsem = "sq%d" % bi
                ci_load[bi] = p.o("gpsimd", lambda e, wqr=wqr, ci=ci: e.dma_start(
                    out=wqr, in_=wq[ci]), ((qsem, 16),))
                p.w("vector", qsem, ci_load[bi])
                p.w("vector", "sdc", t_consts)
                p.wt("vector", wqg_pe_done[bi])
                tl = None
                for kd in range(KD):
                    tl = p.o("vector", lambda e, wqgb=wqgb, wqr=wqr, kd=kd: e.tensor_scalar(
                        wqgb[:, kd * 128:(kd + 1) * 128], wqr[:, kd * 128:(kd + 1) * 128],
                        gam_s[:, kd:kd + 1], None, ALU.mult), (("sve", 1),))
                wqr_free[bi] = ("sve", tl)
                wqg_rdy[bi] = tl

            t_p1pe = p.cur("spe")
            wq_chunk_sched(0)
            wq_chunk_sched(1)

            mm_slot_free = [None] * 8
            mm_slot_i = [0]

            def mm_slot():
                i = mm_slot_i[0] % 8
                mm_slot_i[0] += 1
                return i

            first_kv = [True]
            for j in range(KD):
                for part, dst in ((0, kv_s), (1, a_s)):
                    ci = 3 * j + part
                    bi = ci % 2
                    wqgb = acarb(wqg_off[bi], 1024)
                    p.w("tensor", "sve", wqg_rdy[bi])
                    if first_kv[0]:
                        p.w("tensor", "sve", t_h_all)
                        p.w("tensor", "sac", t_h_sac)
                        first_kv[0] = False
                    tpe = None
                    for n in range(8):
                        i = mm_slot()
                        psl = bank(i)
                        p.wt("tensor", mm_slot_free[i])
                        for kd in range(KD):
                            tpe = p.o("tensor", lambda e, psl=psl, wqgb=wqgb, kd=kd, n=n: e.matmul(
                                psl, wqgb[:, kd * 128:(kd + 1) * 128],
                                h_dT(kd, n * 512, (n + 1) * 512),
                                start=(kd == 0), stop=(kd == KD - 1)),
                                (("spe", 1),) if kd == KD - 1 else ())
                        if part == 0:
                            p.w("vector", "spe", tpe)
                            tev = p.o("vector", lambda e, dst=dst, psl=psl, n=n: e.tensor_copy(
                                dst[:, n * 512:(n + 1) * 512], psl), (("sve", 1),))
                            mm_slot_free[i] = ("sve", tev)
                        else:
                            p.w("scalar", "spe", tpe)
                            tev = p.o("scalar", lambda e, dst=dst, psl=psl, n=n: e.activation(
                                dst[:, n * 512:(n + 1) * 512], psl, AF.Sigmoid), (("sac", 1),))
                            mm_slot_free[i] = ("sac", tev)
                    wqg_pe_done[bi] = ("spe", tpe)
                    if ci + 2 < 24:
                        wq_chunk_sched(ci + 2)
                p.w("vector", "sac", p.cur("sac"))
                if j == 0:
                    p.w("vector", "spe", t_p1pe)
                p.o("vector", lambda e: e.tensor_tensor_scan(
                    y_s[:, :], a_s[:, :], kv_s[:, :], 0.0, ALU.mult, ALU.add), (("sve", 1),))
                ci = 3 * j + 2
                bi = ci % 2
                wqgb = acarb(wqg_off[bi], 1024)
                p.w("tensor", "sve", wqg_rdy[bi])
                tpe = None
                for n in range(4):
                    i = mm_slot()
                    psl = bank(i)
                    p.wt("tensor", mm_slot_free[i])
                    for kd in range(KD):
                        tpe = p.o("tensor", lambda e, psl=psl, wqgb=wqgb, kd=kd, n=n: e.matmul(
                            psl, wqgb[:, kd * 128:(kd + 1) * 128],
                            h_dT(kd, 2048 + n * 512, 2048 + (n + 1) * 512),
                            start=(kd == 0), stop=(kd == KD - 1)),
                            (("spe", 1),) if kd == KD - 1 else ())
                    p.w("vector", "spe", tpe)
                    t1 = acar(t1_off, 512)
                    p.o("vector", lambda e, t1=t1, psl=psl, n=n: e.scalar_tensor_tensor(
                        t1, psl, 1.0, y_s[:, 2048 + n * 512:2048 + (n + 1) * 512],
                        ALU.mult, ALU.mult), (("sve", 1),))
                    xsl = x_dT[:, j * OWN + n * 512:j * OWN + (n + 1) * 512]
                    tv = p.o("vector", lambda e, xsl=xsl, t1=t1: e.tensor_add(xsl, xsl, t1), (("sve", 1),))
                    mm_slot_free[i] = ("sve", tv)
                wqg_pe_done[bi] = ("spe", tpe)
                if ci + 2 < 24:
                    wq_chunk_sched(ci + 2)

        t_p2pe = p.cur("spe")

        t_ln2dma = 0
        if stage >= 3:
            ln2dma_slot = [None] * 4
            for g in range(NT3 // G):
                steps = []
                tks = []
                for k in range(G):
                    t = g * G + k
                    xt, xsq, z, htok = slotap(k)

                    def mk_src(t=t, k=k, xt=xt, g=g):
                        def src_dma():
                            out = []
                            for j0 in range(2):
                                ew = [xt_free[k]] if j0 == 0 else []
                                if g == 0 and k == 0 and j0 == 0:
                                    ew.append(("sve", p.cur("sve")))
                                tpe, tcp = pe_tr_group(
                                    [x_dT[:, (j0 * 4 + j) * OWN + t * 128:(j0 * 4 + j) * OWN + t * 128 + 128]
                                     for j in range(4)],
                                    xt[:, j0 * 512:(j0 + 1) * 512], False,
                                    extra_waits=ew,
                                    copy_eng="scalar" if j0 else "vector")
                                out.append(tcp)
                            return out
                        return src_dma

                    stp, tk = norm_steps(t, k, False, mk_src())
                    tk["l2gate"] = ln2dma_slot[k]
                    steps.append(stp)
                    tks.append(tk)
                nst = max(len(x) for x in steps)
                for si in range(nst):
                    for k in range(G):
                        if si < len(steps[k]):
                            steps[k][si]()
                lv3 = arena[:, 0:8192].bitcast(BF16).rearrange("p (kd c) -> p kd c", kd=8)
                for k in range(G):
                    t = g * G + k
                    xt, xsq, z, htok = slotap(k)
                    p.w("sync", "sve", tks[k]["z"])
                    t_ln2dma = p.o("sync", lambda e, htok=htok, t=t: e.dma_start(
                        out=ln2tok[t * 128:(t + 1) * 128, :], in_=htok), (("sl2", 16),))
                    ln2dma_slot[k] = ("sl2", t_ln2dma)
                    tpe, _ = pe_tr_group(
                        [htok[:, kd * 128:(kd + 1) * 128] for kd in range(KD)],
                        lv3[:, :, t * 128:(t + 1) * 128], True,
                        extra_waits=(("sve", tks[k]["z"]),),
                        copy_eng="scalar" if k % 2 else "vector")
                    last_htr[0] = tpe
                    xt_free[k] = ("sve", tks[k]["z"])

        t_ln2 = p.cur("sve")
        t_ln2sac = p.cur("sac")
        t_ln2tok_dma = p.cur("sl2")

        if stage >= 4:
            mskT = [acar(mskT_off[0], 2048), acar(mskT_off[1], 2048)]
            lg_ev = None
            mt_ev = None
            for t in range(NT3):
                bi = t % 2
                c = bi * 16
                ltile = p4b[:, 0:7]
                msk1 = p4b[:, 8:15]
                l2 = p4b[:, 16:23]
                msk2 = p4b[:, 24:31]
                m1 = st[:, c + 0:c + 1]; m2 = st[:, c + 1:c + 2]; d12 = st[:, c + 2:c + 3]
                d21 = st[:, c + 5:c + 6]
                sg = sgall[:, t:t + 1]; tau = tauall[:, t:t + 1]
                pl = pD[:, 0:7]
                p.w("tensor", "srw", t_rw)
                if t == 0:
                    p.w("tensor", "sve", t_ln2)
                if lg_ev is not None:
                    p.w("tensor", "sve", lg_ev)
                tpe = None
                for kd in range(KD):
                    tpe = p.o("tensor", lambda e, pl=pl, kd=kd, t=t: e.matmul(
                        pl, ln2_dT(kd, t * 128, (t + 1) * 128), rw_s[:, kd * 7:(kd + 1) * 7],
                        start=(kd == 0), stop=(kd == KD - 1)),
                        (("spe", 1),) if kd == KD - 1 else ())
                p.w("vector", "spe", tpe)
                if mt_ev is not None:
                    p.w("vector", "spe", mt_ev)
                lg_ev = p.o("vector", lambda e, ltile=ltile, pl=pl: e.tensor_copy(ltile, pl), (("sve", 1),))
                p.w("vector", "sve", lg_ev)
                tv = p.o("vector", lambda e, m1=m1, ltile=ltile: e.reduce_max(m1, ltile, axis=AX), (("sve", 1),))
                p.w("vector", "sve", tv)
                tv = p.o("vector", lambda e, msk1=msk1, ltile=ltile, m1=m1: e.tensor_scalar(
                    msk1, ltile, m1, None, ALU.is_ge), (("sve", 1),))
                p.w("vector", "sve", tv)
                tv = p.o("vector", lambda e, l2=l2, msk1=msk1, ltile=ltile: e.scalar_tensor_tensor(
                    l2, msk1, -1e30, ltile, ALU.mult, ALU.add), (("sve", 1),))
                p.w("vector", "sve", tv)
                tv = p.o("vector", lambda e, m2=m2, l2=l2: e.reduce_max(m2, l2, axis=AX), (("sve", 1),))
                p.w("vector", "sve", tv)
                tv = p.o("vector", lambda e, d12=d12, m1=m1, m2=m2: e.tensor_scalar(
                    d12, m1, m2, None, ALU.subtract), (("sve", 1),))
                p.w("vector", "sve", tv)
                tvb = p.o("vector", lambda e, d21=d21, m2=m2, m1=m1: e.tensor_scalar(
                    d21, m2, m1, None, ALU.subtract), (("sve", 1),))
                tv2 = p.o("vector", lambda e, msk2=msk2, l2=l2, m2=m2: e.tensor_scalar(
                    msk2, l2, m2, None, ALU.is_ge), (("sve", 1),))
                p.w("scalar", "sve", tvb)
                p.o("scalar", lambda e, sg=sg, d12=d12: e.activation(sg, d12, AF.Sigmoid), (("sac", 1),))
                p.o("scalar", lambda e, tau=tau, d21=d21: e.activation(tau, d21, AF.Sigmoid), (("sac", 1),))
                for r in range(2):
                    src = msk1 if r == 0 else msk2
                    mt_pe, _ = pe_transpose(
                        src, mskT[r][0:7, t * 128:(t + 1) * 128], False,
                        extra_waits=(("sve", tv2),), n_avail=6, base=0, rows=7)
                mt_ev = mt_pe
        t_p4 = p.cur("sve")

        if stage >= 5:
            ones7 = acar(PB[0], 2048)
            csB = acar(PB[1], 2048)
            tB2 = acar(PB[2], 2048)
            tB3 = acar(PB[3], 2048)
            selB = acar(PB[4], 2048)
            fdB = acar(PB[5], 2048)
            iB = acari(PB[6], 2048)
            iN = acari(PB[2], 2048)
            p.w("vector", "szr", t_prezero)
            t_o7 = p.o("vector", lambda e: e.memset(ones7[0:7, :], 1.0), (("sve", 1),))
            t_scat_ids = []
            msum = tB2
            p.w("vector", "sve", t_p4)
            tv = p.o("vector", lambda e: e.tensor_add(msum[0:7, :], mskT[0][0:7, :], mskT[1][0:7, :]), (("sve", 1),))
            p.w("vector", "sve", tv)
            tv = p.o("vector", lambda e: e.tensor_tensor_scan(
                csB[0:7, :], ones7[0:7, :], msum[0:7, :], 0.0, ALU.mult, ALU.add), (("sve", 1),))
            p.w("vector", "sve", tv)
            tv = p.o("vector", lambda e: e.tensor_scalar(
                tB3[0:7, :], csB[0:7, :], 1.0, None, ALU.subtract), (("sve", 1),))
            p.w("vector", "sve", tv)
            tv = p.o("vector", lambda e: e.tensor_copy(iB[0:7, :], tB3[0:7, :]), (("sve", 1),))
            p.w("vector", "sve", tv)
            tv = p.o("vector", lambda e: e.tensor_single_scalar(
                iN[0:7, :], iB[0:7, :], 127, ALU.bitwise_and), (("sve", 1),))
            p.w("vector", "sve", tv)
            tv = p.o("vector", lambda e: e.tensor_single_scalar(
                iB[0:7, :], iB[0:7, :], 7, ALU.arith_shift_right), (("sve", 1),))
            p.w("vector", "sve", tv)
            tv = p.o("vector", lambda e: e.tensor_copy(tB3[0:7, :], iN[0:7, :]), (("sve", 1),))
            p.w("vector", "sve", tv)
            tv = p.o("vector", lambda e: e.tensor_copy(tB2[0:7, :], iB[0:7, :]), (("sve", 1),))
            p.w("vector", "sve", tv)
            tv = p.o("vector", lambda e: e.scalar_tensor_tensor(
                tB3[0:7, :], tB3[0:7, :], 56.0, tB2[0:7, :], ALU.mult, ALU.add), (("sve", 1),))
            p.w("vector", "sve", tv)
            tv = p.o("vector", lambda e: e.tensor_scalar(
                tB3[0:7, :], tB3[0:7, :], ecol_s[0:7, 0:1], None, ALU.add), (("sve", 1),))
            p.w("vector", "sve", tv)
            tv = p.o("vector", lambda e: e.tensor_single_scalar(
                tB2[0:7, :], csB[0:7, :], 2 * CAP + 0.5, ALU.is_ge), (("sve", 1),))
            p.w("vector", "sve", tv)
            tv = p.o("vector", lambda e: e.scalar_tensor_tensor(
                tB3[0:7, :], tB2[0:7, :], SENT, tB3[0:7, :], ALU.mult, ALU.add), (("sve", 1),))
            p.w("vector", "sve", tv)
            for r in range(2):
                mt = mskT[r]
                t_sel = p.o("vector", lambda e, mt=mt: e.tensor_tensor(
                    selB[0:7, :], tB3[0:7, :], mt[0:7, :], ALU.mult), (("sve", 1),))
                t_fd = None
                for nch in range(4):
                    bk = bank(6 + nch % 2)
                    p.w("tensor", "sve", t_sel)
                    p.wt("tensor", tslot_free[6 + nch % 2])
                    tmm = p.o("tensor", lambda e, bk=bk, nch=nch: e.matmul(
                        bk[0:1, :], ones7[0:7, 0:1], selB[0:7, nch * 512:(nch + 1) * 512],
                        start=True, stop=True), (("spe", 1),))
                    p.w("vector", "spe", tmm)
                    t_fd = p.o("vector", lambda e, bk=bk, nch=nch: e.tensor_copy(
                        csB[0:1, nch * 512:(nch + 1) * 512], bk[0:1, :]), (("sve", 1),))
                    tslot_free[6 + nch % 2] = ("sve", t_fd)
                t_fb = None
                for nch in range(4):
                    bk = bank(6 + nch % 2)
                    p.w("tensor", "sve", t_fd)
                    p.wt("tensor", tslot_free[6 + nch % 2])
                    tmm = p.o("tensor", lambda e, bk=bk, nch=nch: e.matmul(
                        bk[:, :], ones1[0:1, :], csB[0:1, nch * 512:(nch + 1) * 512],
                        start=True, stop=True), (("spe", 1),))
                    p.w("vector", "spe", tmm)
                    t_fb = p.o("vector", lambda e, bk=bk, nch=nch: e.tensor_copy(
                        fdB[:, nch * 512:(nch + 1) * 512], bk[:, :]), (("sve", 1),))
                    tslot_free[6 + nch % 2] = ("sve", t_fb)
                for a in range(16):
                    tpe, tcp = pe_transpose(fdB[:, a * 128:(a + 1) * 128],
                                            fdt_f[:, r * 16 + a:r * 16 + a + 1],
                                            False, extra_waits=(("sve", t_fb),),
                                            n_avail=2, base=6, cols=1)
            p.w("vector", "sve", p.cur("sve"))
            tv = p.o("vector", lambda e: e.tensor_scalar_min(
                fdt_f[:, :], fdt_f[:, :], float(128 * 56)), (("sve", 1),))
            p.w("vector", "sve", tv)
            t_ci = p.o("vector", lambda e: e.tensor_copy(fdt_i[:, :], fdt_f[:, :]), (("sve", 1),))
        else:
            t_ci = p.cur("sve")
        t_gidx = None
        wgu_free = {("g", 0): None, ("g", 1): None, ("u", 0): None, ("u", 1): None}
        wd_free = [None, None, None, None]
        wgu_load = {}
        pgA_free = ("sve", t_p4)
        puB_free = ("sve", t_p4)
        pdC_free = ("sve", t_p4)
        sil_free = [None]
        first_wdma = [True]
        t_down_prev = [None]

        def wdma(src_ap, off, cols_bf, free, semname):
            if first_wdma[0]:
                p.w("gpsimd", "spe", t_p2pe)
                first_wdma[0] = False
            p.wt("gpsimd", free)
            return (semname, p.o("gpsimd", lambda e, src_ap=src_ap, off=off, cols_bf=cols_bf: e.dma_start(
                out=acarb(off, cols_bf), in_=src_ap), ((semname, 16),)))

        def wd_slice(src_e, sl):
            dj, mq = sl // 4, sl % 4
            return src_e[dj][:, mq * 1024:(mq + 1) * 1024]

        prep_done = [False]

        def emit_prep_dmas():
            prep_done[0] = True
            p.w("gpsimd", "spf", t_pref)
            p.w("gpsimd", "sve", t_ci)
            t_sc = None
            for c in range(32):
                t_sc = p.o("gpsimd", lambda e, c=c: e.indirect_dma_start(
                    out=listf, out_offset=IOA(fdt_i[:, c:c + 1], 0),
                    in_=tokid_s[:, c:c + 1], in_offset=None), (("ssc", 16),))
            p.w("gpsimd", "ssc", t_sc)
            t_rb = p.o("gpsimd", lambda e: e.dma_start(
                out=gidx_f[:, :], in_=listf[0:128 * 56].rearrange("(q x) one -> q (x one)", q=128)), (("srb", 16),))
            p.w("vector", "srb", t_rb)
            tv = p.o("vector", lambda e: e.tensor_scalar_min(
                gsc_f[:, :], gidx_f[:, :], float(2 * OWN)), (("sve", 1),))
            p.w("vector", "sve", tv)
            tv = p.o("vector", lambda e: e.tensor_copy(gsc_i[:, :], gsc_f[:, :]), (("sve", 1),))
            p.w("vector", "sve", tv)
            tv = p.o("vector", lambda e: e.tensor_single_scalar(
                gidx_i[:, :], gidx_f[:, :], OWN - 0.5, ALU.is_ge), (("sve", 1),))
            p.w("vector", "sve", tv)
            tv = p.o("vector", lambda e: e.tensor_copy(gsc_f[:, :], gidx_i[:, :]), (("sve", 1),))
            p.w("vector", "sve", tv)
            tv = p.o("vector", lambda e: e.scalar_tensor_tensor(
                gidx_f[:, :], gsc_f[:, :], -float(OWN), gidx_f[:, :], ALU.mult, ALU.add), (("sve", 1),))
            p.w("vector", "sve", tv)
            tv = p.o("vector", lambda e: e.tensor_scalar_min(
                gidx_f[:, :], gidx_f[:, :], float(OWN)), (("sve", 1),))
            p.w("vector", "sve", tv)
            tk_ = p.o("vector", lambda e: e.tensor_copy(gidx_i[:, :], gidx_f[:, :]), (("sve", 1),))
            return tk_


        if stage >= 6:
            for sp in range(2):
                tok0 = sp * 1024
                for g_i in (0, 1):
                    wgu_load[("g", g_i)] = wdma(swg[g_i], wg_off[g_i % 2], 2048, wgu_free[("g", g_i % 2)], "swg%d" % (g_i % 2))
                    wgu_load[("u", g_i)] = wdma(swu[g_i], wu_off[g_i % 2], 2048, wgu_free[("u", g_i % 2)], "swu%d" % (g_i % 2))
                for sl in (0, 1, 2, 3):
                    wgu_load[("d", sl)] = wdma(wd_slice(swd, sl), wd_off[sl % 4], 1024, wd_free[sl % 4], "swd%d" % (sl % 2))
                silb = acar(ttok_off[1], 1024)
                for m in range(MH):
                    g_i = m // 2
                    mm = m % 2
                    wgb = acarb(wg_off[g_i % 2], 2048)
                    wub = acarb(wu_off[g_i % 2], 2048)
                    p.wt("tensor", wgu_load[("g", g_i)])
                    p.wt("tensor", wgu_load[("u", g_i)])
                    if m == 0:
                        p.w("tensor", "sac", t_ln2sac)
                    p.wt("tensor", pgA_free)
                    t_g = None
                    for kd in range(KD):
                        for tt in range(2):
                            t_g = p.o("tensor", lambda e, wgb=wgb, mm=mm, kd=kd, tt=tt, tok0=tok0: e.matmul(
                                pA[:, tt * 512:(tt + 1) * 512],
                                wgb[:, (mm * 8 + kd) * 128:(mm * 8 + kd) * 128 + 128],
                                ln2_dT(kd, tok0 + tt * 512, tok0 + (tt + 1) * 512),
                                start=(kd == 0), stop=(kd == KD - 1)),
                                (("spe", 1),) if (kd == KD - 1 and tt == 1) else ())
                    p.wt("tensor", puB_free)
                    t_u = None
                    for kd in range(KD):
                        for tt in range(2):
                            t_u = p.o("tensor", lambda e, wub=wub, mm=mm, kd=kd, tt=tt, tok0=tok0: e.matmul(
                                pB[:, tt * 512:(tt + 1) * 512],
                                wub[:, (mm * 8 + kd) * 128:(mm * 8 + kd) * 128 + 128],
                                ln2_dT(kd, tok0 + tt * 512, tok0 + (tt + 1) * 512),
                                start=(kd == 0), stop=(kd == KD - 1)),
                                (("spe", 1),) if (kd == KD - 1 and tt == 1) else ())
                    p.w("scalar", "spe", t_g)
                    p.wt("scalar", sil_free[0])
                    if sp == 0 and m == 0:
                        p.w("scalar", "sve", t_ci)
                        p.w("scalar", "szr", t_prezero)
                    t_sil = p.o("scalar", lambda e, silb=silb: e.activation(
                        silb, pA[:, :], AF.Silu), (("sac", 1),))
                    pgA_free = ("sac", t_sil)
                    p.w("vector", "spe", t_u)
                    p.w("vector", "sac", t_sil)
                    if m == 0 and t_down_prev[0] is not None:
                        p.w("vector", "spe", t_down_prev[0])
                    t_mul = p.o("vector", lambda e, m=m, silb=silb: e.tensor_tensor(
                        h1(m, 0, 1024, "sh"), pB[:, :], silb, ALU.mult), (("sve", 1),))
                    puB_free = ("sve", t_mul)
                    sil_free[0] = ("sve", t_mul)
                    if mm == 1:
                        wgu_free[("g", g_i % 2)] = ("sac", t_sil)
                        wgu_free[("u", g_i % 2)] = ("sve", t_mul)
                        if g_i + 2 < 16:
                            wgu_load[("g", g_i + 2)] = wdma(swg[g_i + 2], wg_off[(g_i + 2) % 2],
                                                            2048, wgu_free[("g", (g_i + 2) % 2)], "swg%d" % ((g_i + 2) % 2))
                            wgu_load[("u", g_i + 2)] = wdma(swu[g_i + 2], wu_off[(g_i + 2) % 2],
                                                            2048, wgu_free[("u", (g_i + 2) % 2)], "swu%d" % ((g_i + 2) % 2))
                t_h1_all = p.cur("sve")
                t_d = None
                for dj in range(8):
                    p.wt("tensor", pdC_free)
                    t_d = None
                    for mq in range(4):
                        sl = dj * 4 + mq
                        p.wt("tensor", wgu_load[("d", sl)])
                        if sl == 0:
                            p.w("tensor", "sve", t_h1_all)
                        wdb = acarb(wd_off[sl % 4], 1024)
                        for mloc in range(8):
                            m = mq * 8 + mloc
                            for tt in range(2):
                                t_d = p.o("tensor", lambda e, wdb=wdb, m=m, mloc=mloc, tt=tt: e.matmul(
                                    pC[:, tt * 512:(tt + 1) * 512],
                                    wdb[:, mloc * 128:(mloc + 1) * 128],
                                    h1(m, tt * 512, (tt + 1) * 512, "sh"),
                                    start=(m == 0), stop=(m == MH - 1)),
                                    (("spe", 1),) if (mloc == 7 and tt == 1) else ())
                        wd_free[sl % 4] = ("spe", t_d if t_d is not None else 0)
                        if sl + 4 < 32:
                            wgu_load[("d", sl + 4)] = wdma(wd_slice(swd, sl + 4), wd_off[sl % 4],
                                                           1024, wd_free[sl % 4], "swd%d" % (sl % 2))
                    xsl = x_dT[:, dj * OWN + tok0:dj * OWN + tok0 + 1024]
                    p.w("vector", "spe", t_d)
                    t_acc = p.o("vector", lambda e, xsl=xsl: e.tensor_add(
                        xsl, xsl, pC[:, :]), (("sve", 1),))
                    pdC_free = ("sve", t_acc)
                t_down_prev[0] = t_d
                if sp == 0 and stage >= 5:
                    t_gidx_mid = emit_prep_dmas()

        if stage >= 5 and not prep_done[0]:
            t_gidx = emit_prep_dmas()
        elif prep_done[0]:
            t_gidx = t_gidx_mid
        else:
            t_gidx = p.cur("sve")

        gt_free = [("sve", t_gt0), ("sve", t_gt1)]
        gt_load = [0, 0]
        t_scat_out = []
        scat_last = [None, None]
        if stage >= 7:
            t_shared_done = p.cur("sve")
            t_shared_pe = p.cur("spe")
            tslot_free[6] = ("sve", t_shared_done)
            tslot_free[7] = ("sve", t_shared_done)
            gt_free = [("spe", t_shared_pe), ("spe", t_shared_pe)]
            pgA_free = ("sve", t_shared_done)
            puB_free = ("sve", t_shared_done)
            pdC_free = ("sve", t_shared_done)
            t_gu_done = ("spe", t_shared_pe)
            t_gln_buf = [None]

            def emit_gather(li):
                for t8 in range(8):
                    gb = t8 % 2
                    col = li * 8 + t8
                    p.wt("gpsimd", gt_free[gb])
                    p.w("gpsimd", "sve", t_gidx)
                    p.w("gpsimd", "sl2", t_ln2tok_dma)
                    p.w("gpsimd", "szr", t_prezero)
                    gsem = "sg%d" % gb
                    gt_load[gb] = p.o("gpsimd", lambda e, gb=gb, col=col: e.indirect_dma_start(
                        out=acarb(gtok_off[gb], 1024), out_offset=None,
                        in_=ln2tok, in_offset=IOA(gidx_i[:, col:col + 1], 0)), ((gsem, 16),))
                    gv3 = arena[:, gln_off:gln_off + 4096].bitcast(BF16).rearrange(
                        "p (kd c) -> p kd c", kd=8)
                    tpe, tcp = pe_tr_group(
                        [acarb(gtok_off[gb], 1024)[:, kd * 128:(kd + 1) * 128] for kd in range(KD)],
                        gv3[:, :, t8 * 128:(t8 + 1) * 128], True,
                        extra_waits=((gsem, gt_load[gb]), t_gu_done),
                        n_avail=2, base=6)
                    gt_free[gb] = ("spe", tpe)
                t_gln_buf[0] = p.cur("sve")

            emit_gather(0)
            nx_load = {}
            for li in range(7):
                e_i = li
                if ("g", 0) in nx_load:
                    for g_i in (0, 1):
                        wgu_load[("g", g_i)] = nx_load[("g", g_i)]
                        wgu_load[("u", g_i)] = nx_load[("u", g_i)]
                else:
                    for g_i in (0, 1):
                        wgu_load[("g", g_i)] = wdma(wg[e_i, g_i], wg_off[g_i % 2], 2048, wgu_free[("g", g_i % 2)], "swg%d" % (g_i % 2))
                        wgu_load[("u", g_i)] = wdma(wu[e_i, g_i], wu_off[g_i % 2], 2048, wgu_free[("u", g_i % 2)], "swu%d" % (g_i % 2))
                for sl in (0, 1, 2, 3):
                    wgu_load[("d", sl)] = wdma(wd_slice(wd[e_i], sl), wd_off[sl % 4], 1024, wd_free[sl % 4], "swd%d" % (sl % 2))
                silb = acar(sil_off, 1024)
                t_gln_cur = t_gln_buf[0]
                t_u = None
                for m in range(MH):
                    g_i = m // 2
                    mm = m % 2
                    wgb = acarb(wg_off[g_i % 2], 2048)
                    wub = acarb(wu_off[g_i % 2], 2048)
                    p.wt("tensor", wgu_load[("g", g_i)])
                    p.wt("tensor", wgu_load[("u", g_i)])
                    if m == 0:
                        p.w("tensor", "sve", t_gln_cur)
                    p.wt("tensor", pgA_free)
                    t_g = None
                    for kd in range(KD):
                        for tt in range(2):
                            t_g = p.o("tensor", lambda e, wgb=wgb, mm=mm, kd=kd, tt=tt: e.matmul(
                                pA[:, tt * 512:(tt + 1) * 512],
                                wgb[:, (mm * 8 + kd) * 128:(mm * 8 + kd) * 128 + 128],
                                gln_dT(0, kd, tt * 512, (tt + 1) * 512),
                                start=(kd == 0), stop=(kd == KD - 1)),
                                (("spe", 1),) if (kd == KD - 1 and tt == 1) else ())
                    p.wt("tensor", puB_free)
                    t_u = None
                    for kd in range(KD):
                        for tt in range(2):
                            t_u = p.o("tensor", lambda e, wub=wub, mm=mm, kd=kd, tt=tt: e.matmul(
                                pB[:, tt * 512:(tt + 1) * 512],
                                wub[:, (mm * 8 + kd) * 128:(mm * 8 + kd) * 128 + 128],
                                gln_dT(0, kd, tt * 512, (tt + 1) * 512),
                                start=(kd == 0), stop=(kd == KD - 1)),
                                (("spe", 1),) if (kd == KD - 1 and tt == 1) else ())
                    p.w("scalar", "spe", t_g)
                    p.wt("scalar", sil_free[0])
                    t_sil = p.o("scalar", lambda e, silb=silb: e.activation(
                        silb, pA[:, :], AF.Silu), (("sac", 1),))
                    pgA_free = ("sac", t_sil)
                    p.w("vector", "spe", t_u)
                    p.w("vector", "sac", t_sil)
                    if m == 0 and t_down_prev[0] is not None:
                        p.w("vector", "spe", t_down_prev[0])
                    t_mul = p.o("vector", lambda e, m=m, silb=silb: e.tensor_tensor(
                        h1(m, 0, 1024, "rt"), pB[:, :], silb, ALU.mult), (("sve", 1),))
                    puB_free = ("sve", t_mul)
                    sil_free[0] = ("sve", t_mul)
                    if mm == 1:
                        wgu_free[("g", g_i % 2)] = ("sac", t_sil)
                        wgu_free[("u", g_i % 2)] = ("sve", t_mul)
                        if g_i + 2 < 16:
                            wgu_load[("g", g_i + 2)] = wdma(wg[e_i, g_i + 2], wg_off[(g_i + 2) % 2],
                                                            2048, wgu_free[("g", (g_i + 2) % 2)], "swg%d" % ((g_i + 2) % 2))
                            wgu_load[("u", g_i + 2)] = wdma(wu[e_i, g_i + 2], wu_off[(g_i + 2) % 2],
                                                            2048, wgu_free[("u", (g_i + 2) % 2)], "swu%d" % ((g_i + 2) % 2))
                t_gu_done = ("spe", t_u)
                t_h1_all = p.cur("sve")
                if li + 1 < 7:
                    e_n = li + 1
                    for g_i in (0, 1):
                        nx_load[("g", g_i)] = wdma(wg[e_n, g_i], wg_off[g_i % 2], 2048, wgu_free[("g", g_i % 2)], "swg%d" % (g_i % 2))
                        nx_load[("u", g_i)] = wdma(wu[e_n, g_i], wu_off[g_i % 2], 2048, wgu_free[("u", g_i % 2)], "swu%d" % (g_i % 2))
                    emit_gather(li + 1)
                dout = acarb(dout_off, 8192)
                t_d = None
                for dj in range(8):
                    p.wt("tensor", pdC_free)
                    t_d = None
                    for mq in range(4):
                        sl = dj * 4 + mq
                        p.wt("tensor", wgu_load[("d", sl)])
                        if sl == 0:
                            p.w("tensor", "sve", t_h1_all)
                        wdb = acarb(wd_off[sl % 4], 1024)
                        for mloc in range(8):
                            m = mq * 8 + mloc
                            for tt in range(2):
                                t_d = p.o("tensor", lambda e, wdb=wdb, m=m, mloc=mloc, tt=tt: e.matmul(
                                    pC[:, tt * 512:(tt + 1) * 512],
                                    wdb[:, mloc * 128:(mloc + 1) * 128],
                                    h1(m, tt * 512, (tt + 1) * 512, "rt"),
                                    start=(m == 0), stop=(m == MH - 1)),
                                    (("spe", 1),) if (mloc == 7 and tt == 1) else ())
                        wd_free[sl % 4] = ("spe", t_d if t_d is not None else 0)
                        if sl + 4 < 32:
                            wgu_load[("d", sl + 4)] = wdma(wd_slice(wd[e_i], sl + 4), wd_off[sl % 4],
                                                           1024, wd_free[sl % 4], "swd%d" % (sl % 2))
                    p.w("vector", "spe", t_d)
                    t_dc = p.o("vector", lambda e, dout=dout, dj=dj: e.tensor_copy(
                        dout[:, dj * 1024:(dj + 1) * 1024], pC[:, :]), (("sve", 1),))
                    pdC_free = ("sve", t_dc)
                t_down_prev[0] = t_d
                t_dout = p.cur("sve")
                for t8 in range(8):
                    tb = t8 % 2
                    ttok = acarb(ttok_off[tb], 1024)
                    col = li * 8 + t8
                    ssem = "so%d" % tb
                    ew = [("sve", t_dout)]
                    if scat_last[tb] is not None:
                        ew.append(scat_last[tb])
                    tpe, tcp = pe_tr_group(
                        [dout[:, dj * 1024 + t8 * 128:dj * 1024 + t8 * 128 + 128] for dj in range(KD)],
                        ttok[:, :], True,
                        extra_waits=ew, n_avail=2, base=6)
                    if li == 0 and t8 == 0:
                        p.w("gpsimd", "szr", t_prezero)
                    p.wt("gpsimd", tcp)
                    ts_ = p.o("gpsimd", lambda e, ttok=ttok, col=col: e.indirect_dma_start(
                        out=accC, out_offset=IOA(gsc_i[:, col:col + 1], 0),
                        in_=ttok, in_offset=None), ((ssem, 16),))
                    scat_last[tb] = (ssem, ts_)
                    t_scat_out.append(ts_)

        t_scat0 = p.cur("so0")
        t_scat1 = p.cur("so1")

        if stage >= 5:
            out_tick = [None, None, None, None]
            a_load = [None, None, None, None]
            p.w("vector", "sac", p.cur("sac"))
            p.w("sync", "spe", p.cur("spe"))
            p.w("sync", "so0", t_scat0)
            p.w("sync", "so1", t_scat1)
            p.w("sync", "szr", t_prezero)
            for t in range(NT3):
                b = t % 4
                xtk = acar(fxt_off[b], 1024)
                a1t = acarb(fa1_off[b], 1024)
                a2t = acarb(fa2_off[b], 1024)
                p.wt("sync", out_tick[b])
                a_load[b] = p.o("sync", lambda e, a1t=a1t, t=t: e.dma_start(
                    out=a1t, in_=accC[t * 128:(t + 1) * 128, :]), (("sal", 16),))
                a_load[b] = p.o("sync", lambda e, a2t=a2t, t=t: e.dma_start(
                    out=a2t, in_=accC[OWN + t * 128:OWN + (t + 1) * 128, :]), (("sal", 16),))
                tcp = None
                for j0 in range(2):
                    ew = []
                    if t == 0 and j0 == 0:
                        ew.append(("sve", p.cur("sve")))
                    tpe, tcp = pe_tr_group(
                        [x_dT[:, (j0 * 4 + j) * OWN + t * 128:(j0 * 4 + j) * OWN + t * 128 + 128]
                         for j in range(4)],
                        xtk[:, j0 * 512:(j0 + 1) * 512], False,
                        extra_waits=ew, n_avail=2, base=6)
                p.wt("vector", tcp)
                p.w("vector", "sal", a_load[b])
                tv = p.o("vector", lambda e, a1t=a1t, xtk=xtk, t=t: e.scalar_tensor_tensor(
                    xtk, a1t, sgall[:, t:t + 1], xtk, ALU.mult, ALU.add), (("sve", 1),))
                p.w("vector", "sve", tv)
                tv = p.o("vector", lambda e, a2t=a2t, xtk=xtk, t=t: e.scalar_tensor_tensor(
                    xtk, a2t, tauall[:, t:t + 1], xtk, ALU.mult, ALU.add), (("sve", 1),))
                osem = "sou" if b % 2 == 0 else "sov"
                p.w("sync", "sve", tv)
                t_out = p.o("sync", lambda e, xtk=xtk, t=t: e.dma_start(
                    out=yo[t * 128:(t + 1) * 128, :], in_=xtk), ((osem, 16),))
                out_tick[b] = (osem, t_out)
            p.w("sync", "sou", p.cur("sou"))
            p.w("sync", "sov", p.cur("sov"))
        else:
            p.w("sync", "sve", p.cur("sve"))
            p.o("sync", lambda e: e.dma_start(
                out=yo.rearrange("(q a) d -> q (a d)", q=128), in_=acar(0, 16384)), (("sou", 16),))
            p.w("sync", "sou", 16)

        p.emit(blk)
    return nc


def _host_inputs(L0_attn_x, L0_attn_w_qkva, L0_attn_gamma, L0_ffn_router_w,
                 L0_ffn_w_gate, L0_ffn_w_up, L0_ffn_w_down,
                 L0_ffn_sw_gate, L0_ffn_sw_up, L0_ffn_sw_down):
    f = np.float32
    x = np.ascontiguousarray(np.asarray(L0_attn_x, dtype=f))
    wqkva = np.asarray(L0_attn_w_qkva, dtype=f)
    gamma = np.asarray(L0_attn_gamma, dtype=f)
    router = np.asarray(L0_ffn_router_w, dtype=f)

    W = wqkva.reshape(3, 8, 128, 8, 128)
    arr = W.transpose(1, 0, 4, 3, 2)
    wq_t = np.ascontiguousarray(arr[:, [1, 2, 0]].reshape(24, 128, 1024))

    gam_t = np.ascontiguousarray(gamma.reshape(8, 128).T)
    rw_t = np.ascontiguousarray(router.reshape(7, 8, 128).transpose(2, 1, 0).reshape(128, 56))
    identf = np.eye(128, dtype=f)
    identb = np.eye(128, dtype=BF)
    ecol = np.zeros((7, 2), f)
    for e in range(7):
        ecol[e, 0] = e * 8
    tokid = np.zeros((128, 32), f)
    for a_ in range(16):
        tokid[:, a_] = a_ * 128 + np.arange(128)
        tokid[:, 16 + a_] = 2048 + a_ * 128 + np.arange(128)

    def gu(w):
        E = w.shape[0]
        return np.ascontiguousarray(
            np.asarray(w, f).reshape(E, 8, 128, 16, 2, 128).transpose(0, 3, 2, 4, 1, 5)
            .reshape(E, 16, 128, 2048))

    def dn(w):
        E = w.shape[0]
        return np.ascontiguousarray(
            np.asarray(w, f).reshape(E, 32, 128, 8, 128).transpose(0, 3, 2, 1, 4)
            .reshape(E, 8, 128, 4096))

    wg_t = gu(L0_ffn_w_gate); wu_t = gu(L0_ffn_w_up); wd_t = dn(L0_ffn_w_down)
    swg_t = gu(L0_ffn_sw_gate)[0]; swu_t = gu(L0_ffn_sw_up)[0]; swd_t = dn(L0_ffn_sw_down)[0]

    common = dict(wq=wq_t, gam=gam_t, rw=rw_t, identf=identf, identb=identb,
                  ecolf=ecol, tokidf=tokid,
                  wg=wg_t, wu=wu_t, wd=wd_t, swg=swg_t, swu=swu_t, swd=swd_t)
    in_maps = []
    for c in range(8):
        b, h = c // 2, c % 2
        own = x[b, h * OWN:(h + 1) * OWN]
        prefix = x[b, 0:OWN] if h == 1 else np.zeros((OWN, D), f)
        xs_ = np.ascontiguousarray(np.concatenate([prefix, own], axis=0))
        in_maps.append(dict(common, xs=xs_))
    return in_maps


def kernel(L0_attn_x, L0_attn_w_qkva, L0_attn_gamma, L0_ffn_router_w,
           L0_ffn_w_gate, L0_ffn_w_up, L0_ffn_w_down,
           L0_ffn_sw_gate, L0_ffn_sw_up, L0_ffn_sw_down):
    in_maps = _host_inputs(L0_attn_x, L0_attn_w_qkva, L0_attn_gamma, L0_ffn_router_w,
                           L0_ffn_w_gate, L0_ffn_w_up, L0_ffn_w_down,
                           L0_ffn_sw_gate, L0_ffn_sw_up, L0_ffn_sw_down)
    nc = build_nc(stage=8)
    res = run_bass_kernel_spmd(nc, in_maps, core_ids=list(range(8)))
    out = np.empty((4, SEQ, D), np.float32)
    for c in range(8):
        b, h = c // 2, c % 2
        out[b, h * OWN:(h + 1) * OWN] = np.asarray(res.results[c]["yo"])
    return out


# version 14
# speedup vs baseline: 1.1445x; 1.0213x over previous
import numpy as np
import ml_dtypes
from contextlib import ExitStack

import concourse.bass as bass
import concourse.mybir as mybir
from concourse.bass_core_rust import IndirectOffsetOnAxis as IOA
from concourse.bass_utils import run_bass_kernel_spmd

F32 = mybir.dt.float32
BF16 = mybir.dt.bfloat16
I32 = mybir.dt.int32
AF = mybir.ActivationFunctionType
ALU = mybir.AluOpType
AX = mybir.AxisListType.X

D = 1024
KD = 8
SEQ = 4096
OWN = 2048
NT1 = SEQ // 128
NT3 = OWN // 128
NE = 7
H = 4096
MH = 32
CAP = 512
NT4 = CAP // 128
SENT = 1.0e7
BF = ml_dtypes.bfloat16

ENGS = ("tensor", "vector", "scalar", "sync", "gpsimd")


class Prog:
    def __init__(self):
        self.q = {e: [] for e in ENGS}
        self.sems = {}

    def add_sem(self, name, handle):
        self.sems[name] = [handle, 0]

    def cur(self, name):
        return self.sems[name][1]

    def w(self, eng, name, val):
        if val > self.sems[name][1]:
            raise RuntimeError(f"wait on {name} for {val} > emitted {self.sems[name][1]}")
        if val > 0:
            self.q[eng].append(("w", name, val))

    def wt(self, eng, st_):
        if st_ is not None:
            self.w(eng, st_[0], st_[1])

    def o(self, eng, fn, incs=()):
        tick = None
        rec = []
        for nm, n in incs:
            self.sems[nm][1] += n
            rec.append((nm, n))
            if tick is None:
                tick = self.sems[nm][1]
        self.q[eng].append(("o", fn, rec))
        return tick

    def emit(self, blk):
        def runner(eng_name):
            def body(eng):
                for item in self.q[eng_name]:
                    if item[0] == "w":
                        eng.wait_ge(self.sems[item[1]][0], item[2])
                    else:
                        ins = item[1](eng)
                        for nm, n in item[2]:
                            ins = ins.then_inc(self.sems[nm][0], n)
            return body
        blk.tensor(runner("tensor"))
        blk.vector(runner("vector"))
        blk.scalar(runner("scalar"))
        blk.sync(runner("sync"))
        blk.gpsimd(runner("gpsimd"))


def build_nc(stage=8, dbg=False):
    nc = bass.Bass("TRN2", target_bir_lowering=False, debug=False)

    xs = nc.dram_tensor("xs", [SEQ, D], F32, kind="ExternalInput").ap()
    wq = nc.dram_tensor("wq", [24, 128, 1024], F32, kind="ExternalInput").ap()
    gam = nc.dram_tensor("gam", [128, 8], F32, kind="ExternalInput").ap()
    rw = nc.dram_tensor("rw", [128, 56], F32, kind="ExternalInput").ap()
    identf = nc.dram_tensor("identf", [128, 128], F32, kind="ExternalInput").ap()
    identb = nc.dram_tensor("identb", [128, 128], BF16, kind="ExternalInput").ap()
    ecolf = nc.dram_tensor("ecolf", [7, 2], F32, kind="ExternalInput").ap()
    tokidf = nc.dram_tensor("tokidf", [128, 32], F32, kind="ExternalInput").ap()
    wg = nc.dram_tensor("wg", [NE, 16, 128, 2048], F32, kind="ExternalInput").ap()
    wu = nc.dram_tensor("wu", [NE, 16, 128, 2048], F32, kind="ExternalInput").ap()
    wd = nc.dram_tensor("wd", [NE, 8, 128, 4096], F32, kind="ExternalInput").ap()
    swg = nc.dram_tensor("swg", [16, 128, 2048], F32, kind="ExternalInput").ap()
    swu = nc.dram_tensor("swu", [16, 128, 2048], F32, kind="ExternalInput").ap()
    swd = nc.dram_tensor("swd", [8, 128, 4096], F32, kind="ExternalInput").ap()
    yo = nc.dram_tensor("yo", [OWN, D], F32, kind="ExternalOutput").ap()
    dbgo = nc.dram_tensor("dbgo", [128, 96], F32, kind="ExternalOutput").ap() if dbg else None
    dbg2o = nc.dram_tensor("dbg2o", [256, D], BF16, kind="ExternalOutput").ap() if dbg else None
    dbg3o = nc.dram_tensor("dbg3o", [512, D], F32, kind="ExternalOutput").ap() if dbg else None
    dbg4o = nc.dram_tensor("dbg4o", [128, KD * OWN], F32, kind="ExternalOutput").ap() if dbg else None
    ln2tok = nc.dram_tensor("ln2tok", [OWN + 1, D], BF16, kind="Internal").ap()
    listf = nc.dram_tensor("listf", [128 * 56 + 1, 1], F32, kind="Internal").ap()
    accC = nc.dram_tensor("accC", [2 * OWN + 1, D], BF16, kind="Internal").ap()

    p = Prog()
    es = ExitStack()
    with es:
        sb = lambda n, s, d: es.enter_context(nc.sbuf_tensor(n, s, d))
        psu = lambda n, s, d: es.enter_context(nc.psum_tensor(n, s, d))

        gam_s = sb("gam_s", [128, 8], F32)
        rw_s = sb("rw_s", [128, 56], BF16)
        identf_s = sb("identf_s", [128, 128], F32)
        identb_s = sb("identb_s", [128, 128], BF16)
        epscol = sb("epscol", [128, 1], F32)
        ones1 = sb("ones1", [1, 128], F32)
        ecol_s = sb("ecol_s", [7, 2], F32)
        tokid_s = sb("tokid_s", [128, 32], F32)
        sent_s = sb("sent_s", [128, 56], F32)
        st = sb("st", [128, 64], F32)
        p4b = sb("p4b", [128, 64], F32)
        sgall = sb("sgall", [128, 16], F32)
        tauall = sb("tauall", [128, 16], F32)
        gidx_f = sb("gidx_f", [128, 56], F32)
        gidx_i = sb("gidx_i", [128, 56], I32)
        gsc_f = sb("gsc_f", [128, 56], F32)
        gsc_i = sb("gsc_i", [128, 56], I32)
        fdt_f = sb("fdt_f", [128, 32], F32)
        fdt_i = sb("fdt_i", [128, 32], I32)
        x_dT = sb("x_dT", [128, KD * OWN], F32)
        AR = 34816
        arena = sb("arena", [128, AR], F32)

        def acar(off, cols):
            assert off + cols <= AR
            return arena[:, off:off + cols]

        def acarb(off, cols_bf16):
            assert off * 2 + cols_bf16 <= 2 * AR
            return arena[:, off:off + (cols_bf16 + 1) // 2].bitcast(BF16)

        def acari(off, cols):
            assert off + cols <= AR
            return arena[:, off:off + cols].bitcast(I32)

        h_off = 0

        def h_dT(kd, c0, c1):
            v = arena[:, h_off + kd * 2048:h_off + (kd + 1) * 2048].bitcast(BF16)
            return v[:, c0:c1]

        def h_dT3(t):
            v = arena[:, 0:16384].bitcast(BF16).rearrange("p (kd c) -> p kd c", kd=8)
            return v[:, :, t * 128:(t + 1) * 128]
        SLOT = [16384 + k * 3584 for k in range(4)]
        y_off = 16384
        kv_off = 20480
        a_off = 24576
        t1_off = 28672
        wqr_off = [31744, 32256]
        wqg_off = [32768, 33280]
        zz_off = 33792


        ln2_off = 0

        def ln2_dT(kd, c0, c1):
            v = arena[:, ln2_off + kd * 1024:ln2_off + (kd + 1) * 1024].bitcast(BF16)
            return v[:, c0:c1]
        fo = 8192
        wg_off = [fo, fo + 1024]; fo += 2048
        wu_off = [fo, fo + 1024]; fo += 2048
        wd_off = [fo, fo + 512, fo + 1024, fo + 1536]; fo += 2048
        h1hi_off = fo; fo += 8192

        def h1(m, c0, c1, kind):
            if kind == "rt":
                off = (m * 512) if m < 16 else (h1hi_off + (m - 16) * 512)
            else:
                off = h1hi_off + m * 512
            v = arena[:, off:off + 512].bitcast(BF16)
            return v[:, c0:c1]
        gln_off = fo; fo += 4096

        def gln_dT(b, kd, c0, c1):
            v = arena[:, gln_off + kd * 512:gln_off + (kd + 1) * 512].bitcast(BF16)
            return v[:, c0:c1]
        gtok_off = [31744, 32256]
        fo += 1024
        sil_off = fo; fo += 1024
        dout_off = fo; fo += 4096
        ttok_off = [fo, fo + 512]; fo += 1024
        assert fo <= AR
        fa1_off = [22528, 24576, 14336, 16384]
        fa2_off = [23040, 25088, 14848, 16896]
        fxt_off = [23552, 25600, 15360, 17408]

        mskT_off = [16384, 18432]
        PB = [20480 + k * 2048 for k in range(7)]

        pA = psu("pA", [128, 1024], F32)
        pB = psu("pB", [128, 1024], F32)
        pC = psu("pC", [128, 1024], F32)
        pD = psu("pD", [128, 1024], F32)
        PS = [pA, pB, pC, pD]

        def bank(i):
            return PS[i // 2][:, (i % 2) * 512:(i % 2) * 512 + 512]

        SEMS = ("spe", "sve", "sac", "sou",
                "sdc", "srw", "spf", "szr", "sx0", "sx1", "sq0", "sq1", "sl2",
                "swg0", "swg1", "swu0", "swu1", "swd0", "swd1",
                "sg0", "sg1", "ssc", "srb", "so0", "so1", "sal", "sov")
        for nm in SEMS:
            p.add_sem(nm, es.enter_context(nc.semaphore(nm)))

        blk = es.enter_context(nc.Block())

        p.o("sync", lambda e: e.dma_start(out=gam_s[:, :], in_=gam), (("sdc", 16),))
        p.o("sync", lambda e: e.dma_start(out=identf_s[:, :], in_=identf), (("sdc", 16),))
        p.o("sync", lambda e: e.dma_start(out=identb_s[:, :], in_=identb), (("sdc", 16),))
        p.o("sync", lambda e: e.dma_start(out=ecol_s[:, :], in_=ecolf), (("sdc", 16),))
        p.o("sync", lambda e: e.dma_start(out=tokid_s[:, :], in_=tokidf), (("sdc", 16),))
        p.o("gpsimd", lambda e: e.dma_start(out=rw_s[:, :], in_=rw), (("srw", 16),))
        t_consts = p.cur("sdc")
        t_rw = p.cur("srw")
        t_eps = p.o("vector", lambda e: e.memset(epscol[:, :], 1e-5), (("sve", 1),))
        p.o("vector", lambda e: e.memset(ones1[:, :], 1.0), (("sve", 1),))
        p.o("vector", lambda e: e.memset(sent_s[:, :], SENT), (("sve", 1),))
        zz = acar(zz_off, 1024)
        t_zz = p.o("vector", lambda e: e.memset(zz, 0.0), (("sve", 1),))
        t_setup_v = p.cur("sve")
        t_gt0 = p.o("vector", lambda e: e.memset(acarb(gtok_off[0], 1024), 0.0), (("sve", 1),))
        t_gt1 = p.o("vector", lambda e: e.memset(acarb(gtok_off[1], 1024), 0.0), (("sve", 1),))
        p.w("gpsimd", "sve", t_setup_v)
        t_pref = p.o("gpsimd", lambda e: e.dma_start(
            out=listf[0:128 * 56].rearrange("(q x) one -> q (x one)", q=128), in_=sent_s[:, :]), (("spf", 16),))
        for t in range(32):
            p.o("gpsimd", lambda e, t=t: e.dma_start(
                out=accC[t * 128:(t + 1) * 128, :], in_=zz), (("szr", 16),))
        p.o("gpsimd", lambda e: e.dma_start(
            out=ln2tok[OWN:OWN + 1, :], in_=zz[0:1, :]), (("szr", 16),))
        t_prezero = p.cur("szr")

        tslot_free = [None] * 8
        tslot_i = [0]

        def tslot(n_avail=8, base=0):
            i = base + (tslot_i[0] % n_avail)
            tslot_i[0] += 1
            return i

        def pe_tr_group(srcs, dst_ap, bf16, extra_waits=(), n_avail=8, base=0,
                        copy_eng="vector"):
            assert len(srcs) <= (8 if bf16 else 4)
            i = tslot(n_avail, base)
            bk = bank(i)
            bview = bk.bitcast(BF16) if bf16 else bk
            p.w("tensor", "sdc", t_consts)
            p.wt("tensor", tslot_free[i])
            for w_ in extra_waits:
                p.wt("tensor", w_)
            tpe = None
            for j, src_ap in enumerate(srcs):
                pv = bview[:, j * 128:(j + 1) * 128]
                tpe = p.o("tensor", lambda e, pv=pv, src_ap=src_ap: e.transpose(
                    pv, src_ap, identb_s[:, :] if bf16 else identf_s[:, :]), (("spe", 1),))
            sem = "sve" if copy_eng == "vector" else "sac"
            p.w(copy_eng, "spe", tpe)
            pall = bview[:, 0:len(srcs) * 128]
            if copy_eng == "vector":
                tcp = p.o(copy_eng, lambda e, dst_ap=dst_ap, pall=pall: e.tensor_copy(
                    dst_ap, pall), ((sem, 1),))
            else:
                tcp = p.o(copy_eng, lambda e, dst_ap=dst_ap, pall=pall: e.copy(
                    dst_ap, pall), ((sem, 1),))
            tslot_free[i] = (sem, tcp)
            return tpe, (sem, tcp)

        def pe_transpose(src_ap, dst_ap, bf16, extra_waits=(), n_avail=8, base=0,
                         copy_eng="vector", rows=128, cols=128):
            i = tslot(n_avail, base)
            if bf16:
                pview = bank(i)[:, 0:64].bitcast(BF16)
            else:
                pview = bank(i)[:, 0:128]
            pout = pview[0:rows, :]
            pcp = pview[0:rows, 0:cols]
            p.w("tensor", "sdc", t_consts)
            p.wt("tensor", tslot_free[i])
            for w_ in extra_waits:
                p.wt("tensor", w_)
            tpe = p.o("tensor", lambda e, pout=pout, src_ap=src_ap: e.transpose(
                pout, src_ap, identb_s[:, :] if bf16 else identf_s[:, :]), (("spe", 1),))
            sem = "sve" if copy_eng == "vector" else "sac"
            p.w(copy_eng, "spe", tpe)
            if copy_eng == "vector":
                tcp = p.o(copy_eng, lambda e, dst_ap=dst_ap, pcp=pcp: e.tensor_copy(
                    dst_ap, pcp), ((sem, 1),))
            else:
                tcp = p.o(copy_eng, lambda e, dst_ap=dst_ap, pcp=pcp: e.copy(
                    dst_ap, pcp), ((sem, 1),))
            tslot_free[i] = (sem, tcp)
            return tpe, (sem, tcp)

        xt_free = [None] * 4
        xt_load = [0] * 4
        last_htr = [None]
        G = 4

        def slotap(k):
            xt = acar(SLOT[k], 1024)
            xsq = acar(SLOT[k] + 1024, 1024)
            z = acar(SLOT[k] + 2048, 1024)
            htok = acarb(SLOT[k] + 3072, 1024)
            return xt, xsq, z, htok

        def norm_steps(t, k, rms, src_dma):
            xt, xsq, z, htok = slotap(k)
            c = k * 16
            s1 = st[:, c + 0:c + 1]; mcol = st[:, c + 1:c + 2]; ssc = st[:, c + 2:c + 3]
            msq = st[:, c + 3:c + 4]; var = st[:, c + 4:c + 5]; svc = st[:, c + 5:c + 6]
            rstd = st[:, c + 6:c + 7]; zss = st[:, c + 7:c + 8]; nrm = st[:, c + 8:c + 9]
            nmx = st[:, c + 9:c + 10]; rec = st[:, c + 10:c + 11]
            tk = {}

            def wt_ld(eng):
                for x_ in tk["ld"]:
                    p.wt(eng, x_)

            def st0():
                tk["ld"] = src_dma()

            def st1():
                wt_ld("scalar")
                tk["ss"] = p.o("scalar", lambda e: e.activation(
                    xsq, xt, AF.Square, accum_out=ssc), (("sac", 1),))
                wt_ld("vector")
                tk["s1"] = p.o("vector", lambda e: e.reduce_sum(s1, xt, axis=AX), (("sve", 1),))

            def st2():
                p.w("vector", "sve", tk["s1"])
                tk["m"] = p.o("vector", lambda e: e.tensor_scalar(
                    mcol, s1, 1.0 / D, None, ALU.mult), (("sve", 1),))

            def st3():
                p.w("vector", "sve", tk["m"])
                tk["msq"] = p.o("vector", lambda e: e.tensor_mul(msq, mcol, mcol), (("sve", 1),))

            def st4():
                p.w("vector", "sve", tk["msq"])
                p.w("vector", "sac", tk["ss"])
                tk["var"] = p.o("vector", lambda e: e.scalar_tensor_tensor(
                    var, ssc, 1.0 / D, msq, ALU.mult, ALU.subtract), (("sve", 1),))

            def st5():
                p.w("scalar", "sve", tk["var"])
                p.w("scalar", "sve", t_eps)
                tk["sv"] = p.o("scalar", lambda e: e.activation(
                    svc, var, AF.Sqrt, bias=epscol[:, 0:1]), (("sac", 1),))

            def st6():
                p.w("vector", "sac", tk["sv"])
                tk["rstd"] = p.o("vector", lambda e: e.reciprocal(rstd, svc), (("sve", 1),))

            def st7():
                p.w("vector", "sve", tk["rstd"])
                if rms:
                    tk["z"] = p.o("vector", lambda e: e.tensor_scalar(
                        z, xt, mcol, rstd, ALU.subtract, ALU.mult), (("sve", 1),))
                else:
                    if last_htr[0] is not None:
                        p.w("vector", "spe", last_htr[0])
                    if tk.get("l2gate") is not None:
                        p.wt("vector", tk["l2gate"])
                    tk["z"] = p.o("vector", lambda e: e.tensor_scalar(
                        htok, xt, mcol, rstd, ALU.subtract, ALU.mult), (("sve", 1),))

            def st8():
                p.w("scalar", "sve", tk["z"])
                tk["zss"] = p.o("scalar", lambda e: e.activation(
                    xsq, z, AF.Square, accum_out=zss), (("sac", 1),))

            def st9():
                p.w("scalar", "sac", tk["zss"])
                tk["nrm"] = p.o("scalar", lambda e: e.activation(nrm, zss, AF.Sqrt), (("sac", 1),))

            def st10():
                p.w("vector", "sac", tk["nrm"])
                tv = p.o("vector", lambda e: e.tensor_scalar_max(nmx, nrm, 1e-12), (("sve", 1),))
                p.w("vector", "sve", tv)
                tk["rec"] = p.o("vector", lambda e: e.reciprocal(rec, nmx), (("sve", 1),))

            def st11():
                p.w("vector", "sve", tk["rec"])
                if last_htr[0] is not None:
                    p.w("vector", "spe", last_htr[0])
                tk["h"] = p.o("vector", lambda e: e.tensor_scalar(
                    htok, z, rec, float(np.sqrt(D)), ALU.mult, ALU.mult), (("sve", 1),))

            if rms:
                return [st0, st1, st2, st3, st4, st5, st6, st7, st8, st9, st10, st11], tk
            return [st0, st1, st2, st3, st4, st5, st6, st7], tk

        for g in range(NT1 // G):
            steps = []
            tks = []
            for k in range(G):
                t = g * G + k
                xt, xsq, z, htok = slotap(k)

                def mk_dma(t=t, k=k, xt=xt):
                    def src_dma():
                        xsem = "sx%d" % (k % 2)
                        p.wt("sync", xt_free[k])
                        tick = p.o("sync", lambda e: e.dma_start(
                            out=xt, in_=xs[t * 128:(t + 1) * 128, :]), ((xsem, 16),))
                        return [(xsem, tick)]
                    return src_dma
                stp, tk = norm_steps(t, k, True, mk_dma())
                steps.append(stp)
                tks.append(tk)
            nst = max(len(x) for x in steps)
            for si in range(nst):
                for k in range(G):
                    if si < len(steps[k]):
                        steps[k][si]()
            xv3 = x_dT.ap().rearrange("p (kd c) -> p kd c", kd=8)
            for k in range(G):
                t = g * G + k
                xt, xsq, z, htok = slotap(k)
                tpe, tcp = pe_tr_group(
                    [htok[:, kd * 128:(kd + 1) * 128] for kd in range(KD)],
                    h_dT3(t), True, extra_waits=(("sve", tks[k]["h"]),),
                    copy_eng="scalar" if k % 2 else "vector")
                last_htr[0] = tpe
                if t >= 16:
                    for j0 in range(2):
                        ew = list(tks[k]["ld"]) if j0 == 0 else []
                        tpe, tcp = pe_tr_group(
                            [xt[:, (j0 * 4 + j) * 128:(j0 * 4 + j + 1) * 128] for j in range(4)],
                            xv3[:, j0 * 4:(j0 + 1) * 4, (t - 16) * 128:(t - 16) * 128 + 128],
                            False, extra_waits=ew,
                            copy_eng="scalar" if j0 else "vector")
                    xt_free[k] = ("spe", tpe)
                else:
                    xt_free[k] = ("sve", tks[k]["z"])

        t_h_all = p.cur("sve")
        t_h_sac = p.cur("sac")

        if stage >= 2:
            kv_s = acar(kv_off, 4096); a_s = acar(a_off, 4096); y_s = acar(y_off, 4096)
            wqr_free = [None, None]
            wqg_rdy = [0, 0]
            wqg_pe_done = [None, None]
            ci_load = [0, 0]

            def wq_chunk_sched(ci):
                bi = ci % 2
                wqr = acarb(wqr_off[bi], 1024)
                wqgb = acarb(wqg_off[bi], 1024)
                p.wt("gpsimd", wqr_free[bi])
                qsem = "sq%d" % bi
                ci_load[bi] = p.o("gpsimd", lambda e, wqr=wqr, ci=ci: e.dma_start(
                    out=wqr, in_=wq[ci]), ((qsem, 16),))
                p.w("vector", qsem, ci_load[bi])
                p.w("vector", "sdc", t_consts)
                p.wt("vector", wqg_pe_done[bi])
                tl = None
                for kd in range(KD):
                    tl = p.o("vector", lambda e, wqgb=wqgb, wqr=wqr, kd=kd: e.tensor_scalar(
                        wqgb[:, kd * 128:(kd + 1) * 128], wqr[:, kd * 128:(kd + 1) * 128],
                        gam_s[:, kd:kd + 1], None, ALU.mult), (("sve", 1),))
                wqr_free[bi] = ("sve", tl)
                wqg_rdy[bi] = tl

            t_p1pe = p.cur("spe")
            wq_chunk_sched(0)
            wq_chunk_sched(1)

            mm_slot_free = [None] * 8
            mm_slot_i = [0]

            def mm_slot():
                i = mm_slot_i[0] % 8
                mm_slot_i[0] += 1
                return i

            first_kv = [True]
            for j in range(KD):
                for part, dst in ((0, kv_s), (1, a_s)):
                    ci = 3 * j + part
                    bi = ci % 2
                    wqgb = acarb(wqg_off[bi], 1024)
                    p.w("tensor", "sve", wqg_rdy[bi])
                    if first_kv[0]:
                        p.w("tensor", "sve", t_h_all)
                        p.w("tensor", "sac", t_h_sac)
                        first_kv[0] = False
                    tpe = None
                    for n in range(8):
                        i = mm_slot()
                        psl = bank(i)
                        p.wt("tensor", mm_slot_free[i])
                        for kd in range(KD):
                            tpe = p.o("tensor", lambda e, psl=psl, wqgb=wqgb, kd=kd, n=n: e.matmul(
                                psl, wqgb[:, kd * 128:(kd + 1) * 128],
                                h_dT(kd, n * 512, (n + 1) * 512),
                                start=(kd == 0), stop=(kd == KD - 1)),
                                (("spe", 1),) if kd == KD - 1 else ())
                        if part == 0:
                            p.w("vector", "spe", tpe)
                            tev = p.o("vector", lambda e, dst=dst, psl=psl, n=n: e.tensor_copy(
                                dst[:, n * 512:(n + 1) * 512], psl), (("sve", 1),))
                            mm_slot_free[i] = ("sve", tev)
                        else:
                            p.w("scalar", "spe", tpe)
                            tev = p.o("scalar", lambda e, dst=dst, psl=psl, n=n: e.activation(
                                dst[:, n * 512:(n + 1) * 512], psl, AF.Sigmoid), (("sac", 1),))
                            mm_slot_free[i] = ("sac", tev)
                    wqg_pe_done[bi] = ("spe", tpe)
                    if ci + 2 < 24:
                        wq_chunk_sched(ci + 2)
                p.w("vector", "sac", p.cur("sac"))
                if j == 0:
                    p.w("vector", "spe", t_p1pe)
                p.o("vector", lambda e: e.tensor_tensor_scan(
                    y_s[:, :], a_s[:, :], kv_s[:, :], 0.0, ALU.mult, ALU.add), (("sve", 1),))
                ci = 3 * j + 2
                bi = ci % 2
                wqgb = acarb(wqg_off[bi], 1024)
                p.w("tensor", "sve", wqg_rdy[bi])
                tpe = None
                for n in range(4):
                    i = mm_slot()
                    psl = bank(i)
                    p.wt("tensor", mm_slot_free[i])
                    for kd in range(KD):
                        tpe = p.o("tensor", lambda e, psl=psl, wqgb=wqgb, kd=kd, n=n: e.matmul(
                            psl, wqgb[:, kd * 128:(kd + 1) * 128],
                            h_dT(kd, 2048 + n * 512, 2048 + (n + 1) * 512),
                            start=(kd == 0), stop=(kd == KD - 1)),
                            (("spe", 1),) if kd == KD - 1 else ())
                    p.w("vector", "spe", tpe)
                    t1 = acar(t1_off, 512)
                    p.o("vector", lambda e, t1=t1, psl=psl, n=n: e.scalar_tensor_tensor(
                        t1, psl, 1.0, y_s[:, 2048 + n * 512:2048 + (n + 1) * 512],
                        ALU.mult, ALU.mult), (("sve", 1),))
                    xsl = x_dT[:, j * OWN + n * 512:j * OWN + (n + 1) * 512]
                    tv = p.o("vector", lambda e, xsl=xsl, t1=t1: e.tensor_add(xsl, xsl, t1), (("sve", 1),))
                    mm_slot_free[i] = ("sve", tv)
                wqg_pe_done[bi] = ("spe", tpe)
                if ci + 2 < 24:
                    wq_chunk_sched(ci + 2)

        t_p2pe = p.cur("spe")

        t_ln2dma = 0
        if stage >= 3:
            ln2dma_slot = [None] * 4
            for g in range(NT3 // G):
                steps = []
                tks = []
                for k in range(G):
                    t = g * G + k
                    xt, xsq, z, htok = slotap(k)

                    def mk_src(t=t, k=k, xt=xt, g=g):
                        def src_dma():
                            out = []
                            for j0 in range(2):
                                ew = [xt_free[k]] if j0 == 0 else []
                                if g == 0 and k == 0 and j0 == 0:
                                    ew.append(("sve", p.cur("sve")))
                                tpe, tcp = pe_tr_group(
                                    [x_dT[:, (j0 * 4 + j) * OWN + t * 128:(j0 * 4 + j) * OWN + t * 128 + 128]
                                     for j in range(4)],
                                    xt[:, j0 * 512:(j0 + 1) * 512], False,
                                    extra_waits=ew,
                                    copy_eng="scalar" if j0 else "vector")
                                out.append(tcp)
                            return out
                        return src_dma

                    stp, tk = norm_steps(t, k, False, mk_src())
                    tk["l2gate"] = ln2dma_slot[k]
                    steps.append(stp)
                    tks.append(tk)
                nst = max(len(x) for x in steps)
                for si in range(nst):
                    for k in range(G):
                        if si < len(steps[k]):
                            steps[k][si]()
                lv3 = arena[:, 0:8192].bitcast(BF16).rearrange("p (kd c) -> p kd c", kd=8)
                for k in range(G):
                    t = g * G + k
                    xt, xsq, z, htok = slotap(k)
                    p.w("sync", "sve", tks[k]["z"])
                    t_ln2dma = p.o("sync", lambda e, htok=htok, t=t: e.dma_start(
                        out=ln2tok[t * 128:(t + 1) * 128, :], in_=htok), (("sl2", 16),))
                    ln2dma_slot[k] = ("sl2", t_ln2dma)
                    tpe, _ = pe_tr_group(
                        [htok[:, kd * 128:(kd + 1) * 128] for kd in range(KD)],
                        lv3[:, :, t * 128:(t + 1) * 128], True,
                        extra_waits=(("sve", tks[k]["z"]),),
                        copy_eng="scalar" if k % 2 else "vector")
                    last_htr[0] = tpe
                    xt_free[k] = ("sve", tks[k]["z"])

        t_ln2 = p.cur("sve")
        t_ln2sac = p.cur("sac")
        t_ln2tok_dma = p.cur("sl2")

        if stage >= 4:
            mskT = [acar(mskT_off[0], 2048), acar(mskT_off[1], 2048)]
            lg_ev = None
            mt_ev = None
            for t in range(NT3):
                bi = t % 2
                c = bi * 16
                ltile = p4b[:, 0:7]
                msk1 = p4b[:, 8:15]
                l2 = p4b[:, 16:23]
                msk2 = p4b[:, 24:31]
                m1 = st[:, c + 0:c + 1]; m2 = st[:, c + 1:c + 2]; d12 = st[:, c + 2:c + 3]
                d21 = st[:, c + 5:c + 6]
                sg = sgall[:, t:t + 1]; tau = tauall[:, t:t + 1]
                pl = pD[:, 0:7]
                p.w("tensor", "srw", t_rw)
                if t == 0:
                    p.w("tensor", "sve", t_ln2)
                if lg_ev is not None:
                    p.w("tensor", "sve", lg_ev)
                tpe = None
                for kd in range(KD):
                    tpe = p.o("tensor", lambda e, pl=pl, kd=kd, t=t: e.matmul(
                        pl, ln2_dT(kd, t * 128, (t + 1) * 128), rw_s[:, kd * 7:(kd + 1) * 7],
                        start=(kd == 0), stop=(kd == KD - 1)),
                        (("spe", 1),) if kd == KD - 1 else ())
                p.w("vector", "spe", tpe)
                if mt_ev is not None:
                    p.w("vector", "spe", mt_ev)
                lg_ev = p.o("vector", lambda e, ltile=ltile, pl=pl: e.tensor_copy(ltile, pl), (("sve", 1),))
                p.w("vector", "sve", lg_ev)
                tv = p.o("vector", lambda e, m1=m1, ltile=ltile: e.reduce_max(m1, ltile, axis=AX), (("sve", 1),))
                p.w("vector", "sve", tv)
                tv = p.o("vector", lambda e, msk1=msk1, ltile=ltile, m1=m1: e.tensor_scalar(
                    msk1, ltile, m1, None, ALU.is_ge), (("sve", 1),))
                p.w("vector", "sve", tv)
                tv = p.o("vector", lambda e, l2=l2, msk1=msk1, ltile=ltile: e.scalar_tensor_tensor(
                    l2, msk1, -1e30, ltile, ALU.mult, ALU.add), (("sve", 1),))
                p.w("vector", "sve", tv)
                tv = p.o("vector", lambda e, m2=m2, l2=l2: e.reduce_max(m2, l2, axis=AX), (("sve", 1),))
                p.w("vector", "sve", tv)
                tv = p.o("vector", lambda e, d12=d12, m1=m1, m2=m2: e.tensor_scalar(
                    d12, m1, m2, None, ALU.subtract), (("sve", 1),))
                p.w("vector", "sve", tv)
                tvb = p.o("vector", lambda e, d21=d21, m2=m2, m1=m1: e.tensor_scalar(
                    d21, m2, m1, None, ALU.subtract), (("sve", 1),))
                tv2 = p.o("vector", lambda e, msk2=msk2, l2=l2, m2=m2: e.tensor_scalar(
                    msk2, l2, m2, None, ALU.is_ge), (("sve", 1),))
                p.w("scalar", "sve", tvb)
                p.o("scalar", lambda e, sg=sg, d12=d12: e.activation(sg, d12, AF.Sigmoid), (("sac", 1),))
                p.o("scalar", lambda e, tau=tau, d21=d21: e.activation(tau, d21, AF.Sigmoid), (("sac", 1),))
                for r in range(2):
                    src = msk1 if r == 0 else msk2
                    mt_pe, _ = pe_transpose(
                        src, mskT[r][0:7, t * 128:(t + 1) * 128], False,
                        extra_waits=(("sve", tv2),), n_avail=6, base=0, rows=7)
                mt_ev = mt_pe
        t_p4 = p.cur("sve")

        if stage >= 5:
            ones7 = acar(PB[0], 2048)
            csB = acar(PB[1], 2048)
            tB2 = acar(PB[2], 2048)
            tB3 = acar(PB[3], 2048)
            selB = acar(PB[4], 2048)
            fdB = acar(PB[5], 2048)
            iB = acari(PB[6], 2048)
            iN = acari(PB[2], 2048)
            p.w("vector", "szr", t_prezero)
            t_o7 = p.o("vector", lambda e: e.memset(ones7[0:7, :], 1.0), (("sve", 1),))
            t_scat_ids = []
            msum = tB2
            p.w("vector", "sve", t_p4)
            tv = p.o("vector", lambda e: e.tensor_add(msum[0:7, :], mskT[0][0:7, :], mskT[1][0:7, :]), (("sve", 1),))
            p.w("vector", "sve", tv)
            tv = p.o("vector", lambda e: e.tensor_tensor_scan(
                csB[0:7, :], ones7[0:7, :], msum[0:7, :], 0.0, ALU.mult, ALU.add), (("sve", 1),))
            p.w("vector", "sve", tv)
            tv = p.o("vector", lambda e: e.tensor_scalar(
                tB3[0:7, :], csB[0:7, :], 1.0, None, ALU.subtract), (("sve", 1),))
            p.w("vector", "sve", tv)
            tv = p.o("vector", lambda e: e.tensor_copy(iB[0:7, :], tB3[0:7, :]), (("sve", 1),))
            p.w("vector", "sve", tv)
            tv = p.o("vector", lambda e: e.tensor_single_scalar(
                iN[0:7, :], iB[0:7, :], 127, ALU.bitwise_and), (("sve", 1),))
            p.w("vector", "sve", tv)
            tv = p.o("vector", lambda e: e.tensor_single_scalar(
                iB[0:7, :], iB[0:7, :], 7, ALU.arith_shift_right), (("sve", 1),))
            p.w("vector", "sve", tv)
            tv = p.o("vector", lambda e: e.tensor_copy(tB3[0:7, :], iN[0:7, :]), (("sve", 1),))
            p.w("vector", "sve", tv)
            tv = p.o("vector", lambda e: e.tensor_copy(tB2[0:7, :], iB[0:7, :]), (("sve", 1),))
            p.w("vector", "sve", tv)
            tv = p.o("vector", lambda e: e.scalar_tensor_tensor(
                tB3[0:7, :], tB3[0:7, :], 56.0, tB2[0:7, :], ALU.mult, ALU.add), (("sve", 1),))
            p.w("vector", "sve", tv)
            tv = p.o("vector", lambda e: e.tensor_scalar(
                tB3[0:7, :], tB3[0:7, :], ecol_s[0:7, 0:1], None, ALU.add), (("sve", 1),))
            p.w("vector", "sve", tv)
            tv = p.o("vector", lambda e: e.tensor_single_scalar(
                tB2[0:7, :], csB[0:7, :], 2 * CAP + 0.5, ALU.is_ge), (("sve", 1),))
            p.w("vector", "sve", tv)
            tv = p.o("vector", lambda e: e.scalar_tensor_tensor(
                tB3[0:7, :], tB2[0:7, :], SENT, tB3[0:7, :], ALU.mult, ALU.add), (("sve", 1),))
            p.w("vector", "sve", tv)
            for r in range(2):
                mt = mskT[r]
                t_sel = p.o("vector", lambda e, mt=mt: e.tensor_tensor(
                    selB[0:7, :], tB3[0:7, :], mt[0:7, :], ALU.mult), (("sve", 1),))
                t_fd = None
                for nch in range(4):
                    bk = bank(6 + nch % 2)
                    p.w("tensor", "sve", t_sel)
                    p.wt("tensor", tslot_free[6 + nch % 2])
                    tmm = p.o("tensor", lambda e, bk=bk, nch=nch: e.matmul(
                        bk[0:1, :], ones7[0:7, 0:1], selB[0:7, nch * 512:(nch + 1) * 512],
                        start=True, stop=True), (("spe", 1),))
                    p.w("vector", "spe", tmm)
                    t_fd = p.o("vector", lambda e, bk=bk, nch=nch: e.tensor_copy(
                        csB[0:1, nch * 512:(nch + 1) * 512], bk[0:1, :]), (("sve", 1),))
                    tslot_free[6 + nch % 2] = ("sve", t_fd)
                t_fb = None
                for nch in range(4):
                    bk = bank(6 + nch % 2)
                    p.w("tensor", "sve", t_fd)
                    p.wt("tensor", tslot_free[6 + nch % 2])
                    tmm = p.o("tensor", lambda e, bk=bk, nch=nch: e.matmul(
                        bk[:, :], ones1[0:1, :], csB[0:1, nch * 512:(nch + 1) * 512],
                        start=True, stop=True), (("spe", 1),))
                    p.w("vector", "spe", tmm)
                    t_fb = p.o("vector", lambda e, bk=bk, nch=nch: e.tensor_copy(
                        fdB[:, nch * 512:(nch + 1) * 512], bk[:, :]), (("sve", 1),))
                    tslot_free[6 + nch % 2] = ("sve", t_fb)
                for a in range(16):
                    tpe, tcp = pe_transpose(fdB[:, a * 128:(a + 1) * 128],
                                            fdt_f[:, r * 16 + a:r * 16 + a + 1],
                                            False, extra_waits=(("sve", t_fb),),
                                            n_avail=2, base=6, cols=1)
            p.w("vector", "sve", p.cur("sve"))
            tv = p.o("vector", lambda e: e.tensor_scalar_min(
                fdt_f[:, :], fdt_f[:, :], float(128 * 56)), (("sve", 1),))
            p.w("vector", "sve", tv)
            t_ci = p.o("vector", lambda e: e.tensor_copy(fdt_i[:, :], fdt_f[:, :]), (("sve", 1),))
        else:
            t_ci = p.cur("sve")
        t_gidx = None
        wgu_free = {("g", 0): None, ("g", 1): None, ("u", 0): None, ("u", 1): None}
        wd_free = [None, None, None, None]
        wgu_load = {}
        pgA_free = ("sve", t_p4)
        puB_free = ("sve", t_p4)
        pdC_free = ("sve", t_p4)
        sil_free = [None]
        first_wdma = [True]
        t_down_prev = [None]

        def wdma(src_ap, off, cols_bf, free, semname):
            if first_wdma[0]:
                p.w("gpsimd", "spe", t_p2pe)
                first_wdma[0] = False
            p.wt("gpsimd", free)
            return (semname, p.o("gpsimd", lambda e, src_ap=src_ap, off=off, cols_bf=cols_bf: e.dma_start(
                out=acarb(off, cols_bf), in_=src_ap), ((semname, 16),)))

        def wd_slice(src_e, sl):
            dj, mq = sl // 4, sl % 4
            return src_e[dj][:, mq * 1024:(mq + 1) * 1024]

        prep_done = [False]

        scat_ptr = [0]

        def emit_id_scatters(n):
            lo = scat_ptr[0]
            hi = min(lo + n, 32)
            if lo >= hi:
                return
            if lo == 0:
                p.w("gpsimd", "spf", t_pref)
                p.w("gpsimd", "sve", t_ci)
            for c in range(lo, hi):
                p.o("gpsimd", lambda e, c=c: e.indirect_dma_start(
                    out=listf, out_offset=IOA(fdt_i[:, c:c + 1], 0),
                    in_=tokid_s[:, c:c + 1], in_offset=None), (("ssc", 16),))
            scat_ptr[0] = hi

        def emit_prep_dmas():
            prep_done[0] = True
            emit_id_scatters(32)
            p.w("gpsimd", "ssc", p.cur("ssc"))
            t_rb = p.o("gpsimd", lambda e: e.dma_start(
                out=gidx_f[:, :], in_=listf[0:128 * 56].rearrange("(q x) one -> q (x one)", q=128)), (("srb", 16),))
            p.w("vector", "srb", t_rb)
            tv = p.o("vector", lambda e: e.tensor_scalar_min(
                gsc_f[:, :], gidx_f[:, :], float(2 * OWN)), (("sve", 1),))
            p.w("vector", "sve", tv)
            tv = p.o("vector", lambda e: e.tensor_copy(gsc_i[:, :], gsc_f[:, :]), (("sve", 1),))
            p.w("vector", "sve", tv)
            tv = p.o("vector", lambda e: e.tensor_single_scalar(
                gidx_i[:, :], gidx_f[:, :], OWN - 0.5, ALU.is_ge), (("sve", 1),))
            p.w("vector", "sve", tv)
            tv = p.o("vector", lambda e: e.tensor_copy(gsc_f[:, :], gidx_i[:, :]), (("sve", 1),))
            p.w("vector", "sve", tv)
            tv = p.o("vector", lambda e: e.scalar_tensor_tensor(
                gidx_f[:, :], gsc_f[:, :], -float(OWN), gidx_f[:, :], ALU.mult, ALU.add), (("sve", 1),))
            p.w("vector", "sve", tv)
            tv = p.o("vector", lambda e: e.tensor_scalar_min(
                gidx_f[:, :], gidx_f[:, :], float(OWN)), (("sve", 1),))
            p.w("vector", "sve", tv)
            tk_ = p.o("vector", lambda e: e.tensor_copy(gidx_i[:, :], gidx_f[:, :]), (("sve", 1),))
            return tk_


        if stage >= 6:
            for sp in range(2):
                tok0 = sp * 1024
                for g_i in (0, 1):
                    wgu_load[("g", g_i)] = wdma(swg[g_i], wg_off[g_i % 2], 2048, wgu_free[("g", g_i % 2)], "swg%d" % (g_i % 2))
                    wgu_load[("u", g_i)] = wdma(swu[g_i], wu_off[g_i % 2], 2048, wgu_free[("u", g_i % 2)], "swu%d" % (g_i % 2))
                for sl in (0, 1, 2, 3):
                    wgu_load[("d", sl)] = wdma(wd_slice(swd, sl), wd_off[sl % 4], 1024, wd_free[sl % 4], "swd%d" % (sl % 2))
                silb = acar(ttok_off[1], 1024)
                for m in range(MH):
                    g_i = m // 2
                    mm = m % 2
                    wgb = acarb(wg_off[g_i % 2], 2048)
                    wub = acarb(wu_off[g_i % 2], 2048)
                    p.wt("tensor", wgu_load[("g", g_i)])
                    p.wt("tensor", wgu_load[("u", g_i)])
                    if m == 0:
                        p.w("tensor", "sac", t_ln2sac)
                    p.wt("tensor", pgA_free)
                    t_g = None
                    for kd in range(KD):
                        for tt in range(2):
                            t_g = p.o("tensor", lambda e, wgb=wgb, mm=mm, kd=kd, tt=tt, tok0=tok0: e.matmul(
                                pA[:, tt * 512:(tt + 1) * 512],
                                wgb[:, (mm * 8 + kd) * 128:(mm * 8 + kd) * 128 + 128],
                                ln2_dT(kd, tok0 + tt * 512, tok0 + (tt + 1) * 512),
                                start=(kd == 0), stop=(kd == KD - 1)),
                                (("spe", 1),) if (kd == KD - 1 and tt == 1) else ())
                    p.wt("tensor", puB_free)
                    t_u = None
                    for kd in range(KD):
                        for tt in range(2):
                            t_u = p.o("tensor", lambda e, wub=wub, mm=mm, kd=kd, tt=tt, tok0=tok0: e.matmul(
                                pB[:, tt * 512:(tt + 1) * 512],
                                wub[:, (mm * 8 + kd) * 128:(mm * 8 + kd) * 128 + 128],
                                ln2_dT(kd, tok0 + tt * 512, tok0 + (tt + 1) * 512),
                                start=(kd == 0), stop=(kd == KD - 1)),
                                (("spe", 1),) if (kd == KD - 1 and tt == 1) else ())
                    p.w("scalar", "spe", t_g)
                    p.wt("scalar", sil_free[0])
                    if sp == 0 and m == 0:
                        p.w("scalar", "sve", t_ci)
                        p.w("scalar", "szr", t_prezero)
                    t_sil = p.o("scalar", lambda e, silb=silb: e.activation(
                        silb, pA[:, :], AF.Silu), (("sac", 1),))
                    pgA_free = ("sac", t_sil)
                    p.w("vector", "spe", t_u)
                    p.w("vector", "sac", t_sil)
                    if m == 0 and t_down_prev[0] is not None:
                        p.w("vector", "spe", t_down_prev[0])
                    t_mul = p.o("vector", lambda e, m=m, silb=silb: e.tensor_tensor(
                        h1(m, 0, 1024, "sh"), pB[:, :], silb, ALU.mult), (("sve", 1),))
                    puB_free = ("sve", t_mul)
                    sil_free[0] = ("sve", t_mul)
                    if mm == 1:
                        wgu_free[("g", g_i % 2)] = ("sac", t_sil)
                        wgu_free[("u", g_i % 2)] = ("sve", t_mul)
                        if g_i + 2 < 16:
                            wgu_load[("g", g_i + 2)] = wdma(swg[g_i + 2], wg_off[(g_i + 2) % 2],
                                                            2048, wgu_free[("g", (g_i + 2) % 2)], "swg%d" % ((g_i + 2) % 2))
                            wgu_load[("u", g_i + 2)] = wdma(swu[g_i + 2], wu_off[(g_i + 2) % 2],
                                                            2048, wgu_free[("u", (g_i + 2) % 2)], "swu%d" % ((g_i + 2) % 2))
                        if sp == 0 and stage >= 5 and m >= 9:
                            emit_id_scatters(4)
                t_h1_all = p.cur("sve")
                t_d = None
                for dj in range(8):
                    p.wt("tensor", pdC_free)
                    t_d = None
                    for mq in range(4):
                        sl = dj * 4 + mq
                        p.wt("tensor", wgu_load[("d", sl)])
                        if sl == 0:
                            p.w("tensor", "sve", t_h1_all)
                        wdb = acarb(wd_off[sl % 4], 1024)
                        for mloc in range(8):
                            m = mq * 8 + mloc
                            for tt in range(2):
                                t_d = p.o("tensor", lambda e, wdb=wdb, m=m, mloc=mloc, tt=tt: e.matmul(
                                    pC[:, tt * 512:(tt + 1) * 512],
                                    wdb[:, mloc * 128:(mloc + 1) * 128],
                                    h1(m, tt * 512, (tt + 1) * 512, "sh"),
                                    start=(m == 0), stop=(m == MH - 1)),
                                    (("spe", 1),) if (mloc == 7 and tt == 1) else ())
                        wd_free[sl % 4] = ("spe", t_d if t_d is not None else 0)
                        if sl + 4 < 32:
                            wgu_load[("d", sl + 4)] = wdma(wd_slice(swd, sl + 4), wd_off[sl % 4],
                                                           1024, wd_free[sl % 4], "swd%d" % (sl % 2))
                    xsl = x_dT[:, dj * OWN + tok0:dj * OWN + tok0 + 1024]
                    p.w("vector", "spe", t_d)
                    t_acc = p.o("vector", lambda e, xsl=xsl: e.tensor_add(
                        xsl, xsl, pC[:, :]), (("sve", 1),))
                    pdC_free = ("sve", t_acc)
                t_down_prev[0] = t_d
                if sp == 0 and stage >= 5:
                    t_gidx_mid = emit_prep_dmas()

        if stage >= 5 and not prep_done[0]:
            t_gidx = emit_prep_dmas()
        elif prep_done[0]:
            t_gidx = t_gidx_mid
        else:
            t_gidx = p.cur("sve")

        gt_free = [("sve", t_gt0), ("sve", t_gt1)]
        gt_load = [0, 0]
        t_scat_out = []
        scat_last = [None, None]
        if stage >= 7:
            t_shared_done = p.cur("sve")
            t_shared_pe = p.cur("spe")
            tslot_free[6] = ("sve", t_shared_done)
            tslot_free[7] = ("sve", t_shared_done)
            gt_free = [("spe", t_shared_pe), ("spe", t_shared_pe)]
            pgA_free = ("sve", t_shared_done)
            puB_free = ("sve", t_shared_done)
            pdC_free = ("sve", t_shared_done)
            t_gu_done = ("spe", t_shared_pe)
            t_gln_buf = [None]

            def emit_gather(li):
                for t8 in range(8):
                    gb = t8 % 2
                    col = li * 8 + t8
                    p.wt("gpsimd", gt_free[gb])
                    p.w("gpsimd", "sve", t_gidx)
                    p.w("gpsimd", "sl2", t_ln2tok_dma)
                    p.w("gpsimd", "szr", t_prezero)
                    gsem = "sg%d" % gb
                    gt_load[gb] = p.o("gpsimd", lambda e, gb=gb, col=col: e.indirect_dma_start(
                        out=acarb(gtok_off[gb], 1024), out_offset=None,
                        in_=ln2tok, in_offset=IOA(gidx_i[:, col:col + 1], 0)), ((gsem, 16),))
                    gv3 = arena[:, gln_off:gln_off + 4096].bitcast(BF16).rearrange(
                        "p (kd c) -> p kd c", kd=8)
                    tpe, tcp = pe_tr_group(
                        [acarb(gtok_off[gb], 1024)[:, kd * 128:(kd + 1) * 128] for kd in range(KD)],
                        gv3[:, :, t8 * 128:(t8 + 1) * 128], True,
                        extra_waits=((gsem, gt_load[gb]), t_gu_done),
                        n_avail=2, base=6)
                    gt_free[gb] = ("spe", tpe)
                t_gln_buf[0] = p.cur("sve")

            emit_gather(0)
            nx_load = {}
            for li in range(7):
                e_i = li
                if ("g", 0) in nx_load:
                    for g_i in (0, 1):
                        wgu_load[("g", g_i)] = nx_load[("g", g_i)]
                        wgu_load[("u", g_i)] = nx_load[("u", g_i)]
                else:
                    for g_i in (0, 1):
                        wgu_load[("g", g_i)] = wdma(wg[e_i, g_i], wg_off[g_i % 2], 2048, wgu_free[("g", g_i % 2)], "swg%d" % (g_i % 2))
                        wgu_load[("u", g_i)] = wdma(wu[e_i, g_i], wu_off[g_i % 2], 2048, wgu_free[("u", g_i % 2)], "swu%d" % (g_i % 2))
                for sl in (0, 1, 2, 3):
                    wgu_load[("d", sl)] = wdma(wd_slice(wd[e_i], sl), wd_off[sl % 4], 1024, wd_free[sl % 4], "swd%d" % (sl % 2))
                silb = acar(sil_off, 1024)
                t_gln_cur = t_gln_buf[0]
                t_u = None
                for m in range(MH):
                    g_i = m // 2
                    mm = m % 2
                    wgb = acarb(wg_off[g_i % 2], 2048)
                    wub = acarb(wu_off[g_i % 2], 2048)
                    p.wt("tensor", wgu_load[("g", g_i)])
                    p.wt("tensor", wgu_load[("u", g_i)])
                    if m == 0:
                        p.w("tensor", "sve", t_gln_cur)
                    p.wt("tensor", pgA_free)
                    t_g = None
                    for kd in range(KD):
                        for tt in range(2):
                            t_g = p.o("tensor", lambda e, wgb=wgb, mm=mm, kd=kd, tt=tt: e.matmul(
                                pA[:, tt * 512:(tt + 1) * 512],
                                wgb[:, (mm * 8 + kd) * 128:(mm * 8 + kd) * 128 + 128],
                                gln_dT(0, kd, tt * 512, (tt + 1) * 512),
                                start=(kd == 0), stop=(kd == KD - 1)),
                                (("spe", 1),) if (kd == KD - 1 and tt == 1) else ())
                    p.wt("tensor", puB_free)
                    t_u = None
                    for kd in range(KD):
                        for tt in range(2):
                            t_u = p.o("tensor", lambda e, wub=wub, mm=mm, kd=kd, tt=tt: e.matmul(
                                pB[:, tt * 512:(tt + 1) * 512],
                                wub[:, (mm * 8 + kd) * 128:(mm * 8 + kd) * 128 + 128],
                                gln_dT(0, kd, tt * 512, (tt + 1) * 512),
                                start=(kd == 0), stop=(kd == KD - 1)),
                                (("spe", 1),) if (kd == KD - 1 and tt == 1) else ())
                    p.w("scalar", "spe", t_g)
                    p.wt("scalar", sil_free[0])
                    t_sil = p.o("scalar", lambda e, silb=silb: e.activation(
                        silb, pA[:, :], AF.Silu), (("sac", 1),))
                    pgA_free = ("sac", t_sil)
                    p.w("vector", "spe", t_u)
                    p.w("vector", "sac", t_sil)
                    if m == 0 and t_down_prev[0] is not None:
                        p.w("vector", "spe", t_down_prev[0])
                    t_mul = p.o("vector", lambda e, m=m, silb=silb: e.tensor_tensor(
                        h1(m, 0, 1024, "rt"), pB[:, :], silb, ALU.mult), (("sve", 1),))
                    puB_free = ("sve", t_mul)
                    sil_free[0] = ("sve", t_mul)
                    if mm == 1:
                        wgu_free[("g", g_i % 2)] = ("sac", t_sil)
                        wgu_free[("u", g_i % 2)] = ("sve", t_mul)
                        if g_i + 2 < 16:
                            wgu_load[("g", g_i + 2)] = wdma(wg[e_i, g_i + 2], wg_off[(g_i + 2) % 2],
                                                            2048, wgu_free[("g", (g_i + 2) % 2)], "swg%d" % ((g_i + 2) % 2))
                            wgu_load[("u", g_i + 2)] = wdma(wu[e_i, g_i + 2], wu_off[(g_i + 2) % 2],
                                                            2048, wgu_free[("u", (g_i + 2) % 2)], "swu%d" % ((g_i + 2) % 2))
                t_gu_done = ("spe", t_u)
                t_h1_all = p.cur("sve")
                if li + 1 < 7:
                    e_n = li + 1
                    for g_i in (0, 1):
                        nx_load[("g", g_i)] = wdma(wg[e_n, g_i], wg_off[g_i % 2], 2048, wgu_free[("g", g_i % 2)], "swg%d" % (g_i % 2))
                        nx_load[("u", g_i)] = wdma(wu[e_n, g_i], wu_off[g_i % 2], 2048, wgu_free[("u", g_i % 2)], "swu%d" % (g_i % 2))
                    emit_gather(li + 1)
                dout = acarb(dout_off, 8192)
                t_d = None
                for dj in range(8):
                    p.wt("tensor", pdC_free)
                    t_d = None
                    for mq in range(4):
                        sl = dj * 4 + mq
                        p.wt("tensor", wgu_load[("d", sl)])
                        if sl == 0:
                            p.w("tensor", "sve", t_h1_all)
                        wdb = acarb(wd_off[sl % 4], 1024)
                        for mloc in range(8):
                            m = mq * 8 + mloc
                            for tt in range(2):
                                t_d = p.o("tensor", lambda e, wdb=wdb, m=m, mloc=mloc, tt=tt: e.matmul(
                                    pC[:, tt * 512:(tt + 1) * 512],
                                    wdb[:, mloc * 128:(mloc + 1) * 128],
                                    h1(m, tt * 512, (tt + 1) * 512, "rt"),
                                    start=(m == 0), stop=(m == MH - 1)),
                                    (("spe", 1),) if (mloc == 7 and tt == 1) else ())
                        wd_free[sl % 4] = ("spe", t_d if t_d is not None else 0)
                        if sl + 4 < 32:
                            wgu_load[("d", sl + 4)] = wdma(wd_slice(wd[e_i], sl + 4), wd_off[sl % 4],
                                                           1024, wd_free[sl % 4], "swd%d" % (sl % 2))
                    p.w("vector", "spe", t_d)
                    t_dc = p.o("vector", lambda e, dout=dout, dj=dj: e.tensor_copy(
                        dout[:, dj * 1024:(dj + 1) * 1024], pC[:, :]), (("sve", 1),))
                    pdC_free = ("sve", t_dc)
                t_down_prev[0] = t_d
                t_dout = p.cur("sve")
                for t8 in range(8):
                    tb = t8 % 2
                    ttok = acarb(ttok_off[tb], 1024)
                    col = li * 8 + t8
                    ssem = "so%d" % tb
                    ew = [("sve", t_dout)]
                    if scat_last[tb] is not None:
                        ew.append(scat_last[tb])
                    tpe, tcp = pe_tr_group(
                        [dout[:, dj * 1024 + t8 * 128:dj * 1024 + t8 * 128 + 128] for dj in range(KD)],
                        ttok[:, :], True,
                        extra_waits=ew, n_avail=2, base=6)
                    if li == 0 and t8 == 0:
                        p.w("gpsimd", "szr", t_prezero)
                    p.wt("gpsimd", tcp)
                    ts_ = p.o("gpsimd", lambda e, ttok=ttok, col=col: e.indirect_dma_start(
                        out=accC, out_offset=IOA(gsc_i[:, col:col + 1], 0),
                        in_=ttok, in_offset=None), ((ssem, 16),))
                    scat_last[tb] = (ssem, ts_)
                    t_scat_out.append(ts_)

        t_scat0 = p.cur("so0")
        t_scat1 = p.cur("so1")

        if stage >= 5:
            out_tick = [None, None, None, None]
            a_load = [None, None, None, None]
            p.w("vector", "sac", p.cur("sac"))
            p.w("sync", "spe", p.cur("spe"))
            p.w("sync", "so0", t_scat0)
            p.w("sync", "so1", t_scat1)
            p.w("sync", "szr", t_prezero)
            for t in range(NT3):
                b = t % 4
                xtk = acar(fxt_off[b], 1024)
                a1t = acarb(fa1_off[b], 1024)
                a2t = acarb(fa2_off[b], 1024)
                p.wt("sync", out_tick[b])
                a_load[b] = p.o("sync", lambda e, a1t=a1t, t=t: e.dma_start(
                    out=a1t, in_=accC[t * 128:(t + 1) * 128, :]), (("sal", 16),))
                a_load[b] = p.o("sync", lambda e, a2t=a2t, t=t: e.dma_start(
                    out=a2t, in_=accC[OWN + t * 128:OWN + (t + 1) * 128, :]), (("sal", 16),))
                tcp = None
                for j0 in range(2):
                    ew = []
                    if t == 0 and j0 == 0:
                        ew.append(("sve", p.cur("sve")))
                    tpe, tcp = pe_tr_group(
                        [x_dT[:, (j0 * 4 + j) * OWN + t * 128:(j0 * 4 + j) * OWN + t * 128 + 128]
                         for j in range(4)],
                        xtk[:, j0 * 512:(j0 + 1) * 512], False,
                        extra_waits=ew, n_avail=2, base=6)
                p.wt("vector", tcp)
                p.w("vector", "sal", a_load[b])
                tv = p.o("vector", lambda e, a1t=a1t, xtk=xtk, t=t: e.scalar_tensor_tensor(
                    xtk, a1t, sgall[:, t:t + 1], xtk, ALU.mult, ALU.add), (("sve", 1),))
                p.w("vector", "sve", tv)
                tv = p.o("vector", lambda e, a2t=a2t, xtk=xtk, t=t: e.scalar_tensor_tensor(
                    xtk, a2t, tauall[:, t:t + 1], xtk, ALU.mult, ALU.add), (("sve", 1),))
                osem = "sou" if b % 2 == 0 else "sov"
                p.w("sync", "sve", tv)
                t_out = p.o("sync", lambda e, xtk=xtk, t=t: e.dma_start(
                    out=yo[t * 128:(t + 1) * 128, :], in_=xtk), ((osem, 16),))
                out_tick[b] = (osem, t_out)
            p.w("sync", "sou", p.cur("sou"))
            p.w("sync", "sov", p.cur("sov"))
        else:
            p.w("sync", "sve", p.cur("sve"))
            p.o("sync", lambda e: e.dma_start(
                out=yo.rearrange("(q a) d -> q (a d)", q=128), in_=acar(0, 16384)), (("sou", 16),))
            p.w("sync", "sou", 16)

        p.emit(blk)
    return nc


def _host_inputs(L0_attn_x, L0_attn_w_qkva, L0_attn_gamma, L0_ffn_router_w,
                 L0_ffn_w_gate, L0_ffn_w_up, L0_ffn_w_down,
                 L0_ffn_sw_gate, L0_ffn_sw_up, L0_ffn_sw_down):
    f = np.float32
    x = np.ascontiguousarray(np.asarray(L0_attn_x, dtype=f))
    wqkva = np.asarray(L0_attn_w_qkva, dtype=f)
    gamma = np.asarray(L0_attn_gamma, dtype=f)
    router = np.asarray(L0_ffn_router_w, dtype=f)

    W = wqkva.reshape(3, 8, 128, 8, 128)
    arr = W.transpose(1, 0, 4, 3, 2)
    wq_t = np.ascontiguousarray(arr[:, [1, 2, 0]].reshape(24, 128, 1024))

    gam_t = np.ascontiguousarray(gamma.reshape(8, 128).T)
    rw_t = np.ascontiguousarray(router.reshape(7, 8, 128).transpose(2, 1, 0).reshape(128, 56))
    identf = np.eye(128, dtype=f)
    identb = np.eye(128, dtype=BF)
    ecol = np.zeros((7, 2), f)
    for e in range(7):
        ecol[e, 0] = e * 8
    tokid = np.zeros((128, 32), f)
    for a_ in range(16):
        tokid[:, a_] = a_ * 128 + np.arange(128)
        tokid[:, 16 + a_] = 2048 + a_ * 128 + np.arange(128)

    def gu(w):
        E = w.shape[0]
        return np.ascontiguousarray(
            np.asarray(w, f).reshape(E, 8, 128, 16, 2, 128).transpose(0, 3, 2, 4, 1, 5)
            .reshape(E, 16, 128, 2048))

    def dn(w):
        E = w.shape[0]
        return np.ascontiguousarray(
            np.asarray(w, f).reshape(E, 32, 128, 8, 128).transpose(0, 3, 2, 1, 4)
            .reshape(E, 8, 128, 4096))

    wg_t = gu(L0_ffn_w_gate); wu_t = gu(L0_ffn_w_up); wd_t = dn(L0_ffn_w_down)
    swg_t = gu(L0_ffn_sw_gate)[0]; swu_t = gu(L0_ffn_sw_up)[0]; swd_t = dn(L0_ffn_sw_down)[0]

    common = dict(wq=wq_t, gam=gam_t, rw=rw_t, identf=identf, identb=identb,
                  ecolf=ecol, tokidf=tokid,
                  wg=wg_t, wu=wu_t, wd=wd_t, swg=swg_t, swu=swu_t, swd=swd_t)
    in_maps = []
    for c in range(8):
        b, h = c // 2, c % 2
        own = x[b, h * OWN:(h + 1) * OWN]
        prefix = x[b, 0:OWN] if h == 1 else np.zeros((OWN, D), f)
        xs_ = np.ascontiguousarray(np.concatenate([prefix, own], axis=0))
        in_maps.append(dict(common, xs=xs_))
    return in_maps


def kernel(L0_attn_x, L0_attn_w_qkva, L0_attn_gamma, L0_ffn_router_w,
           L0_ffn_w_gate, L0_ffn_w_up, L0_ffn_w_down,
           L0_ffn_sw_gate, L0_ffn_sw_up, L0_ffn_sw_down):
    in_maps = _host_inputs(L0_attn_x, L0_attn_w_qkva, L0_attn_gamma, L0_ffn_router_w,
                           L0_ffn_w_gate, L0_ffn_w_up, L0_ffn_w_down,
                           L0_ffn_sw_gate, L0_ffn_sw_up, L0_ffn_sw_down)
    nc = build_nc(stage=8)
    res = run_bass_kernel_spmd(nc, in_maps, core_ids=list(range(8)))
    out = np.empty((4, SEQ, D), np.float32)
    for c in range(8):
        b, h = c // 2, c % 2
        out[b, h * OWN:(h + 1) * OWN] = np.asarray(res.results[c]["yo"])
    return out


# version 17 (final)
# speedup vs baseline: 1.3335x; 1.1651x over previous
import numpy as np
import ml_dtypes
from contextlib import ExitStack

import concourse.bass as bass
import concourse.mybir as mybir
from concourse.bass_core_rust import IndirectOffsetOnAxis as IOA
from concourse.bass_utils import run_bass_kernel_spmd

F32 = mybir.dt.float32
BF16 = mybir.dt.bfloat16
I32 = mybir.dt.int32
AF = mybir.ActivationFunctionType
ALU = mybir.AluOpType
AX = mybir.AxisListType.X

D = 1024
KD = 8
SEQ = 4096
OWN = 2048
NT1 = SEQ // 128
NT3 = OWN // 128
NE = 7
H = 4096
MH = 32
CAP = 512
CAP2 = 896
NT8 = 7
NT4 = CAP // 128
SENT = 1.0e7
BF = ml_dtypes.bfloat16

ENGS = ("tensor", "vector", "scalar", "sync", "gpsimd")


class Prog:
    def __init__(self):
        self.q = {e: [] for e in ENGS}
        self.sems = {}

    def add_sem(self, name, handle):
        self.sems[name] = [handle, 0]

    def cur(self, name):
        return self.sems[name][1]

    def w(self, eng, name, val):
        if val > self.sems[name][1]:
            raise RuntimeError(f"wait on {name} for {val} > emitted {self.sems[name][1]}")
        if val > 0:
            self.q[eng].append(("w", name, val))

    def wt(self, eng, st_):
        if st_ is not None:
            self.w(eng, st_[0], st_[1])

    def o(self, eng, fn, incs=()):
        tick = None
        rec = []
        for nm, n in incs:
            self.sems[nm][1] += n
            rec.append((nm, n))
            if tick is None:
                tick = self.sems[nm][1]
        self.q[eng].append(("o", fn, rec))
        return tick

    def emit(self, blk):
        def runner(eng_name):
            def body(eng):
                for item in self.q[eng_name]:
                    if item[0] == "w":
                        eng.wait_ge(self.sems[item[1]][0], item[2])
                    else:
                        ins = item[1](eng)
                        for nm, n in item[2]:
                            ins = ins.then_inc(self.sems[nm][0], n)
            return body
        blk.tensor(runner("tensor"))
        blk.vector(runner("vector"))
        blk.scalar(runner("scalar"))
        blk.sync(runner("sync"))
        blk.gpsimd(runner("gpsimd"))


def build_nc(stage=8, dbg=False):
    nc = bass.Bass("TRN2", target_bir_lowering=False, debug=False)

    xs = nc.dram_tensor("xs", [SEQ, D], F32, kind="ExternalInput").ap()
    wq = nc.dram_tensor("wq", [24, 128, 1024], F32, kind="ExternalInput").ap()
    gam = nc.dram_tensor("gam", [128, 8], F32, kind="ExternalInput").ap()
    rw = nc.dram_tensor("rw", [128, 56], F32, kind="ExternalInput").ap()
    identf = nc.dram_tensor("identf", [128, 128], F32, kind="ExternalInput").ap()
    identb = nc.dram_tensor("identb", [128, 128], BF16, kind="ExternalInput").ap()
    ecolf = nc.dram_tensor("ecolf", [7, 2], F32, kind="ExternalInput").ap()
    tokidf = nc.dram_tensor("tokidf", [128, 32], F32, kind="ExternalInput").ap()
    wg = nc.dram_tensor("wg", [NE, 16, 128, 2048], F32, kind="ExternalInput").ap()
    wu = nc.dram_tensor("wu", [NE, 16, 128, 2048], F32, kind="ExternalInput").ap()
    wd = nc.dram_tensor("wd", [NE, 8, 128, 4096], F32, kind="ExternalInput").ap()
    swg = nc.dram_tensor("swg", [16, 128, 2048], F32, kind="ExternalInput").ap()
    swu = nc.dram_tensor("swu", [16, 128, 2048], F32, kind="ExternalInput").ap()
    swd = nc.dram_tensor("swd", [8, 128, 4096], F32, kind="ExternalInput").ap()
    yo = nc.dram_tensor("yo", [OWN, D], F32, kind="ExternalOutput").ap()
    dbgo = nc.dram_tensor("dbgo", [128, 96], F32, kind="ExternalOutput").ap() if dbg else None
    dbg2o = nc.dram_tensor("dbg2o", [256, D], BF16, kind="ExternalOutput").ap() if dbg else None
    dbg3o = nc.dram_tensor("dbg3o", [512, D], F32, kind="ExternalOutput").ap() if dbg else None
    dbg4o = nc.dram_tensor("dbg4o", [128, KD * OWN], F32, kind="ExternalOutput").ap() if dbg else None
    dbg5o = nc.dram_tensor("dbg5o", [2 * OWN + 1, D], BF16, kind="ExternalOutput").ap() if dbg else None
    ln2tok = nc.dram_tensor("ln2tok", [OWN + 1, D], BF16, kind="Internal").ap()
    listf = nc.dram_tensor("listf", [128 * 56 + 1, 1], F32, kind="Internal").ap()
    accC = nc.dram_tensor("accC", [2 * OWN + 1, D], BF16, kind="Internal").ap()

    p = Prog()
    es = ExitStack()
    with es:
        sb = lambda n, s, d: es.enter_context(nc.sbuf_tensor(n, s, d))
        psu = lambda n, s, d: es.enter_context(nc.psum_tensor(n, s, d))

        gam_s = sb("gam_s", [128, 8], F32)
        rw_s = sb("rw_s", [128, 56], BF16)
        identf_s = sb("identf_s", [128, 128], F32)
        identb_s = sb("identb_s", [128, 128], BF16)
        epscol = sb("epscol", [128, 1], F32)
        ones1 = sb("ones1", [1, 128], F32)
        ecol_s = sb("ecol_s", [7, 2], F32)
        tokid_s = sb("tokid_s", [128, 32], F32)
        sent_s = sb("sent_s", [128, 56], F32)
        st = sb("st", [128, 64], F32)
        p4b = sb("p4b", [128, 64], F32)
        sgall = sb("sgall", [128, 16], F32)
        tauall = sb("tauall", [128, 16], F32)
        gidx_f = sb("gidx_f", [128, 56], F32)
        gidx_i = sb("gidx_i", [128, 56], I32)
        gsc_f = sb("gsc_f", [128, 56], F32)
        gsc_i = sb("gsc_i", [128, 56], I32)
        fdt_f = sb("fdt_f", [128, 32], F32)
        fdt_i = sb("fdt_i", [128, 32], I32)
        dbgm = sb("dbgm", [7, 1024], F32) if dbg else None
        x_dT = sb("x_dT", [128, KD * OWN], F32)
        AR = 34816
        arena = sb("arena", [128, AR], F32)

        def acar(off, cols):
            assert off + cols <= AR
            return arena[:, off:off + cols]

        def acarb(off, cols_bf16):
            assert off * 2 + cols_bf16 <= 2 * AR
            return arena[:, off:off + (cols_bf16 + 1) // 2].bitcast(BF16)

        def acari(off, cols):
            assert off + cols <= AR
            return arena[:, off:off + cols].bitcast(I32)

        h_off = 0

        def h_dT(kd, c0, c1):
            v = arena[:, h_off + kd * 2048:h_off + (kd + 1) * 2048].bitcast(BF16)
            return v[:, c0:c1]

        def h_dT3(t):
            v = arena[:, 0:16384].bitcast(BF16).rearrange("p (kd c) -> p kd c", kd=8)
            return v[:, :, t * 128:(t + 1) * 128]
        SLOT = [16384 + k * 3584 for k in range(4)]
        y_off = 16384
        kv_off = 20480
        a_off = 24576
        t1_off = 28672
        wqr_off = [31744, 32256]
        wqg_off = [32768, 33280]
        zz_off = 33792


        ln2_off = 0

        def ln2_dT(kd, c0, c1):
            v = arena[:, ln2_off + kd * 1024:ln2_off + (kd + 1) * 1024].bitcast(BF16)
            return v[:, c0:c1]
        fo = 8192
        wg_off = [fo, fo + 1024]; fo += 2048
        wu_off = [fo, fo + 1024]; fo += 2048
        wd_off = [fo, fo + 512, fo + 1024, fo + 1536]; fo += 2048
        h1hi_off = fo; fo += 8192

        def h1(m, c0, c1, kind):
            if kind == "rt":
                off = (m * 448) if m < 16 else (h1hi_off + (m - 16) * 448)
                w = 448
            else:
                off = h1hi_off + m * 512
                w = 512
            v = arena[:, off:off + w].bitcast(BF16)
            return v[:, c0:c1]
        gln_off = fo; fo += 4096

        def gln_dT(b, kd, c0, c1):
            v = arena[:, gln_off + kd * 448:gln_off + (kd + 1) * 448].bitcast(BF16)
            return v[:, c0:c1]
        gtok_off = [26624, 27136]
        fo += 1024
        sil_off = fo; fo += 1024
        dout_off = fo; fo += 4096
        ttok_off = [fo, fo + 512]; fo += 1024
        assert fo <= AR
        fa1_off = [22528, 24576, 14336, 16384]
        fa2_off = [23040, 25088, 14848, 16896]
        fxt_off = [23552, 25600, 15360, 17408]

        mskT_off = [16384, 18432]
        PB = [20480 + k * 2048 for k in range(7)]

        pA = psu("pA", [128, 1024], F32)
        pB = psu("pB", [128, 1024], F32)
        pC = psu("pC", [128, 1024], F32)
        pD = psu("pD", [128, 1024], F32)
        PS = [pA, pB, pC, pD]

        def bank(i):
            return PS[i // 2][:, (i % 2) * 512:(i % 2) * 512 + 512]

        SEMS = ("spe", "sve", "sac", "sou",
                "sdc", "srw", "spf", "szr", "sx0", "sx1", "sx2", "sx3", "sq0", "sq1", "sl2",
                "swg0", "swg1", "swu0", "swu1", "swd0", "swd1",
                "sg0", "sg1", "ssc", "srb", "so0", "so1", "sal", "sov")
        for nm in SEMS:
            p.add_sem(nm, es.enter_context(nc.semaphore(nm)))

        blk = es.enter_context(nc.Block())

        p.o("sync", lambda e: e.dma_start(out=gam_s[:, :], in_=gam), (("sdc", 16),))
        p.o("sync", lambda e: e.dma_start(out=identf_s[:, :], in_=identf), (("sdc", 16),))
        p.o("sync", lambda e: e.dma_start(out=identb_s[:, :], in_=identb), (("sdc", 16),))
        p.o("sync", lambda e: e.dma_start(out=ecol_s[:, :], in_=ecolf), (("sdc", 16),))
        p.o("sync", lambda e: e.dma_start(out=tokid_s[:, :], in_=tokidf), (("sdc", 16),))
        p.o("gpsimd", lambda e: e.dma_start(out=rw_s[:, :], in_=rw), (("srw", 16),))
        t_consts = p.cur("sdc")
        t_rw = p.cur("srw")
        t_eps = p.o("vector", lambda e: e.memset(epscol[:, :], 1e-5), (("sve", 1),))
        p.o("vector", lambda e: e.memset(ones1[:, :], 1.0), (("sve", 1),))
        p.o("vector", lambda e: e.memset(sent_s[:, :], SENT), (("sve", 1),))
        zz = acar(zz_off, 1024)
        t_zz = p.o("vector", lambda e: e.memset(zz, 0.0), (("sve", 1),))
        t_setup_v = p.cur("sve")
        p.w("gpsimd", "sve", t_setup_v)
        t_pref = p.o("gpsimd", lambda e: e.dma_start(
            out=listf[0:128 * 56].rearrange("(q x) one -> q (x one)", q=128), in_=sent_s[:, :]), (("spf", 16),))
        for t in range(32):
            p.o("gpsimd", lambda e, t=t: e.dma_start(
                out=accC[t * 128:(t + 1) * 128, :], in_=zz), (("szr", 16),))
        p.o("gpsimd", lambda e: e.dma_start(
            out=ln2tok[OWN:OWN + 1, :], in_=zz[0:1, :]), (("szr", 16),))
        t_prezero = p.cur("szr")

        tslot_free = [None] * 8
        tslot_i = [0]

        def tslot(n_avail=8, base=0):
            i = base + (tslot_i[0] % n_avail)
            tslot_i[0] += 1
            return i

        def pe_tr_group(srcs, dst_ap, bf16, extra_waits=(), n_avail=8, base=0,
                        copy_eng="vector"):
            assert len(srcs) <= (8 if bf16 else 4)
            i = tslot(n_avail, base)
            bk = bank(i)
            bview = bk.bitcast(BF16) if bf16 else bk
            p.w("tensor", "sdc", t_consts)
            p.wt("tensor", tslot_free[i])
            for w_ in extra_waits:
                p.wt("tensor", w_)
            tpe = None
            for j, src_ap in enumerate(srcs):
                pv = bview[:, j * 128:(j + 1) * 128]
                tpe = p.o("tensor", lambda e, pv=pv, src_ap=src_ap: e.transpose(
                    pv, src_ap, identb_s[:, :] if bf16 else identf_s[:, :]), (("spe", 1),))
            sem = "sve" if copy_eng == "vector" else "sac"
            p.w(copy_eng, "spe", tpe)
            pall = bview[:, 0:len(srcs) * 128]
            if copy_eng == "vector":
                tcp = p.o(copy_eng, lambda e, dst_ap=dst_ap, pall=pall: e.tensor_copy(
                    dst_ap, pall), ((sem, 1),))
            else:
                tcp = p.o(copy_eng, lambda e, dst_ap=dst_ap, pall=pall: e.copy(
                    dst_ap, pall), ((sem, 1),))
            tslot_free[i] = (sem, tcp)
            return tpe, (sem, tcp)

        def pe_transpose(src_ap, dst_ap, bf16, extra_waits=(), n_avail=8, base=0,
                         copy_eng="vector", rows=128, cols=128):
            i = tslot(n_avail, base)
            if bf16:
                pview = bank(i)[:, 0:64].bitcast(BF16)
            else:
                pview = bank(i)[:, 0:128]
            pout = pview[0:rows, :]
            pcp = pview[0:rows, 0:cols]
            p.w("tensor", "sdc", t_consts)
            p.wt("tensor", tslot_free[i])
            for w_ in extra_waits:
                p.wt("tensor", w_)
            tpe = p.o("tensor", lambda e, pout=pout, src_ap=src_ap: e.transpose(
                pout, src_ap, identb_s[:, :] if bf16 else identf_s[:, :]), (("spe", 1),))
            sem = "sve" if copy_eng == "vector" else "sac"
            p.w(copy_eng, "spe", tpe)
            if copy_eng == "vector":
                tcp = p.o(copy_eng, lambda e, dst_ap=dst_ap, pcp=pcp: e.tensor_copy(
                    dst_ap, pcp), ((sem, 1),))
            else:
                tcp = p.o(copy_eng, lambda e, dst_ap=dst_ap, pcp=pcp: e.copy(
                    dst_ap, pcp), ((sem, 1),))
            tslot_free[i] = (sem, tcp)
            return tpe, (sem, tcp)

        xt_free = [None] * 4
        xt_load = [0] * 4
        last_htr = [None]
        G = 4

        def slotap(k):
            xt = acar(SLOT[k], 1024)
            xsq = acar(SLOT[k] + 1024, 1024)
            z = acar(SLOT[k] + 2048, 1024)
            htok = acarb(SLOT[k] + 3072, 1024)
            return xt, xsq, z, htok

        def norm_steps(t, k, rms, src_dma):
            xt, xsq, z, htok = slotap(k)
            c = k * 16
            s1 = st[:, c + 0:c + 1]; mcol = st[:, c + 1:c + 2]; ssc = st[:, c + 2:c + 3]
            msq = st[:, c + 3:c + 4]; var = st[:, c + 4:c + 5]; svc = st[:, c + 5:c + 6]
            rstd = st[:, c + 6:c + 7]; zss = st[:, c + 7:c + 8]; nrm = st[:, c + 8:c + 9]
            nmx = st[:, c + 9:c + 10]; rec = st[:, c + 10:c + 11]
            tk = {}

            def wt_ld(eng):
                for x_ in tk["ld"]:
                    p.wt(eng, x_)

            def st0():
                tk["ld"] = src_dma()

            def st1():
                wt_ld("scalar")
                tk["ss"] = p.o("scalar", lambda e: e.activation(
                    xsq, xt, AF.Square, accum_out=ssc), (("sac", 1),))
                wt_ld("vector")
                tk["s1"] = p.o("vector", lambda e: e.reduce_sum(s1, xt, axis=AX), (("sve", 1),))

            def st2():
                p.w("vector", "sve", tk["s1"])
                tk["m"] = p.o("vector", lambda e: e.tensor_scalar(
                    mcol, s1, 1.0 / D, None, ALU.mult), (("sve", 1),))

            def st3():
                p.w("vector", "sve", tk["m"])
                tk["msq"] = p.o("vector", lambda e: e.tensor_mul(msq, mcol, mcol), (("sve", 1),))

            def st4():
                p.w("vector", "sve", tk["msq"])
                p.w("vector", "sac", tk["ss"])
                tk["var"] = p.o("vector", lambda e: e.scalar_tensor_tensor(
                    var, ssc, 1.0 / D, msq, ALU.mult, ALU.subtract), (("sve", 1),))

            def st5():
                p.w("scalar", "sve", tk["var"])
                p.w("scalar", "sve", t_eps)
                tk["sv"] = p.o("scalar", lambda e: e.activation(
                    svc, var, AF.Sqrt, bias=epscol[:, 0:1]), (("sac", 1),))

            def st6():
                p.w("vector", "sac", tk["sv"])
                tk["rstd"] = p.o("vector", lambda e: e.reciprocal(rstd, svc), (("sve", 1),))

            def st7():
                p.w("vector", "sve", tk["rstd"])
                if rms:
                    tk["z"] = p.o("vector", lambda e: e.tensor_scalar(
                        z, xt, mcol, rstd, ALU.subtract, ALU.mult), (("sve", 1),))
                else:
                    if last_htr[0] is not None:
                        p.w("vector", "spe", last_htr[0])
                    if tk.get("l2gate") is not None:
                        p.wt("vector", tk["l2gate"])
                    tk["z"] = p.o("vector", lambda e: e.tensor_scalar(
                        htok, xt, mcol, rstd, ALU.subtract, ALU.mult), (("sve", 1),))

            def st8():
                p.w("scalar", "sve", tk["z"])
                tk["zss"] = p.o("scalar", lambda e: e.activation(
                    xsq, z, AF.Square, accum_out=zss), (("sac", 1),))

            def st9():
                p.w("scalar", "sac", tk["zss"])
                tk["nrm"] = p.o("scalar", lambda e: e.activation(nrm, zss, AF.Sqrt), (("sac", 1),))

            def st10():
                p.w("vector", "sac", tk["nrm"])
                tv = p.o("vector", lambda e: e.tensor_scalar_max(nmx, nrm, 1e-12), (("sve", 1),))
                p.w("vector", "sve", tv)
                tk["rec"] = p.o("vector", lambda e: e.reciprocal(rec, nmx), (("sve", 1),))

            def st11():
                p.w("vector", "sve", tk["rec"])
                if last_htr[0] is not None:
                    p.w("vector", "spe", last_htr[0])
                tk["h"] = p.o("vector", lambda e: e.tensor_scalar(
                    htok, z, rec, float(np.sqrt(D)), ALU.mult, ALU.mult), (("sve", 1),))

            if rms:
                return [st0, st1, st2, st3, st4, st5, st6, st7, st8, st9, st10, st11], tk
            return [st0, st1, st2, st3, st4, st5, st6, st7], tk

        for g in range(NT1 // G):
            steps = []
            tks = []
            for k in range(G):
                t = g * G + k
                xt, xsq, z, htok = slotap(k)

                def mk_dma(t=t, k=k, xt=xt):
                    def src_dma():
                        xsem = "sx%d" % k
                        p.wt("sync", xt_free[k])
                        tick = p.o("sync", lambda e: e.dma_start(
                            out=xt, in_=xs[t * 128:(t + 1) * 128, :]), ((xsem, 16),))
                        return [(xsem, tick)]
                    return src_dma
                stp, tk = norm_steps(t, k, True, mk_dma())
                steps.append(stp)
                tks.append(tk)
            nst = max(len(x) for x in steps)
            for si in range(nst):
                for k in range(G):
                    if si < len(steps[k]):
                        steps[k][si]()
            xv3 = x_dT.ap().rearrange("p (kd c) -> p kd c", kd=8)
            for k in range(G):
                t = g * G + k
                xt, xsq, z, htok = slotap(k)
                tpe, tcp = pe_tr_group(
                    [htok[:, kd * 128:(kd + 1) * 128] for kd in range(KD)],
                    h_dT3(t), True, extra_waits=(("sve", tks[k]["h"]),),
                    copy_eng="scalar" if k % 2 else "vector")
                last_htr[0] = tpe
                if t >= 16:
                    for j0 in range(2):
                        ew = list(tks[k]["ld"]) if j0 == 0 else []
                        tpe, tcp = pe_tr_group(
                            [xt[:, (j0 * 4 + j) * 128:(j0 * 4 + j + 1) * 128] for j in range(4)],
                            xv3[:, j0 * 4:(j0 + 1) * 4, (t - 16) * 128:(t - 16) * 128 + 128],
                            False, extra_waits=ew,
                            copy_eng="scalar" if j0 else "vector")
                    xt_free[k] = ("spe", tpe)
                else:
                    xt_free[k] = ("sve", tks[k]["z"])

        t_h_all = p.cur("sve")
        t_h_sac = p.cur("sac")

        if stage >= 2:
            kv_s = acar(kv_off, 4096); a_s = acar(a_off, 4096); y_s = acar(y_off, 4096)
            wqr_free = [None, None]
            wqg_rdy = [0, 0]
            wqg_pe_done = [None, None]
            ci_load = [0, 0]

            def wq_chunk_sched(ci):
                bi = ci % 2
                wqr = acarb(wqr_off[bi], 1024)
                wqgb = acarb(wqg_off[bi], 1024)
                p.wt("gpsimd", wqr_free[bi])
                qsem = "sq%d" % bi
                ci_load[bi] = p.o("gpsimd", lambda e, wqr=wqr, ci=ci: e.dma_start(
                    out=wqr, in_=wq[ci]), ((qsem, 16),))
                p.w("vector", qsem, ci_load[bi])
                p.w("vector", "sdc", t_consts)
                p.wt("vector", wqg_pe_done[bi])
                tl = None
                for kd in range(KD):
                    tl = p.o("vector", lambda e, wqgb=wqgb, wqr=wqr, kd=kd: e.tensor_scalar(
                        wqgb[:, kd * 128:(kd + 1) * 128], wqr[:, kd * 128:(kd + 1) * 128],
                        gam_s[:, kd:kd + 1], None, ALU.mult), (("sve", 1),))
                wqr_free[bi] = ("sve", tl)
                wqg_rdy[bi] = tl

            t_p1pe = p.cur("spe")
            wq_chunk_sched(0)
            wq_chunk_sched(1)

            mm_slot_free = [None] * 8
            mm_slot_i = [0]

            def mm_slot():
                i = mm_slot_i[0] % 8
                mm_slot_i[0] += 1
                return i

            first_kv = [True]
            for j in range(KD):
                for part, dst in ((0, kv_s), (1, a_s)):
                    ci = 3 * j + part
                    bi = ci % 2
                    wqgb = acarb(wqg_off[bi], 1024)
                    p.w("tensor", "sve", wqg_rdy[bi])
                    if first_kv[0]:
                        p.w("tensor", "sve", t_h_all)
                        p.w("tensor", "sac", t_h_sac)
                        first_kv[0] = False
                    tpe = None
                    for n in range(8):
                        i = mm_slot()
                        psl = bank(i)
                        p.wt("tensor", mm_slot_free[i])
                        for kd in range(KD):
                            tpe = p.o("tensor", lambda e, psl=psl, wqgb=wqgb, kd=kd, n=n: e.matmul(
                                psl, wqgb[:, kd * 128:(kd + 1) * 128],
                                h_dT(kd, n * 512, (n + 1) * 512),
                                start=(kd == 0), stop=(kd == KD - 1)),
                                (("spe", 1),) if kd == KD - 1 else ())
                        if part == 0:
                            p.w("vector", "spe", tpe)
                            tev = p.o("vector", lambda e, dst=dst, psl=psl, n=n: e.tensor_copy(
                                dst[:, n * 512:(n + 1) * 512], psl), (("sve", 1),))
                            mm_slot_free[i] = ("sve", tev)
                        else:
                            p.w("scalar", "spe", tpe)
                            tev = p.o("scalar", lambda e, dst=dst, psl=psl, n=n: e.activation(
                                dst[:, n * 512:(n + 1) * 512], psl, AF.Sigmoid), (("sac", 1),))
                            mm_slot_free[i] = ("sac", tev)
                    wqg_pe_done[bi] = ("spe", tpe)
                    if ci + 2 < 24:
                        wq_chunk_sched(ci + 2)
                p.w("vector", "sac", p.cur("sac"))
                if j == 0:
                    p.w("vector", "spe", t_p1pe)
                t_scan = p.o("vector", lambda e: e.tensor_tensor_scan(
                    y_s[:, :], a_s[:, :], kv_s[:, :], 0.0, ALU.mult, ALU.add), (("sve", 1),))
                p.w("vector", "sve", t_scan)
                ci = 3 * j + 2
                bi = ci % 2
                wqgb = acarb(wqg_off[bi], 1024)
                p.w("tensor", "sve", wqg_rdy[bi])
                tpe = None
                for n in range(4):
                    i = mm_slot()
                    psl = bank(i)
                    p.wt("tensor", mm_slot_free[i])
                    for kd in range(KD):
                        tpe = p.o("tensor", lambda e, psl=psl, wqgb=wqgb, kd=kd, n=n: e.matmul(
                            psl, wqgb[:, kd * 128:(kd + 1) * 128],
                            h_dT(kd, 2048 + n * 512, 2048 + (n + 1) * 512),
                            start=(kd == 0), stop=(kd == KD - 1)),
                            (("spe", 1),) if kd == KD - 1 else ())
                    p.w("vector", "spe", tpe)
                    t1 = acar(t1_off, 512)
                    if n > 0:
                        p.w("vector", "sve", t1_prev)
                    t_stt = p.o("vector", lambda e, t1=t1, psl=psl, n=n: e.scalar_tensor_tensor(
                        t1, psl, 1.0, y_s[:, 2048 + n * 512:2048 + (n + 1) * 512],
                        ALU.mult, ALU.mult), (("sve", 1),))
                    p.w("vector", "sve", t_stt)
                    xsl = x_dT[:, j * OWN + n * 512:j * OWN + (n + 1) * 512]
                    tv = p.o("vector", lambda e, xsl=xsl, t1=t1: e.tensor_add(xsl, xsl, t1), (("sve", 1),))
                    t1_prev = tv
                    mm_slot_free[i] = ("sve", tv)
                wqg_pe_done[bi] = ("spe", tpe)
                if ci + 2 < 24:
                    wq_chunk_sched(ci + 2)

        t_p2pe = p.cur("spe")

        t_ln2dma = 0
        if stage >= 3:
            ln2dma_slot = [None] * 4
            for g in range(NT3 // G):
                steps = []
                tks = []
                for k in range(G):
                    t = g * G + k
                    xt, xsq, z, htok = slotap(k)

                    def mk_src(t=t, k=k, xt=xt, g=g):
                        def src_dma():
                            out = []
                            for j0 in range(2):
                                ew = [xt_free[k]] if j0 == 0 else []
                                if g == 0 and k == 0 and j0 == 0:
                                    ew.append(("sve", p.cur("sve")))
                                tpe, tcp = pe_tr_group(
                                    [x_dT[:, (j0 * 4 + j) * OWN + t * 128:(j0 * 4 + j) * OWN + t * 128 + 128]
                                     for j in range(4)],
                                    xt[:, j0 * 512:(j0 + 1) * 512], False,
                                    extra_waits=ew,
                                    copy_eng="scalar" if j0 else "vector")
                                out.append(tcp)
                            return out
                        return src_dma

                    stp, tk = norm_steps(t, k, False, mk_src())
                    tk["l2gate"] = ln2dma_slot[k]
                    steps.append(stp)
                    tks.append(tk)
                nst = max(len(x) for x in steps)
                for si in range(nst):
                    for k in range(G):
                        if si < len(steps[k]):
                            steps[k][si]()
                lv3 = arena[:, 0:8192].bitcast(BF16).rearrange("p (kd c) -> p kd c", kd=8)
                for k in range(G):
                    t = g * G + k
                    xt, xsq, z, htok = slotap(k)
                    p.w("sync", "sve", tks[k]["z"])
                    t_ln2dma = p.o("sync", lambda e, htok=htok, t=t: e.dma_start(
                        out=ln2tok[t * 128:(t + 1) * 128, :], in_=htok), (("sl2", 16),))
                    ln2dma_slot[k] = ("sl2", t_ln2dma)
                    tpe, _ = pe_tr_group(
                        [htok[:, kd * 128:(kd + 1) * 128] for kd in range(KD)],
                        lv3[:, :, t * 128:(t + 1) * 128], True,
                        extra_waits=(("sve", tks[k]["z"]),),
                        copy_eng="scalar" if k % 2 else "vector")
                    last_htr[0] = tpe
                    xt_free[k] = ("sve", tks[k]["z"])

        t_ln2 = p.cur("sve")
        t_ln2sac = p.cur("sac")
        t_ln2tok_dma = p.cur("sl2")

        if stage >= 4:
            mskT = [acar(mskT_off[0], 2048), acar(mskT_off[1], 2048)]
            lg_ev = None
            mt_ev = None
            for t in range(NT3):
                bi = t % 2
                c = bi * 16
                ltile = p4b[:, 0:7]
                msk1 = p4b[:, 8:15]
                l2 = p4b[:, 16:23]
                msk2 = p4b[:, 24:31]
                m1 = st[:, c + 0:c + 1]; m2 = st[:, c + 1:c + 2]; d12 = st[:, c + 2:c + 3]
                d21 = st[:, c + 5:c + 6]
                sg = sgall[:, t:t + 1]; tau = tauall[:, t:t + 1]
                pl = pD[:, 0:7]
                p.w("tensor", "srw", t_rw)
                if t == 0:
                    p.w("tensor", "sve", t_ln2)
                if lg_ev is not None:
                    p.w("tensor", "sve", lg_ev)
                tpe = None
                for kd in range(KD):
                    tpe = p.o("tensor", lambda e, pl=pl, kd=kd, t=t: e.matmul(
                        pl, ln2_dT(kd, t * 128, (t + 1) * 128), rw_s[:, kd * 7:(kd + 1) * 7],
                        start=(kd == 0), stop=(kd == KD - 1)),
                        (("spe", 1),) if kd == KD - 1 else ())
                p.w("vector", "spe", tpe)
                if mt_ev is not None:
                    p.w("vector", "spe", mt_ev)
                lg_ev = p.o("vector", lambda e, ltile=ltile, pl=pl: e.tensor_copy(ltile, pl), (("sve", 1),))
                p.w("vector", "sve", lg_ev)
                tv = p.o("vector", lambda e, m1=m1, ltile=ltile: e.reduce_max(m1, ltile, axis=AX), (("sve", 1),))
                p.w("vector", "sve", tv)
                tv = p.o("vector", lambda e, msk1=msk1, ltile=ltile, m1=m1: e.tensor_scalar(
                    msk1, ltile, m1, None, ALU.is_ge), (("sve", 1),))
                p.w("vector", "sve", tv)
                tv = p.o("vector", lambda e, l2=l2, msk1=msk1, ltile=ltile: e.scalar_tensor_tensor(
                    l2, msk1, -1e30, ltile, ALU.mult, ALU.add), (("sve", 1),))
                p.w("vector", "sve", tv)
                tv = p.o("vector", lambda e, m2=m2, l2=l2: e.reduce_max(m2, l2, axis=AX), (("sve", 1),))
                p.w("vector", "sve", tv)
                tv = p.o("vector", lambda e, d12=d12, m1=m1, m2=m2: e.tensor_scalar(
                    d12, m1, m2, None, ALU.subtract), (("sve", 1),))
                p.w("vector", "sve", tv)
                tvb = p.o("vector", lambda e, d21=d21, m2=m2, m1=m1: e.tensor_scalar(
                    d21, m2, m1, None, ALU.subtract), (("sve", 1),))
                tv2 = p.o("vector", lambda e, msk2=msk2, l2=l2, m2=m2: e.tensor_scalar(
                    msk2, l2, m2, None, ALU.is_ge), (("sve", 1),))
                p.w("scalar", "sve", tvb)
                p.o("scalar", lambda e, sg=sg, d12=d12: e.activation(sg, d12, AF.Sigmoid), (("sac", 1),))
                p.o("scalar", lambda e, tau=tau, d21=d21: e.activation(tau, d21, AF.Sigmoid), (("sac", 1),))
                for r in range(2):
                    src = msk1 if r == 0 else msk2
                    mt_pe, _ = pe_transpose(
                        src, mskT[r][0:7, t * 128:(t + 1) * 128], False,
                        extra_waits=(("sve", tv2),), n_avail=6, base=0, rows=7)
                mt_ev = mt_pe
        t_p4 = p.cur("sve")

        if stage >= 5:
            ones7 = acar(PB[0], 2048)
            csB = acar(PB[1], 2048)
            tB2 = acar(PB[2], 2048)
            tB3 = acar(PB[3], 2048)
            selB = acar(PB[4], 2048)
            fdB = acar(PB[5], 2048)
            iB = acari(PB[6], 2048)
            iN = acari(PB[2], 2048)
            p.w("vector", "szr", t_prezero)
            t_o7 = p.o("vector", lambda e: e.memset(ones7[0:7, :], 1.0), (("sve", 1),))
            t_scat_ids = []
            msum = tB2
            p.w("vector", "sve", t_p4)
            tv = p.o("vector", lambda e: e.tensor_add(msum[0:7, :], mskT[0][0:7, :], mskT[1][0:7, :]), (("sve", 1),))
            p.w("vector", "sve", tv)
            if dbg:
                tv = p.o("vector", lambda e: e.tensor_copy(dbgm[:, :], msum[0:7, 0:1024]), (("sve", 1),))
                p.w("vector", "sve", tv)
                p.w("sync", "sve", tv)
                p.o("sync", lambda e: e.dma_start(out=dbg4o[0:7, 0:1024], in_=dbgm[:, :]), (("srw", 16),))
            tv = p.o("vector", lambda e: e.tensor_tensor_scan(
                csB[0:7, :], ones7[0:7, :], msum[0:7, :], 0.0, ALU.mult, ALU.add), (("sve", 1),))
            p.w("vector", "sve", tv)
            tv = p.o("vector", lambda e: e.tensor_scalar(
                tB3[0:7, :], csB[0:7, :], 1.0, None, ALU.subtract), (("sve", 1),))
            p.w("vector", "sve", tv)
            tv = p.o("vector", lambda e: e.tensor_copy(iB[0:7, :], tB3[0:7, :]), (("sve", 1),))
            p.w("vector", "sve", tv)
            tv = p.o("vector", lambda e: e.tensor_single_scalar(
                iN[0:7, :], iB[0:7, :], 127, ALU.bitwise_and), (("sve", 1),))
            p.w("vector", "sve", tv)
            tv = p.o("vector", lambda e: e.tensor_single_scalar(
                iB[0:7, :], iB[0:7, :], 7, ALU.arith_shift_right), (("sve", 1),))
            p.w("vector", "sve", tv)
            tv = p.o("vector", lambda e: e.tensor_copy(tB3[0:7, :], iN[0:7, :]), (("sve", 1),))
            p.w("vector", "sve", tv)
            tv = p.o("vector", lambda e: e.tensor_copy(tB2[0:7, :], iB[0:7, :]), (("sve", 1),))
            p.w("vector", "sve", tv)
            tv = p.o("vector", lambda e: e.scalar_tensor_tensor(
                tB3[0:7, :], tB3[0:7, :], 56.0, tB2[0:7, :], ALU.mult, ALU.add), (("sve", 1),))
            p.w("vector", "sve", tv)
            tv = p.o("vector", lambda e: e.tensor_scalar(
                tB3[0:7, :], tB3[0:7, :], ecol_s[0:7, 0:1], None, ALU.add), (("sve", 1),))
            p.w("vector", "sve", tv)
            tv = p.o("vector", lambda e: e.tensor_single_scalar(
                tB2[0:7, :], csB[0:7, :], CAP2 + 0.5, ALU.is_ge), (("sve", 1),))
            p.w("vector", "sve", tv)
            tv = p.o("vector", lambda e: e.scalar_tensor_tensor(
                tB3[0:7, :], tB2[0:7, :], SENT, tB3[0:7, :], ALU.mult, ALU.add), (("sve", 1),))
            t_prep_early = tv
            if dbg:
                p.w("sync", "sve", t_prep_early)
                for h_ in range(2):
                    p.o("sync", lambda e, h_=h_: e.dma_start(
                        out=dbg3o[h_ * 7:h_ * 7 + 7, :], in_=tB3[0:7, h_ * 1024:(h_ + 1) * 1024]), (("srw", 16),))
                    p.o("sync", lambda e, h_=h_: e.dma_start(
                        out=dbg3o[16 + h_ * 7:16 + h_ * 7 + 7, :], in_=csB[0:7, h_ * 1024:(h_ + 1) * 1024]), (("srw", 16),))
        else:
            t_prep_early = p.cur("sve")
        t_ci_cell = [None]

        def emit_prep_pe():
            p.w("vector", "sve", t_prep_early)
            for r in range(2):
                mt = mskT[r]
                t_sel = p.o("vector", lambda e, mt=mt: e.tensor_tensor(
                    selB[0:7, :], tB3[0:7, :], mt[0:7, :], ALU.mult), (("sve", 1),))
                t_fd = None
                for nch in range(4):
                    bk = bank(6 + nch % 2)
                    p.w("tensor", "sve", t_sel)
                    p.wt("tensor", tslot_free[6 + nch % 2])
                    tmm = p.o("tensor", lambda e, bk=bk, nch=nch: e.matmul(
                        bk[0:1, :], ones7[0:7, 0:1], selB[0:7, nch * 512:(nch + 1) * 512],
                        start=True, stop=True), (("spe", 1),))
                    p.w("vector", "spe", tmm)
                    t_fd = p.o("vector", lambda e, bk=bk, nch=nch: e.tensor_copy(
                        csB[0:1, nch * 512:(nch + 1) * 512], bk[0:1, :]), (("sve", 1),))
                    tslot_free[6 + nch % 2] = ("sve", t_fd)
                t_fb = None
                for nch in range(4):
                    bk = bank(6 + nch % 2)
                    p.w("tensor", "sve", t_fd)
                    p.wt("tensor", tslot_free[6 + nch % 2])
                    tmm = p.o("tensor", lambda e, bk=bk, nch=nch: e.matmul(
                        bk[:, :], ones1[0:1, :], csB[0:1, nch * 512:(nch + 1) * 512],
                        start=True, stop=True), (("spe", 1),))
                    p.w("vector", "spe", tmm)
                    t_fb = p.o("vector", lambda e, bk=bk, nch=nch: e.tensor_copy(
                        fdB[:, nch * 512:(nch + 1) * 512], bk[:, :]), (("sve", 1),))
                    tslot_free[6 + nch % 2] = ("sve", t_fb)
                for a in range(16):
                    tpe, tcp = pe_transpose(fdB[:, a * 128:(a + 1) * 128],
                                            fdt_f[:, r * 16 + a:r * 16 + a + 1],
                                            False, extra_waits=(("sve", t_fb),),
                                            n_avail=2, base=6, cols=1)
            p.w("vector", "sve", p.cur("sve"))
            tv2_ = p.o("vector", lambda e: e.tensor_scalar_min(
                fdt_f[:, :], fdt_f[:, :], float(128 * 56)), (("sve", 1),))
            p.w("vector", "sve", tv2_)
            t_ci_cell[0] = p.o("vector", lambda e: e.tensor_copy(fdt_i[:, :], fdt_f[:, :]), (("sve", 1),))
        t_gidx = None
        wgu_free = {("g", 0): None, ("g", 1): None, ("u", 0): None, ("u", 1): None}
        wd_free = [None, None, None, None]
        wgu_load = {}
        pgA_free = ("sve", t_p4)
        puB_free = ("sve", t_p4)
        pdC_free = ("sve", t_p4)
        sil_free = [None]
        first_wdma = [True]
        t_down_prev = [None]

        def wdma(src_ap, off, cols_bf, free, semname):
            if first_wdma[0]:
                p.w("gpsimd", "spe", t_p2pe)
                first_wdma[0] = False
            p.wt("gpsimd", free)
            return (semname, p.o("gpsimd", lambda e, src_ap=src_ap, off=off, cols_bf=cols_bf: e.dma_start(
                out=acarb(off, cols_bf), in_=src_ap), ((semname, 16),)))

        def wd_slice(src_e, sl):
            dj, mq = sl // 4, sl % 4
            return src_e[dj][:, mq * 1024:(mq + 1) * 1024]

        prep_done = [False]

        scat_ptr = [0]

        def emit_id_scatters(n):
            lo = scat_ptr[0]
            hi = min(lo + n, 32)
            if lo >= hi:
                return
            if lo == 0:
                p.w("gpsimd", "spf", t_pref)
                p.w("gpsimd", "sve", t_ci_cell[0])
            for c in range(lo, hi):
                p.o("gpsimd", lambda e, c=c: e.indirect_dma_start(
                    out=listf, out_offset=IOA(fdt_i[:, c:c + 1], 0),
                    in_=tokid_s[:, c:c + 1], in_offset=None), (("ssc", 16),))
            scat_ptr[0] = hi

        def emit_prep_dmas():
            prep_done[0] = True
            emit_id_scatters(32)
            p.w("gpsimd", "ssc", p.cur("ssc"))
            t_rb = p.o("gpsimd", lambda e: e.dma_start(
                out=gidx_f[:, :], in_=listf[0:128 * 56].rearrange("(q x) one -> q (x one)", q=128)), (("srb", 16),))
            if dbg:
                p.w("gpsimd", "srb", t_rb)
                p.w("gpsimd", "sve", t_ci_cell[0])
                p.o("gpsimd", lambda e: e.dma_start(out=dbgo[:, 0:56], in_=gidx_f[:, :]), (("srb", 16),))
                p.o("gpsimd", lambda e: e.dma_start(out=dbgo[:, 56:88], in_=fdt_f[:, :]), (("srb", 16),))
            p.w("vector", "srb", t_rb)
            tv = p.o("vector", lambda e: e.tensor_scalar_min(
                gsc_f[:, :], gidx_f[:, :], float(2 * OWN)), (("sve", 1),))
            p.w("vector", "sve", tv)
            tv = p.o("vector", lambda e: e.tensor_copy(gsc_i[:, :], gsc_f[:, :]), (("sve", 1),))
            p.w("vector", "sve", tv)
            tv = p.o("vector", lambda e: e.tensor_single_scalar(
                gidx_i[:, :], gidx_f[:, :], OWN - 0.5, ALU.is_ge), (("sve", 1),))
            p.w("vector", "sve", tv)
            tv = p.o("vector", lambda e: e.tensor_copy(gsc_f[:, :], gidx_i[:, :]), (("sve", 1),))
            p.w("vector", "sve", tv)
            tv = p.o("vector", lambda e: e.scalar_tensor_tensor(
                gidx_f[:, :], gsc_f[:, :], -float(OWN), gidx_f[:, :], ALU.mult, ALU.add), (("sve", 1),))
            p.w("vector", "sve", tv)
            tv = p.o("vector", lambda e: e.tensor_scalar_min(
                gidx_f[:, :], gidx_f[:, :], float(OWN)), (("sve", 1),))
            p.w("vector", "sve", tv)
            tk_ = p.o("vector", lambda e: e.tensor_copy(gidx_i[:, :], gidx_f[:, :]), (("sve", 1),))
            t_gidx_cell[0] = tk_
            return tk_

        t_gidx_cell = [None]
        t_gln_cell = [None]
        ga_dma = {}
        ga_free = [None, None]

        def gather_dma(li, t8):
            gb = t8 % 2
            col = li * 8 + t8
            p.wt("gpsimd", ga_free[gb])
            p.w("gpsimd", "sve", t_gidx_cell[0])
            p.w("gpsimd", "sl2", t_ln2tok_dma)
            p.w("gpsimd", "szr", t_prezero)
            gsem = "sg%d" % gb
            ga_dma[(li, t8)] = p.o("gpsimd", lambda e, gb=gb, col=col: e.indirect_dma_start(
                out=acarb(gtok_off[gb], 1024), out_offset=None,
                in_=ln2tok, in_offset=IOA(gidx_i[:, col:col + 1], 0)), ((gsem, 16),))

        def gather_tr(li, t8):
            gb = t8 % 2
            gv3 = arena[:, gln_off:gln_off + 3584].bitcast(BF16).rearrange(
                "p (kd c) -> p kd c", kd=8)
            tpe, tcp = pe_tr_group(
                [acarb(gtok_off[gb], 1024)[:, kd * 128:(kd + 1) * 128] for kd in range(KD)],
                gv3[:, :, t8 * 128:(t8 + 1) * 128], True,
                extra_waits=(("sg%d" % gb, ga_dma[(li, t8)]),),
                n_avail=2, base=6)
            ga_free[gb] = ("spe", tpe)
            if t8 == NT8 - 1:
                t_gln_cell[0] = p.cur("sve")

        def s1b_gather_hook(dj, li):
            if dj <= 6:
                gather_tr(li, dj)
                if dj + 2 < NT8:
                    gather_dma(li, dj + 2)


        if stage >= 6:
            for sp in range(2):
                tok0 = sp * 1024
                for g_i in (0, 1):
                    wgu_load[("g", g_i)] = wdma(swg[g_i], wg_off[g_i % 2], 2048, wgu_free[("g", g_i % 2)], "swg%d" % (g_i % 2))
                    wgu_load[("u", g_i)] = wdma(swu[g_i], wu_off[g_i % 2], 2048, wgu_free[("u", g_i % 2)], "swu%d" % (g_i % 2))
                for sl in (0, 1, 2, 3):
                    wgu_load[("d", sl)] = wdma(wd_slice(swd, sl), wd_off[sl % 4], 1024, wd_free[sl % 4], "swd%d" % (sl % 2))
                silb = acar(ttok_off[1], 1024)
                for m in range(MH):
                    g_i = m // 2
                    mm = m % 2
                    wgb = acarb(wg_off[g_i % 2], 2048)
                    wub = acarb(wu_off[g_i % 2], 2048)
                    p.wt("tensor", wgu_load[("g", g_i)])
                    p.wt("tensor", wgu_load[("u", g_i)])
                    if m == 0:
                        p.w("tensor", "sac", t_ln2sac)
                    p.wt("tensor", pgA_free)
                    t_g = None
                    for kd in range(KD):
                        for tt in range(2):
                            t_g = p.o("tensor", lambda e, wgb=wgb, mm=mm, kd=kd, tt=tt, tok0=tok0: e.matmul(
                                pA[:, tt * 512:(tt + 1) * 512],
                                wgb[:, (mm * 8 + kd) * 128:(mm * 8 + kd) * 128 + 128],
                                ln2_dT(kd, tok0 + tt * 512, tok0 + (tt + 1) * 512),
                                start=(kd == 0), stop=(kd == KD - 1)),
                                (("spe", 1),) if (kd == KD - 1 and tt == 1) else ())
                    p.wt("tensor", puB_free)
                    t_u = None
                    for kd in range(KD):
                        for tt in range(2):
                            t_u = p.o("tensor", lambda e, wub=wub, mm=mm, kd=kd, tt=tt, tok0=tok0: e.matmul(
                                pB[:, tt * 512:(tt + 1) * 512],
                                wub[:, (mm * 8 + kd) * 128:(mm * 8 + kd) * 128 + 128],
                                ln2_dT(kd, tok0 + tt * 512, tok0 + (tt + 1) * 512),
                                start=(kd == 0), stop=(kd == KD - 1)),
                                (("spe", 1),) if (kd == KD - 1 and tt == 1) else ())
                    p.w("scalar", "spe", t_g)
                    p.wt("scalar", sil_free[0])
                    if sp == 0 and m == 0:
                        p.w("scalar", "sve", t_prep_early)
                        p.w("scalar", "szr", t_prezero)
                    t_sil = p.o("scalar", lambda e, silb=silb: e.activation(
                        silb, pA[:, :], AF.Silu), (("sac", 1),))
                    pgA_free = ("sac", t_sil)
                    p.w("vector", "spe", t_u)
                    p.w("vector", "sac", t_sil)
                    if m == 0 and t_down_prev[0] is not None:
                        p.w("vector", "spe", t_down_prev[0])
                    t_mul = p.o("vector", lambda e, m=m, silb=silb: e.tensor_tensor(
                        h1(m, 0, 1024, "sh"), pB[:, :], silb, ALU.mult), (("sve", 1),))
                    puB_free = ("sve", t_mul)
                    sil_free[0] = ("sve", t_mul)
                    if mm == 1:
                        wgu_free[("g", g_i % 2)] = ("sac", t_sil)
                        wgu_free[("u", g_i % 2)] = ("sve", t_mul)
                        if g_i + 2 < 16:
                            wgu_load[("g", g_i + 2)] = wdma(swg[g_i + 2], wg_off[(g_i + 2) % 2],
                                                            2048, wgu_free[("g", (g_i + 2) % 2)], "swg%d" % ((g_i + 2) % 2))
                            wgu_load[("u", g_i + 2)] = wdma(swu[g_i + 2], wu_off[(g_i + 2) % 2],
                                                            2048, wgu_free[("u", (g_i + 2) % 2)], "swu%d" % ((g_i + 2) % 2))
                        if sp == 0 and stage >= 5 and m >= 9:
                            emit_id_scatters(4)
                    if sp == 0 and stage >= 5 and m == 3:
                        emit_prep_pe()
                t_h1_all = p.cur("sve")
                t_d = None
                for dj in range(8):
                    p.wt("tensor", pdC_free)
                    t_d = None
                    for mq in range(4):
                        sl = dj * 4 + mq
                        p.wt("tensor", wgu_load[("d", sl)])
                        if sl == 0:
                            p.w("tensor", "sve", t_h1_all)
                        wdb = acarb(wd_off[sl % 4], 1024)
                        for mloc in range(8):
                            m = mq * 8 + mloc
                            for tt in range(2):
                                t_d = p.o("tensor", lambda e, wdb=wdb, m=m, mloc=mloc, tt=tt: e.matmul(
                                    pC[:, tt * 512:(tt + 1) * 512],
                                    wdb[:, mloc * 128:(mloc + 1) * 128],
                                    h1(m, tt * 512, (tt + 1) * 512, "sh"),
                                    start=(m == 0), stop=(m == MH - 1)),
                                    (("spe", 1),) if (mloc == 7 and tt == 1) else ())
                        wd_free[sl % 4] = ("spe", t_d if t_d is not None else 0)
                        if sl + 4 < 32:
                            wgu_load[("d", sl + 4)] = wdma(wd_slice(swd, sl + 4), wd_off[sl % 4],
                                                           1024, wd_free[sl % 4], "swd%d" % (sl % 2))
                    xsl = x_dT[:, dj * OWN + tok0:dj * OWN + tok0 + 1024]
                    p.w("vector", "spe", t_d)
                    t_acc = p.o("vector", lambda e, xsl=xsl: e.tensor_add(
                        xsl, xsl, pC[:, :]), (("sve", 1),))
                    pdC_free = ("sve", t_acc)
                t_down_prev[0] = t_d
                if sp == 0 and stage >= 5:
                    t_gidx_mid = emit_prep_dmas()

        if stage >= 5 and not prep_done[0]:
            t_gidx = emit_prep_dmas()
        elif prep_done[0]:
            t_gidx = t_gidx_mid
        else:
            t_gidx = p.cur("sve")

        t_scat_out = []
        scat_last = [None, None]
        pending_scat = [None]
        if stage >= 7:
            t_shared_done = p.cur("sve")
            t_shared_pe = p.cur("spe")
            tslot_free[6] = ("sve", t_shared_done)
            tslot_free[7] = ("sve", t_shared_done)
            pgA_free = ("sve", t_shared_done)
            puB_free = ("sve", t_shared_done)
            pdC_free = ("sve", t_shared_done)
            ga_free = [("spe", t_shared_pe), ("spe", t_shared_pe)]
            for t8_ in range(NT8):
                gather_dma(0, t8_)
                gather_tr(0, t8_)

            def mk_scat(li, dout, t_dout):
                def emit_one(t8):
                    tb = t8 % 2
                    ttok = acarb(ttok_off[tb], 1024)
                    col = li * 8 + t8
                    ssem = "so%d" % tb
                    ew = [("sve", t_dout)]
                    if scat_last[tb] is not None:
                        ew.append(scat_last[tb])
                    tpe, tcp = pe_tr_group(
                        [dout[:, dj * 896 + t8 * 128:dj * 896 + t8 * 128 + 128] for dj in range(KD)],
                        ttok[:, :], True,
                        extra_waits=ew, n_avail=2, base=6)
                    if li == 0 and t8 == 0:
                        p.w("gpsimd", "szr", t_prezero)
                    p.wt("gpsimd", tcp)
                    ts_ = p.o("gpsimd", lambda e, ttok=ttok, col=col: e.indirect_dma_start(
                        out=accC, out_offset=IOA(gsc_i[:, col:col + 1], 0),
                        in_=ttok, in_offset=None), ((ssem, 16),))
                    scat_last[tb] = (ssem, ts_)
                    t_scat_out.append(ts_)
                return emit_one

            nx_load = {}
            for li in range(7):
                e_i = li
                if ("g", 0) in nx_load:
                    for g_i in (0, 1):
                        wgu_load[("g", g_i)] = nx_load[("g", g_i)]
                        wgu_load[("u", g_i)] = nx_load[("u", g_i)]
                else:
                    for g_i in (0, 1):
                        wgu_load[("g", g_i)] = wdma(wg[e_i, g_i], wg_off[g_i % 2], 2048, wgu_free[("g", g_i % 2)], "swg%d" % (g_i % 2))
                        wgu_load[("u", g_i)] = wdma(wu[e_i, g_i], wu_off[g_i % 2], 2048, wgu_free[("u", g_i % 2)], "swu%d" % (g_i % 2))
                for sl in (0, 1, 2, 3):
                    wgu_load[("d", sl)] = wdma(wd_slice(wd[e_i], sl), wd_off[sl % 4], 1024, wd_free[sl % 4], "swd%d" % (sl % 2))
                silb = acar(sil_off, 1024)
                t_gln_cur = t_gln_cell[0]
                t_u = None
                for m in range(MH):
                    g_i = m // 2
                    mm = m % 2
                    wgb = acarb(wg_off[g_i % 2], 2048)
                    wub = acarb(wu_off[g_i % 2], 2048)
                    p.wt("tensor", wgu_load[("g", g_i)])
                    p.wt("tensor", wgu_load[("u", g_i)])
                    if m == 0:
                        p.w("tensor", "sve", t_gln_cur)
                    p.wt("tensor", pgA_free)
                    t_g = None
                    for kd in range(KD):
                        for c0, c1 in ((0, 512), (512, 896)):
                            t_g = p.o("tensor", lambda e, wgb=wgb, mm=mm, kd=kd, c0=c0, c1=c1: e.matmul(
                                pA[:, c0:c1],
                                wgb[:, (mm * 8 + kd) * 128:(mm * 8 + kd) * 128 + 128],
                                gln_dT(0, kd, c0, c1),
                                start=(kd == 0), stop=(kd == KD - 1)),
                                (("spe", 1),) if (kd == KD - 1 and c0 > 0) else ())
                    p.wt("tensor", puB_free)
                    t_u = None
                    for kd in range(KD):
                        for c0, c1 in ((0, 512), (512, 896)):
                            t_u = p.o("tensor", lambda e, wub=wub, mm=mm, kd=kd, c0=c0, c1=c1: e.matmul(
                                pB[:, c0:c1],
                                wub[:, (mm * 8 + kd) * 128:(mm * 8 + kd) * 128 + 128],
                                gln_dT(0, kd, c0, c1),
                                start=(kd == 0), stop=(kd == KD - 1)),
                                (("spe", 1),) if (kd == KD - 1 and c0 > 0) else ())
                    p.w("scalar", "spe", t_g)
                    p.wt("scalar", sil_free[0])
                    t_sil = p.o("scalar", lambda e, silb=silb: e.activation(
                        silb[:, 0:896], pA[:, 0:896], AF.Silu), (("sac", 1),))
                    pgA_free = ("sac", t_sil)
                    p.w("vector", "spe", t_u)
                    p.w("vector", "sac", t_sil)
                    if m == 0 and t_down_prev[0] is not None:
                        p.w("vector", "spe", t_down_prev[0])
                    t_mul = p.o("vector", lambda e, m=m, silb=silb: e.tensor_tensor(
                        h1(m, 0, 896, "rt"), pB[:, 0:896], silb[:, 0:896], ALU.mult), (("sve", 1),))
                    puB_free = ("sve", t_mul)
                    sil_free[0] = ("sve", t_mul)
                    if mm == 1:
                        wgu_free[("g", g_i % 2)] = ("sac", t_sil)
                        wgu_free[("u", g_i % 2)] = ("sve", t_mul)
                        if g_i + 2 < 16:
                            wgu_load[("g", g_i + 2)] = wdma(wg[e_i, g_i + 2], wg_off[(g_i + 2) % 2],
                                                            2048, wgu_free[("g", (g_i + 2) % 2)], "swg%d" % ((g_i + 2) % 2))
                            wgu_load[("u", g_i + 2)] = wdma(wu[e_i, g_i + 2], wu_off[(g_i + 2) % 2],
                                                            2048, wgu_free[("u", (g_i + 2) % 2)], "swu%d" % ((g_i + 2) % 2))
                        if m <= 13 and pending_scat[0] is not None:
                            pending_scat[0](m // 2)
                t_h1_all = p.cur("sve")
                if li + 1 < 7:
                    e_n = li + 1
                    for g_i in (0, 1):
                        nx_load[("g", g_i)] = wdma(wg[e_n, g_i], wg_off[g_i % 2], 2048, wgu_free[("g", g_i % 2)], "swg%d" % (g_i % 2))
                        nx_load[("u", g_i)] = wdma(wu[e_n, g_i], wu_off[g_i % 2], 2048, wgu_free[("u", g_i % 2)], "swu%d" % (g_i % 2))
                    gather_dma(li + 1, 0)
                    gather_dma(li + 1, 1)
                dout = acarb(dout_off, 8192)
                t_d = None
                for dj in range(8):
                    p.wt("tensor", pdC_free)
                    t_d = None
                    for mq in range(4):
                        sl = dj * 4 + mq
                        p.wt("tensor", wgu_load[("d", sl)])
                        if sl == 0:
                            p.w("tensor", "sve", t_h1_all)
                        wdb = acarb(wd_off[sl % 4], 1024)
                        for mloc in range(8):
                            m = mq * 8 + mloc
                            for c0, c1 in ((0, 512), (512, 896)):
                                t_d = p.o("tensor", lambda e, wdb=wdb, m=m, mloc=mloc, c0=c0, c1=c1: e.matmul(
                                    pC[:, c0:c1],
                                    wdb[:, mloc * 128:(mloc + 1) * 128],
                                    h1(m, c0, c1, "rt"),
                                    start=(m == 0), stop=(m == MH - 1)),
                                    (("spe", 1),) if (mloc == 7 and c0 > 0) else ())
                        wd_free[sl % 4] = ("spe", t_d if t_d is not None else 0)
                        if sl + 4 < 32:
                            wgu_load[("d", sl + 4)] = wdma(wd_slice(wd[e_i], sl + 4), wd_off[sl % 4],
                                                           1024, wd_free[sl % 4], "swd%d" % (sl % 2))
                    p.w("vector", "spe", t_d)
                    t_dc = p.o("vector", lambda e, dout=dout, dj=dj: e.tensor_copy(
                        dout[:, dj * 896:(dj + 1) * 896], pC[:, 0:896]), (("sve", 1),))
                    pdC_free = ("sve", t_dc)
                    if li + 1 < 7:
                        s1b_gather_hook(dj, li + 1)
                t_down_prev[0] = t_d
                t_dout = p.cur("sve")
                pending_scat[0] = mk_scat(li, dout, t_dout)
                if li == 6:
                    for t8 in range(NT8):
                        pending_scat[0](t8)
                    pending_scat[0] = None

        t_scat0 = p.cur("so0")
        t_scat1 = p.cur("so1")
        if dbg and stage >= 7:
            p.w("gpsimd", "so0", t_scat0)
            p.w("gpsimd", "so1", t_scat1)
            p.o("gpsimd", lambda e: e.dma_start(out=dbg5o, in_=accC), (("srb", 16),))

        if stage >= 5:
            out_tick = [None, None, None, None]
            a_load = [None, None, None, None]
            p.w("vector", "sac", p.cur("sac"))
            p.w("sync", "spe", p.cur("spe"))
            p.w("sync", "so0", t_scat0)
            p.w("sync", "so1", t_scat1)
            p.w("sync", "szr", t_prezero)
            for t in range(NT3):
                b = t % 4
                xtk = acar(fxt_off[b], 1024)
                a1t = acarb(fa1_off[b], 1024)
                a2t = acarb(fa2_off[b], 1024)
                p.wt("sync", out_tick[b])
                a_load[b] = p.o("sync", lambda e, a1t=a1t, t=t: e.dma_start(
                    out=a1t, in_=accC[t * 128:(t + 1) * 128, :]), (("sal", 16),))
                a_load[b] = p.o("sync", lambda e, a2t=a2t, t=t: e.dma_start(
                    out=a2t, in_=accC[OWN + t * 128:OWN + (t + 1) * 128, :]), (("sal", 16),))
                tcp = None
                for j0 in range(2):
                    ew = []
                    if t == 0 and j0 == 0:
                        ew.append(("sve", p.cur("sve")))
                    tpe, tcp = pe_tr_group(
                        [x_dT[:, (j0 * 4 + j) * OWN + t * 128:(j0 * 4 + j) * OWN + t * 128 + 128]
                         for j in range(4)],
                        xtk[:, j0 * 512:(j0 + 1) * 512], False,
                        extra_waits=ew, n_avail=2, base=6)
                p.wt("vector", tcp)
                p.w("vector", "sal", a_load[b])
                tv = p.o("vector", lambda e, a1t=a1t, xtk=xtk, t=t: e.scalar_tensor_tensor(
                    xtk, a1t, sgall[:, t:t + 1], xtk, ALU.mult, ALU.add), (("sve", 1),))
                p.w("vector", "sve", tv)
                tv = p.o("vector", lambda e, a2t=a2t, xtk=xtk, t=t: e.scalar_tensor_tensor(
                    xtk, a2t, tauall[:, t:t + 1], xtk, ALU.mult, ALU.add), (("sve", 1),))
                osem = "sou" if b % 2 == 0 else "sov"
                p.w("sync", "sve", tv)
                t_out = p.o("sync", lambda e, xtk=xtk, t=t: e.dma_start(
                    out=yo[t * 128:(t + 1) * 128, :], in_=xtk), ((osem, 16),))
                out_tick[b] = (osem, t_out)
            p.w("sync", "sou", p.cur("sou"))
            p.w("sync", "sov", p.cur("sov"))
        else:
            p.w("sync", "sve", p.cur("sve"))
            p.o("sync", lambda e: e.dma_start(
                out=yo.rearrange("(q a) d -> q (a d)", q=128), in_=acar(0, 16384)), (("sou", 16),))
            p.w("sync", "sou", 16)

        p.emit(blk)
    return nc


def _host_inputs(L0_attn_x, L0_attn_w_qkva, L0_attn_gamma, L0_ffn_router_w,
                 L0_ffn_w_gate, L0_ffn_w_up, L0_ffn_w_down,
                 L0_ffn_sw_gate, L0_ffn_sw_up, L0_ffn_sw_down):
    f = np.float32
    x = np.ascontiguousarray(np.asarray(L0_attn_x, dtype=f))
    wqkva = np.asarray(L0_attn_w_qkva, dtype=f)
    gamma = np.asarray(L0_attn_gamma, dtype=f)
    router = np.asarray(L0_ffn_router_w, dtype=f)

    W = wqkva.reshape(3, 8, 128, 8, 128)
    arr = W.transpose(1, 0, 4, 3, 2)
    wq_t = np.ascontiguousarray(arr[:, [1, 2, 0]].reshape(24, 128, 1024))

    gam_t = np.ascontiguousarray(gamma.reshape(8, 128).T)
    rw_t = np.ascontiguousarray(router.reshape(7, 8, 128).transpose(2, 1, 0).reshape(128, 56))
    identf = np.eye(128, dtype=f)
    identb = np.eye(128, dtype=BF)
    ecol = np.zeros((7, 2), f)
    for e in range(7):
        ecol[e, 0] = e * 8
    tokid = np.zeros((128, 32), f)
    for a_ in range(16):
        tokid[:, a_] = a_ * 128 + np.arange(128)
        tokid[:, 16 + a_] = 2048 + a_ * 128 + np.arange(128)

    def gu(w):
        E = w.shape[0]
        return np.ascontiguousarray(
            np.asarray(w, f).reshape(E, 8, 128, 16, 2, 128).transpose(0, 3, 2, 4, 1, 5)
            .reshape(E, 16, 128, 2048))

    def dn(w):
        E = w.shape[0]
        return np.ascontiguousarray(
            np.asarray(w, f).reshape(E, 32, 128, 8, 128).transpose(0, 3, 2, 1, 4)
            .reshape(E, 8, 128, 4096))

    wg_t = gu(L0_ffn_w_gate); wu_t = gu(L0_ffn_w_up); wd_t = dn(L0_ffn_w_down)
    swg_t = gu(L0_ffn_sw_gate)[0]; swu_t = gu(L0_ffn_sw_up)[0]; swd_t = dn(L0_ffn_sw_down)[0]

    common = dict(wq=wq_t, gam=gam_t, rw=rw_t, identf=identf, identb=identb,
                  ecolf=ecol, tokidf=tokid,
                  wg=wg_t, wu=wu_t, wd=wd_t, swg=swg_t, swu=swu_t, swd=swd_t)
    in_maps = []
    for c in range(8):
        b, h = c // 2, c % 2
        own = x[b, h * OWN:(h + 1) * OWN]
        prefix = x[b, 0:OWN] if h == 1 else np.zeros((OWN, D), f)
        xs_ = np.ascontiguousarray(np.concatenate([prefix, own], axis=0))
        in_maps.append(dict(common, xs=xs_))
    return in_maps


def kernel(L0_attn_x, L0_attn_w_qkva, L0_attn_gamma, L0_ffn_router_w,
           L0_ffn_w_gate, L0_ffn_w_up, L0_ffn_w_down,
           L0_ffn_sw_gate, L0_ffn_sw_up, L0_ffn_sw_down):
    in_maps = _host_inputs(L0_attn_x, L0_attn_w_qkva, L0_attn_gamma, L0_ffn_router_w,
                           L0_ffn_w_gate, L0_ffn_w_up, L0_ffn_w_down,
                           L0_ffn_sw_gate, L0_ffn_sw_up, L0_ffn_sw_down)
    nc = build_nc(stage=8)
    res = run_bass_kernel_spmd(nc, in_maps, core_ids=list(range(8)))
    out = np.empty((4, SEQ, D), np.float32)
    for c in range(8):
        b, h = c // 2, c % 2
        out[b, h * OWN:(h + 1) * OWN] = np.asarray(res.results[c]["yo"])
    return out
